# version 06 (final)
# baseline (speedup 1.0000x reference)
import numpy as np
import ml_dtypes
import concourse.bass as bass
import concourse.mybir as mybir
from concourse.bass_utils import run_bass_kernel_spmd

F32 = mybir.dt.float32
BF16 = mybir.dt.bfloat16
AF = mybir.ActivationFunctionType
ALU = mybir.AluOpType
BF16NP = ml_dtypes.bfloat16

ENG_ATTR = {"pe": "tensor", "act": "scalar", "dve": "vector", "pool": "gpsimd", "sp": "sync"}
NCH = 8
P = 128


class Prog:
    def __init__(self, nc):
        self.nc = nc
        self.ops = []
        self.last_write = {}
        self.readers = {}
        self._ctx = []
        self.tensors = {}
        self.dma_counts = {}
        self.seg = 0
        self.key_seg = {}

    def _enter(self, cm):
        v = cm.__enter__()
        self._ctx.append(cm)
        return v

    def sbuf(self, name, shape, dtype, side=None):
        cm = self.nc.sbuf_tensor(name, list(shape), dtype, side=side)
        t = cm.__enter__()
        self._ctx.append(cm)
        self.tensors[name] = t
        self._cms = getattr(self, "_cms", {})
        self._cms[name] = cm
        st = getattr(self, "_scopes", None)
        if side == "right" and st:
            st[-1].append(name)
        return t

    def push_scope(self):
        if not hasattr(self, "_scopes"):
            self._scopes = []
        self._scopes.append([])

    def pop_scope(self):
        for name in reversed(self._scopes.pop()):
            self.free(name)

    def free(self, name):
        cm = self._cms.pop(name)
        self._ctx.remove(cm)
        cm.__exit__(None, None, None)
        for sc in getattr(self, "_scopes", []):
            if name in sc:
                sc.remove(name)

    def phase(self):
        self.seg += 1
        self.ops.append(dict(eng=None, marker=True, dma=None, deps=set(), raw=set(),
                             need_inc=False, label="phase", idx=len(self.ops)))

    def psum(self, name, shape, dtype):
        t = self._enter(self.nc.psum_tensor(name, list(shape), dtype))
        self.tensors[name] = t
        return t

    def op(self, eng, fn, reads=(), writes=(), dma=None, label=""):
        i = len(self.ops)
        deps = set()
        raw = set()
        for t in reads:
            w = self.last_write.get(t)
            if w is not None:
                deps.add(w)
                raw.add(w)
        for t in writes:
            w = self.last_write.get(t)
            if w is not None:
                deps.add(w)
            for r in self.readers.get(t, ()):
                deps.add(r)
        o = dict(eng=eng, fn=fn, dma=dma, deps=deps, raw=raw, need_inc=False, label=label, idx=i)
        self.ops.append(o)
        for t in reads:
            self.readers.setdefault(t, []).append(i)
        for t in writes:
            self.last_write[t] = i
            self.readers[t] = []
        if dma is not None:
            self.dma_counts[dma] = self.dma_counts.get(dma, 0) + 1
            o["dma_seq"] = self.dma_counts[dma]
            self.key_seg.setdefault(dma, set()).add(self.seg)
        return i

    def finalize(self, final_waits=()):
        ops = self.ops

        def edge_needed(pr, o, d):
            if pr["dma"] is not None:
                return True
            if pr["eng"] != o["eng"]:
                return True
            if pr["eng"] == "pe":
                return False
            return d in o["raw"]

        for o in ops:
            for d in o["deps"]:
                pr = ops[d]
                if not edge_needed(pr, o, d):
                    continue
                pr["need_inc"] = True
        for o in ops:
            if o["dma"] is not None:
                o["need_inc"] = True
        seq = {e: 0 for e in ENG_ATTR}
        for o in ops:
            if o["dma"] is None and o["need_inc"]:
                seq[o["eng"]] += 1
                o["seq"] = seq[o["eng"]]
        nc = self.nc
        self.eng_sems = {}
        for e in ENG_ATTR:
            if seq[e] > 0:
                self.eng_sems[e] = self._enter(nc.semaphore(f"sem_{e}"))
        keys_in_order = []
        seen = set()
        for o in ops:
            if o["dma"] is not None and o["dma"] not in seen:
                seen.add(o["dma"])
                keys_in_order.append(o["dma"])
        slot_of = {}
        slot_segs = []
        for key in keys_in_order:
            segs = self.key_seg[key]
            for s_i, ss in enumerate(slot_segs):
                if not (ss & segs):
                    slot_of[key] = s_i
                    ss.update(segs)
                    break
            else:
                slot_of[key] = len(slot_segs)
                slot_segs.append(set(segs))
        assert len(slot_segs) <= 90, f"dma sem pool overflow: {len(slot_segs)}"
        pool = [self._enter(nc.semaphore(f"dpool{i}")) for i in range(len(slot_segs))]
        slot_total = [0] * len(slot_segs)
        self.key_base = {}
        for key in keys_in_order:
            s_i = slot_of[key]
            self.key_base[key] = slot_total[s_i]
            slot_total[s_i] += 16 * self.dma_counts[key]
        self.dma_sems = {key: pool[slot_of[key]] for key in keys_in_order}
        for o in ops:
            wd = {}
            for d in sorted(o["deps"]):
                pr = ops[d]
                if not edge_needed(pr, o, d):
                    continue
                if pr["dma"] is not None:
                    k = (pr["dma"], True)
                    wd[k] = max(wd.get(k, 0), 16 * pr["dma_seq"])
                else:
                    k = (pr["eng"], False)
                    wd[k] = max(wd.get(k, 0), pr["seq"])
            o["waits"] = wd
        segments = [[]]
        for o in ops:
            if o.get("marker"):
                segments.append([])
            else:
                segments[-1].append(o)
        allp = {e: [] for e in ENG_ATTR}
        for o in ops:
            if not o.get("marker"):
                allp[o["eng"]].append(o)
        self._final_eng = "sp" if allp["sp"] else next(e for e in ENG_ATTR if allp[e])
        satisfied = {e: {} for e in ENG_ATTR}

        def mk_body(e, eng_ops, last):
            def body(engine):
                sat = satisfied[e]
                for o in eng_ops:
                    for (s, isdma), c in sorted(o["waits"].items(), key=lambda kv: str(kv[0])):
                        if isdma:
                            c = c + self.key_base[s]
                        if sat.get((s, isdma), 0) >= c:
                            continue
                        sem = self.dma_sems[s] if isdma else self.eng_sems[s]
                        engine.wait_ge(sem, c)
                        sat[(s, isdma)] = c
                    instr = o["fn"](engine)
                    if o["dma"] is not None:
                        instr.then_inc(self.dma_sems[o["dma"]], 16)
                    elif o["need_inc"]:
                        instr.then_inc(self.eng_sems[e], 1)
                if last and e == self._final_eng:
                    for key in final_waits:
                        engine.wait_ge(self.dma_sems[key],
                                       self.key_base[key] + 16 * self.dma_counts[key])
            return body

        for si, seg in enumerate(segments):
            last = si == len(segments) - 1
            per_eng = {e: [] for e in ENG_ATTR}
            for o in seg:
                per_eng[o["eng"]].append(o)
            with self.nc.Block() as block:
                for e in ENG_ATTR:
                    if not per_eng[e] and not (last and e == self._final_eng):
                        continue
                    getattr(block, ENG_ATTR[e])(mk_body(e, per_eng[e], last))

    def close(self):
        while self._ctx:
            self._ctx.pop().__exit__(None, None, None)



def to_fm(x):
    return np.ascontiguousarray(x.T)


def wT(w):
    return np.ascontiguousarray(np.transpose(w))


def chunk_cols(vec):
    return np.ascontiguousarray(vec.reshape(NCH, P).T)



def load_fm(p, dram_ap, sb_chunks, name, T, dcol0=0, col0=0, eng="sp", nch=NCH, extra_tag=None,
            rtag=None):
    for k in range(nch):
        tags = [(name, k)] if extra_tag is None else [(name, k), extra_tag]
        reads = ((rtag, k),) if rtag is not None else ()
        p.op(eng, (lambda k: lambda e: e.dma_start(
            sb_chunks[k][:, col0:col0 + T], dram_ap[P * k:P * (k + 1), dcol0:dcol0 + T]))(k),
            reads=reads, writes=tags, dma=f"L{name}{k}", label=f"load_{name}_{k}")


def store_fm(p, dram_ap, sb_chunks, name, T, dcol0=0, col0=0, eng="sp", nch=NCH, wtag=None):
    keys = []
    for k in range(nch):
        key = f"S{name}{k}"
        wr = ((wtag, k),) if wtag is not None else ()
        p.op(eng, (lambda k: lambda e: e.dma_start(
            dram_ap[P * k:P * (k + 1), dcol0:dcol0 + T], sb_chunks[k][:, col0:col0 + T]))(k),
            reads=((name, k),), writes=wr, dma=key, label=f"store_{name}_{k}")
        keys.append(key)
    return keys


class LNCtx:

    def __init__(self, p: Prog, banks=None):
        self.p = p
        if banks is None:
            self.bank_s = p.psum("ln_s", [P, 512], F32)
            self.bank_q = p.psum("ln_q", [P, 512], F32)
            self.bank_mb = p.psum("ln_mb", [P, 512], F32)
            self.bank_rb = p.psum("ln_rb", [P, 512], F32)
            self.tg_s, self.tg_q, self.tg_mb, self.tg_rb = ("ln_s", 0), ("ln_q", 0), ("ln_mb", 0), ("ln_rb", 0)
        else:
            (self.bank_s, self.tg_s), (self.bank_q, self.tg_q), \
                (self.bank_mb, self.tg_mb), (self.bank_rb, self.tg_rb) = banks
        self.onesc = p.sbuf("ln_onesc", [P, 1], F32)
        self.onesr = p.sbuf("ln_onesr", [1, P], F32)
        self.xsq = [p.sbuf(f"ln_xsq{i}", [P, 512], F32) for i in range(2)]
        self.rows = p.sbuf("ln_rows", [1, 512 * 4], F32)
        self.tmp = p.sbuf("ln_tmp", [P, 512], F32)
        self.epst = p.sbuf("ln_eps", [1, 1], F32)
        p.op("pool", lambda e: e.memset(self.epst[:], 1e-5), writes=(("ln_eps", 0),))
        p.op("pool", lambda e: e.memset(self.onesc[:], 1.0), writes=(("ln_onesc", 0),))
        p.op("pool", lambda e: e.memset(self.onesr[:], 1.0), writes=(("ln_onesr", 0),))


def ln_fm(p, c: LNCtx, x_chunks, x_name, T, z_chunks=None, z_name=None, col0=0, zcol0=0,
          zf_chunks=None, zf_name=None, eps=1e-5, dim=1024, x_col_tags=None, col0s=None,
          gain_ap=None, gain_tag=None, bias_ap=None):
    if col0s is None:
        col0s = [col0] * NCH
    assert T % 2 == 0 and zcol0 % 2 == 0 and all(cc % 2 == 0 for cc in col0s), (T, col0s, zcol0)
    nsq = [0]
    for t0 in range(0, T, 512):
        tw = min(512, T - t0)
        sq_tags = []
        for k in range(NCH):
            xc0 = col0s[k] + t0
            xtag = (x_name, k) if x_col_tags is None else x_col_tags(k)
            sq = nsq[0] % 2
            nsq[0] += 1
            p.op("act", (lambda k, sq, xc0, tw: lambda e: e.activation(
                c.xsq[sq][:, 0:tw], x_chunks[k][:, xc0:xc0 + tw], AF.Square))(k, sq, xc0, tw),
                reads=(xtag,), writes=(("ln_xsq", sq),), label=f"lnsq{k}")
            sq_tags.append(("ln_xsq", sq))
            p.op("pe", (lambda k, xc0, tw: lambda e: e.matmul(
                c.bank_s[0:1, 0:tw], c.onesc[:], x_chunks[k][:, xc0:xc0 + tw],
                start=(k == 0), stop=(k == NCH - 1)))(k, xc0, tw),
                reads=(xtag, ("ln_onesc", 0)), writes=(c.tg_s,), label=f"lnS{k}")
            p.op("pe", (lambda k, sq, tw: lambda e: e.matmul(
                c.bank_q[0:1, 0:tw], c.onesc[:], c.xsq[sq][:, 0:tw],
                start=(k == 0), stop=(k == NCH - 1)))(k, sq, tw),
                reads=(("ln_xsq", sq), ("ln_onesc", 0)), writes=(c.tg_q,), label=f"lnQ{k}")
        R = c.rows
        p.op("dve", (lambda tw: lambda e: e.tensor_scalar(
            R[0:1, 0:tw], c.bank_s[0:1, 0:tw], -1.0 / dim, None, ALU.mult))(tw),
            reads=(c.tg_s,), writes=(("ln_rows", 0),), label="ln_negmean")
        p.op("dve", (lambda tw: lambda e: e.tensor_scalar(
            R[0:1, 512:512 + tw], c.bank_q[0:1, 0:tw], 1.0 / dim, None, ALU.mult))(tw),
            reads=(c.tg_q,), writes=(("ln_rows", 1),), label="ln_qr")
        p.op("dve", (lambda tw: lambda e: e.tensor_tensor(
            R[0:1, 1024:1024 + tw], R[0:1, 0:tw], R[0:1, 0:tw], ALU.mult))(tw),
            reads=(("ln_rows", 0),), writes=(("ln_rows", 2),), label="ln_msq")
        p.op("dve", (lambda tw: lambda e: e.tensor_tensor(
            R[0:1, 1024:1024 + tw], R[0:1, 512:512 + tw], R[0:1, 1024:1024 + tw], ALU.subtract))(tw),
            reads=(("ln_rows", 1), ("ln_rows", 2)), writes=(("ln_rows", 2),), label="ln_var")
        p.op("act", (lambda tw: lambda e: e.activation(
            R[0:1, 1536:1536 + tw], R[0:1, 1024:1024 + tw], AF.Sqrt, bias=c.epst[0:1, 0:1]))(tw),
            reads=(("ln_rows", 2), ("ln_eps", 0)), writes=(("ln_rows", 3),), label="ln_sd")
        p.op("dve", (lambda tw: lambda e: e.reciprocal(
            R[0:1, 1536:1536 + tw], R[0:1, 1536:1536 + tw]))(tw),
            reads=(("ln_rows", 3),), writes=(("ln_rows", 3),), label="ln_rstd")
        p.op("pe", (lambda tw: lambda e: e.matmul(
            c.bank_mb[:, 0:tw], c.onesr[:], R[0:1, 0:tw], start=True, stop=True))(tw),
            reads=(("ln_rows", 0), ("ln_onesr", 0)), writes=(c.tg_mb,), label="ln_bmean")
        p.op("pe", (lambda tw: lambda e: e.matmul(
            c.bank_rb[:, 0:tw], c.onesr[:], R[0:1, 1536:1536 + tw], start=True, stop=True))(tw),
            reads=(("ln_rows", 3), ("ln_onesr", 0)), writes=(c.tg_rb,), label="ln_brstd")
        for k in range(NCH):
            xc0 = col0s[k] + t0
            xtag = (x_name, k) if x_col_tags is None else x_col_tags(k)
            p.op("dve", (lambda k, xc0, tw: lambda e: e.tensor_tensor(
                c.tmp[:, 0:tw], x_chunks[k][:, xc0:xc0 + tw], c.bank_mb[:, 0:tw], ALU.add))(k, xc0, tw),
                reads=(xtag, c.tg_mb), writes=(("ln_tmp", 0),), label=f"ln_sub{k}")
            if gain_ap is not None:
                p.op("dve", (lambda k, tw: lambda e: e.tensor_tensor(
                    c.tmp[:, 0:tw], c.tmp[:, 0:tw], c.bank_rb[:, 0:tw], ALU.mult))(k, tw),
                    reads=(("ln_tmp", 0), c.tg_rb), writes=(("ln_tmp", 0),), label=f"ln_rs{k}")
                p.op("dve", (lambda k, t0, tw: lambda e: e.tensor_scalar(
                    z_chunks[k][:, zcol0 + t0:zcol0 + t0 + tw], c.tmp[:, 0:tw],
                    gain_ap[k], bias_ap[k], ALU.mult, ALU.add))(k, t0, tw),
                    reads=(("ln_tmp", 0), gain_tag), writes=((z_name, k),), label=f"ln_zg{k}")
                continue
            if z_chunks is not None:
                p.op("dve", (lambda k, t0, tw: lambda e: e.tensor_tensor(
                    z_chunks[k][:, zcol0 + t0:zcol0 + t0 + tw], c.tmp[:, 0:tw],
                    c.bank_rb[:, 0:tw], ALU.mult))(k, t0, tw),
                    reads=(("ln_tmp", 0), c.tg_rb), writes=((z_name, k),), label=f"ln_z{k}")
            if zf_chunks is not None:
                p.op("dve", (lambda k, t0, tw: lambda e: e.tensor_tensor(
                    zf_chunks[k][:, zcol0 + t0:zcol0 + t0 + tw], c.tmp[:, 0:tw],
                    c.bank_rb[:, 0:tw], ALU.mult))(k, t0, tw),
                    reads=(("ln_tmp", 0), c.tg_rb), writes=((zf_name, k),), label=f"ln_zf{k}")


class ProjWeights:

    def __init__(self, p, dram_ap, K, M, name, nbuf=2, max_mw=P, dtype=BF16, side=None):
        self.p, self.dram, self.K, self.M, self.name = p, dram_ap, K, M, name
        self.nkc = (K + P - 1) // P
        self.max_mw = max_mw
        self.bufs = [p.sbuf(f"{name}_b{i}", [P, self.nkc * max_mw], dtype, side=side) for i in range(nbuf)]
        self.nbuf = nbuf
        self.uses = 0

    def free(self):
        for i in reversed(range(self.nbuf)):
            self.p.free(f"{self.name}_b{i}")

    def load(self, mi, m_width=None, dram=None, m_ofs=0):
        p = self.p
        mw = m_width or self.max_mw
        b = self.uses % self.nbuf
        self.uses += 1
        tag = (self.name, b)
        m0 = mi * self.max_mw
        mw = min(mw, self.M - m0)
        kp = min(P, self.K)
        d = self.dram if dram is None else dram
        kfull = self.K // P
        slow = (mw * self.nkc <= 64)
        if kfull > 0:
            src = d[0:kfull * P, m_ofs + m0:m_ofs + m0 + mw].rearrange("(kc q) m -> q kc m", q=kp)
            dst = self.bufs[b][0:kp, 0:kfull * mw].rearrange("q (kc m) -> q kc m", m=mw)
            p.op("pool", (lambda b, src, dst, slow: lambda e: e.dma_start(
                dst, src, allow_slow_non_contiguous=slow))(b, src, dst, slow),
                writes=(tag,), dma=f"W{self.name}{b}", label=f"w_{self.name}_m{mi}")
        krem = self.K - kfull * P
        if krem > 0:
            src2 = d[kfull * P:self.K, m_ofs + m0:m_ofs + m0 + mw]
            dst2 = self.bufs[b][0:krem, kfull * mw:kfull * mw + mw]
            p.op("pool", (lambda b, src2, dst2: lambda e: e.dma_start(
                dst2, src2))(b, src2, dst2),
                writes=(tag,), dma=f"W{self.name}{b}r", label=f"w_{self.name}_m{mi}r")
        return b, tag, mw

    def tile2(self, b, kc, m0, mw, loaded_mw=None):
        lw = loaded_mw if loaded_mw is not None else self.max_mw
        kpart = min(P, self.K - kc * P)
        return self.bufs[b][0:kpart, kc * lw + m0:kc * lw + m0 + mw]

    def tile(self, b, kc, mw):
        kpart = min(P, self.K - kc * P)
        return self.bufs[b][0:kpart, kc * mw:kc * mw + mw]


def proj_sweep(p, W: ProjWeights, z_chunks, z_name, T, banks, bank_names, epilogue,
               m_chunks=None, tile_T=512, z_col0=0, kc_list=None, w_m_ofs=0, w_dram=None):
    kcs = kc_list if kc_list is not None else list(range(W.nkc))
    nk = len(kcs)
    mw0 = W.max_mw
    mcs = m_chunks if m_chunks is not None else list(range((W.M + mw0 - 1) // mw0))
    bi = 0
    for mi in mcs:
        b, wtag, mw = W.load(mi, m_ofs=w_m_ofs, dram=w_dram)
        tts = []
        for t0 in range(0, T, tile_T):
            tw = min(tile_T, T - t0)
            bank = banks[bi % len(banks)]
            btag = bank_names[bi % len(banks)]
            bi += 1
            tts.append((bank, btag, t0, tw))
        for ki, kc in enumerate(kcs):
            kpart = min(P, W.K - kc * P)
            for (bank, btag, t0, tw) in tts:
                p.op("pe", (lambda b, kc, t0, tw, bank, ki, mw, kpart: lambda e: e.matmul(
                    bank[0:mw, 0:tw], W.tile(b, kc, mw)[0:kpart, 0:mw],
                    z_chunks[kc][0:kpart, z_col0 + t0:z_col0 + t0 + tw],
                    start=(ki == 0), stop=(ki == nk - 1)))(b, kc, t0, tw, bank, ki, mw, kpart),
                    reads=(wtag, (z_name, kc)), writes=(btag,),
                    label=f"mm_{W.name}_m{mi}k{kc}t{t0}")
        for (bank, btag, t0, tw) in tts:
            epilogue(mi, t0, tw, bank, btag)



def build_l0ffn(Tloc=4096, TB=1024, scale=0.5):
    nc = bass.Bass("TRN2", target_bir_lowering=False)
    x_ext = nc.dram_tensor("x_ext", [1024, 2 + Tloc], F32, kind="ExternalInput").ap()
    wrT = nc.dram_tensor("wrT", [1024, 1024], F32, kind="ExternalInput").ap()
    wkT = nc.dram_tensor("wkT", [1024, 1024], F32, kind="ExternalInput").ap()
    wvT = nc.dram_tensor("wvT", [1024, 1024], F32, kind="ExternalInput").ap()
    mu = nc.dram_tensor("mu", [128, 16], F32, kind="ExternalInput").ap()
    xout = nc.dram_tensor("xout", [1024, Tloc], F32, kind="ExternalOutput").ap()

    p = Prog(nc)
    c = LNCtx(p)
    xc = [p.sbuf(f"x{k}", [P, TB + 2], F32) for k in range(NCH)]
    zc = [p.sbuf(f"z{k}", [P, TB + 2], BF16) for k in range(NCH)]
    xr = [p.sbuf(f"xr{k}", [P, TB], BF16) for k in range(NCH)]
    xk = [p.sbuf(f"xk{k}", [P, TB], BF16) for k in range(NCH)]
    rr = [p.sbuf(f"r{k}", [P, TB], F32) for k in range(NCH)]
    krel = [p.sbuf(f"krel{k}", [P, TB], BF16) for k in range(NCH)]
    kact = [p.sbuf(f"kact{k}", [P, TB], BF16) for k in range(NCH)]
    mus = p.sbuf("mus", [P, 32], F32)
    mtmp = p.sbuf("mtmp", [P, TB], F32)
    banks = [p.psum(f"pb{i}", [P, 512], F32) for i in range(4)]
    bank_names = [("pb", i) for i in range(4)]

    Wr = ProjWeights(p, wrT, 1024, 1024, "wr")
    Wk = ProjWeights(p, wkT, 1024, 1024, "wk")
    Wv = ProjWeights(p, wvT, 1024, 1024, "wv")

    p.op("sp", lambda e: e.dma_start(mus[:, 0:16], mu[:, 0:16]), writes=(("mus", 0),), dma="mus")
    p.op("dve", lambda e: e.tensor_scalar(mus[:, 16:32], mus[:, 0:16], -1.0, 1.0, ALU.mult, ALU.add),
         reads=(("mus", 0),), writes=(("mus", 1),))

    out_keys = []
    nblk = Tloc // TB
    for blk in range(nblk):
        load_fm(p, x_ext, xc, "x", TB + 2, dcol0=blk * TB)
        ln_fm(p, c, xc, "x", TB + 2, z_chunks=zc, z_name="z")
        for k in range(NCH):
            for (dst, dname, mcol) in ((xr, "xr", 0), (xk, "xk", 8)):
                p.op("dve", (lambda k, mcol: lambda e: e.tensor_scalar(
                    mtmp[:, 0:TB], zc[k][:, 1:TB + 1], mus[:, 16 + mcol + k:17 + mcol + k], None, ALU.mult))(k, mcol),
                    reads=(("z", k), ("mus", 1)), writes=(("mtmp", 0),), label=f"mixa{k}")
                p.op("dve", (lambda k, dst, mcol: lambda e: e.scalar_tensor_tensor(
                    dst[k][:, 0:TB], zc[k][:, 2:TB + 2], mus[:, mcol + k:mcol + k + 1],
                    mtmp[:, 0:TB], ALU.mult, ALU.add))(k, dst, mcol),
                    reads=(("z", k), ("mus", 0), ("mtmp", 0)), writes=((dname, k),), label=f"mixb{k}")
        def epi_r(mi, t0, tw, bank, btag):
            p.op("act", (lambda mi, t0, tw, bank: lambda e: e.activation(
                rr[mi][:, t0:t0 + tw], bank[:, 0:tw], AF.Sigmoid))(mi, t0, tw, bank),
                reads=(btag,), writes=(("r", mi),), label=f"sig{mi}")
        proj_sweep(p, Wr, xr, "xr", TB, banks, bank_names, epi_r)
        def epi_k(mi, t0, tw, bank, btag):
            p.op("act", (lambda mi, t0, tw, bank: lambda e: e.activation(
                krel[mi][:, t0:t0 + tw], bank[:, 0:tw], AF.Relu))(mi, t0, tw, bank),
                reads=(btag,), writes=(("krel", mi),), label=f"relu{mi}")
        proj_sweep(p, Wk, xk, "xk", TB, banks, bank_names, epi_k)
        for k in range(NCH):
            p.op("pool", (lambda k: lambda e: e.tensor_mul(
                kact[k][:, 0:TB], krel[k][:, 0:TB], krel[k][:, 0:TB]))(k),
                reads=(("krel", k),), writes=(("kact", k),), label=f"sq{k}")
        def epi_v(mi, t0, tw, bank, btag):
            p.op("dve", (lambda mi, t0, tw, bank: lambda e: e.tensor_tensor(
                mtmp[:, 0:tw], rr[mi][:, t0:t0 + tw], bank[:, 0:tw], ALU.mult))(mi, t0, tw, bank),
                reads=(("r", mi), btag), writes=(("mtmp", 0),), label=f"rkv{mi}")
            p.op("dve", (lambda mi, t0, tw: lambda e: e.scalar_tensor_tensor(
                xc[mi][:, 2 + t0:2 + t0 + tw], mtmp[:, 0:tw], scale,
                xc[mi][:, 2 + t0:2 + t0 + tw], ALU.mult, ALU.add))(mi, t0, tw),
                reads=(("mtmp", 0), ("x", mi)), writes=(("x", mi),), label=f"acc{mi}")
        proj_sweep(p, Wv, kact, "kact", TB, banks, bank_names, epi_v)
        keys = store_fm(p, xout, xc, "x", TB, dcol0=blk * TB, col0=2)
        out_keys = keys
    p.finalize(final_waits=out_keys)
    return nc


def stage_l0ffn(x_fm_batch, h, W_r, W_k, W_v, mu_r, mu_k, Tloc=4096):
    t0 = h * Tloc
    x_ext = np.zeros((1024, 2 + Tloc), np.float32)
    x_ext[:, 2:] = x_fm_batch[:, t0:t0 + Tloc]
    if t0 > 0:
        x_ext[:, 0:2] = x_fm_batch[:, t0 - 2:t0]
    mu = np.concatenate([chunk_cols(mu_r), chunk_cols(mu_k)], axis=1).astype(np.float32)
    return {"x_ext": x_ext, "wrT": wT(W_r), "wkT": wT(W_k), "wvT": wT(W_v),
            "mu": np.ascontiguousarray(mu)}



TWIST_SRC = [(4, 1.0), (5, 1.0), (6, -1.0), (7, -1.0), (0, -1.0), (1, -1.0), (2, 1.0), (3, 1.0)]


def build_l2ffn(Tloc=4096, TB=1024, Tfull=8192, scale=0.5):
    nc = bass.Bass("TRN2", target_bir_lowering=False)
    x_full = nc.dram_tensor("x_full", [1024, Tfull], F32, kind="ExternalInput").ap()
    x_own = nc.dram_tensor("x_own", [1024, Tloc], F32, kind="ExternalInput").ap()
    w_cn1 = nc.dram_tensor("w_cn1", [1024, 1024], F32, kind="ExternalInput").ap()
    w_cn2 = nc.dram_tensor("w_cn2", [1024, 512], F32, kind="ExternalInput").ap()
    w_cn3 = nc.dram_tensor("w_cn3", [512, 256], F32, kind="ExternalInput").ap()
    w_cn4 = nc.dram_tensor("w_cn4", [256, 1], F32, kind="ExternalInput").ap()
    w_gc1 = nc.dram_tensor("w_gc1", [1024, 512], F32, kind="ExternalInput").ap()
    w_gc2 = nc.dram_tensor("w_gc2", [512, 256], F32, kind="ExternalInput").ap()
    w_gc3 = nc.dram_tensor("w_gc3", [256, 1], F32, kind="ExternalInput").ap()
    cnb1 = nc.dram_tensor("cnb1", [128, 8], F32, kind="ExternalInput").ap()
    cnb2 = nc.dram_tensor("cnb2", [128, 4], F32, kind="ExternalInput").ap()
    cnb3 = nc.dram_tensor("cnb3", [128, 2], F32, kind="ExternalInput").ap()
    gcb1 = nc.dram_tensor("gcb1", [128, 4], F32, kind="ExternalInput").ap()
    gcb2 = nc.dram_tensor("gcb2", [128, 2], F32, kind="ExternalInput").ap()
    scl = nc.dram_tensor("scl", [1, 4], F32, kind="ExternalInput").ap()
    xout = nc.dram_tensor("xout", [1024, Tloc], F32, kind="ExternalOutput").ap()

    p = Prog(nc)
    banks = [p.psum(f"pb{i}", [P, 512], F32) for i in range(8)]
    bts = [("pb", i) for i in range(8)]
    c = LNCtx(p, banks=[(banks[i], bts[i]) for i in range(4)])
    sw_banks, sw_bts = banks[0:6], bts[0:6]
    cp_bank, cp_bt = banks[6], bts[6]

    xc = [p.sbuf(f"x{k}", [P, TB], F32) for k in range(NCH)]
    ob = [[p.sbuf(f"o{s}_{k}", [P, TB], F32) for k in range(NCH)] for s in range(2)]
    zb = [p.sbuf(f"zb{k}", [P, TB], BF16) for k in range(NCH)]
    h1b = [p.sbuf(f"h1b{k}", [P, TB], BF16) for k in range(NCH)]
    h2b = [p.sbuf(f"h2b{k}", [P, TB], BF16) for k in range(4)]
    h3b = [p.sbuf(f"h3b{k}", [P, TB], BF16) for k in range(2)]
    tfr = p.sbuf("tfr", [1, TB], F32)
    cpr = p.sbuf("cpr", [1, TB], F32)
    mtmp = p.sbuf("mtmp", [P, 512], F32)
    macc = p.sbuf("macc", [P, NCH], F32)
    rtmp = p.sbuf("rtmp", [P, 1], F32)
    gcb = p.sbuf("gcb", [P, NCH], BF16)
    h1g = p.sbuf("h1g", [P, 4], BF16)
    h2g = p.sbuf("h2g", [P, 2], BF16)
    rowb = p.sbuf("rowb", [1, 8], F32)
    cb = {}
    for nm, t, w in (("cnb1", cnb1, 8), ("cnb2", cnb2, 4), ("cnb3", cnb3, 2),
                     ("gcb1", gcb1, 4), ("gcb2", gcb2, 2), ("scl", scl, 4)):
        buf = p.sbuf("s" + nm, [P if nm != "scl" else 1, w], F32)
        p.op("sp", (lambda buf, t: lambda e: e.dma_start(buf[:], t[:]))(buf, t),
             writes=((nm, 0),), dma="L" + nm)
        cb[nm] = buf

    PWcn1 = ProjWeights(p, w_cn1, 1024, 1024, "cn1")
    PWcn2 = ProjWeights(p, w_cn2, 1024, 512, "cn2")
    PWcn3 = ProjWeights(p, w_cn3, 512, 256, "cn3")
    PWcn4 = ProjWeights(p, w_cn4, 256, 1, "cn4")
    PWgc1 = ProjWeights(p, w_gc1, 1024, 512, "gc1")
    PWgc2 = ProjWeights(p, w_gc2, 512, 256, "gc2")
    PWgc3 = ProjWeights(p, w_gc3, 256, 1, "gc3")

    p.op("pool", lambda e: e.memset(macc[:], 0.0), writes=(("macc", 0),))
    zs = ob[1]
    for blk in range(Tfull // TB):
        load_fm(p, x_full, xc, "x", TB, dcol0=blk * TB)
        ln_fm(p, c, xc, "x", TB, zf_chunks=zs, zf_name="o1")
        for k in range(NCH):
            p.op("dve", (lambda k: lambda e: e.reduce_sum(
                rtmp[:], zs[k][:, 0:TB], axis=mybir.AxisListType.X))(k),
                reads=(("o1", k),), writes=(("rtmp", 0),), label=f"gsum{k}")
            p.op("dve", (lambda k: lambda e: e.tensor_tensor(
                macc[:, k:k + 1], macc[:, k:k + 1], rtmp[:], ALU.add))(k),
                reads=(("rtmp", 0), ("macc", 0)), writes=(("macc", 0),), label=f"gacc{k}")
    p.op("dve", lambda e: e.tensor_scalar(gcb[:], macc[:], 1.0 / Tfull, None, ALU.mult),
         reads=(("macc", 0),), writes=tuple(("gcb", k) for k in range(NCH)), label="gmean")
    def epi_g1(mi, t0, tw, bank, btag):
        p.op("act", (lambda mi, bank: lambda e: e.activation(
            h1g[:, mi:mi + 1], bank[:, 0:1], AF.Gelu, bias=cb["gcb1"][:, mi:mi + 1]))(mi, bank),
            reads=(btag, ("gcb1", 0)), writes=(("h1g", mi),), label=f"g1_{mi}")
    proj_sweep(p, PWgc1, [gcb[:, k:k + 1] for k in range(NCH)], "gcb", 1, sw_banks, sw_bts, epi_g1, tile_T=512)
    def epi_g2(mi, t0, tw, bank, btag):
        p.op("act", (lambda mi, bank: lambda e: e.activation(
            h2g[:, mi:mi + 1], bank[:, 0:1], AF.Gelu, bias=cb["gcb2"][:, mi:mi + 1]))(mi, bank),
            reads=(btag, ("gcb2", 0)), writes=(("h2g", mi),), label=f"g2_{mi}")
    proj_sweep(p, PWgc2, [h1g[:, k:k + 1] for k in range(4)], "h1g", 1, sw_banks, sw_bts, epi_g2, tile_T=512)
    def epi_g3(mi, t0, tw, bank, btag):
        p.op("act", (lambda bank: lambda e: e.activation(
            rowb[0:1, 0:1], bank[0:1, 0:1], AF.Sigmoid, bias=cb["scl"][0:1, 1:2]))(bank),
            reads=(btag, ("scl", 0)), writes=(("rowb", 0),), label="gf")
    proj_sweep(p, PWgc3, [h2g[:, k:k + 1] for k in range(2)], "h2g", 1, sw_banks, sw_bts, epi_g3, tile_T=512)
    p.op("dve", lambda e: e.tensor_scalar(rowb[0:1, 3:4], cb["scl"][0:1, 2:3], 2.0, None, ALU.mult),
         reads=(("scl", 0),), writes=(("rowb", 3),))
    p.op("dve", lambda e: e.tensor_scalar(rowb[0:1, 4:5], cb["scl"][0:1, 2:3], -1.0, 0.1, ALU.mult, ALU.add),
         reads=(("scl", 0),), writes=(("rowb", 4),))
    p.op("dve", lambda e: e.tensor_scalar(rowb[0:1, 5:6], rowb[0:1, 0:1], 0.7, None, ALU.mult),
         reads=(("rowb", 0),), writes=(("rowb", 5),))
    p.op("dve", lambda e: e.tensor_tensor(rowb[0:1, 5:6], rowb[0:1, 5:6], rowb[0:1, 3:4], ALU.mult),
         reads=(("rowb", 5), ("rowb", 3)), writes=(("rowb", 5),))
    p.op("dve", lambda e: e.tensor_tensor(rowb[0:1, 2:3], rowb[0:1, 5:6], rowb[0:1, 4:5], ALU.add),
         reads=(("rowb", 5), ("rowb", 4)), writes=(("rowb", 2),))
    p.op("dve", lambda e: e.tensor_scalar(rowb[0:1, 1:2], rowb[0:1, 3:4], 0.3, None, ALU.mult),
         reads=(("rowb", 3),), writes=(("rowb", 1),))

    out_keys = []
    for blk in range(Tloc // TB):
        load_fm(p, x_own, xc, "x", TB, dcol0=blk * TB)
        ln_fm(p, c, xc, "x", TB, z_chunks=zb, z_name="zb", zf_chunks=ob[0], zf_name="o0")
        for cyc in range(3):
            cur, nxt = cyc % 2, (cyc + 1) % 2
            if cyc > 0:
                for k in range(NCH):
                    p.op("pool", (lambda k, cur: lambda e: e.tensor_copy(
                        zb[k][:, 0:TB], ob[cur][k][:, 0:TB]))(k, cur),
                        reads=((f"o{cur}", k),), writes=(("zb", k),), label=f"czb{k}")
            def epi_h1(mi, t0, tw, bank, btag):
                p.op("act", (lambda mi, t0, tw, bank: lambda e: e.activation(
                    h1b[mi][:, t0:t0 + tw], bank[:, 0:tw], AF.Gelu, bias=cb["cnb1"][:, mi:mi + 1]))(mi, t0, tw, bank),
                    reads=(btag, ("cnb1", 0)), writes=(("h1b", mi),), label=f"h1_{mi}")
            proj_sweep(p, PWcn1, zb, "zb", TB, sw_banks, sw_bts, epi_h1)
            def epi_h2(mi, t0, tw, bank, btag):
                p.op("act", (lambda mi, t0, tw, bank: lambda e: e.activation(
                    h2b[mi][:, t0:t0 + tw], bank[:, 0:tw], AF.Gelu, bias=cb["cnb2"][:, mi:mi + 1]))(mi, t0, tw, bank),
                    reads=(btag, ("cnb2", 0)), writes=(("h2b", mi),), label=f"h2_{mi}")
            proj_sweep(p, PWcn2, h1b, "h1b", TB, sw_banks, sw_bts, epi_h2)
            def epi_h3(mi, t0, tw, bank, btag):
                p.op("act", (lambda mi, t0, tw, bank: lambda e: e.activation(
                    h3b[mi][:, t0:t0 + tw], bank[:, 0:tw], AF.Gelu, bias=cb["cnb3"][:, mi:mi + 1]))(mi, t0, tw, bank),
                    reads=(btag, ("cnb3", 0)), writes=(("h3b", mi),), label=f"h3_{mi}")
            proj_sweep(p, PWcn3, h2b, "h2b", TB, sw_banks, sw_bts, epi_h3)
            def epi_h4(mi, t0, tw, bank, btag):
                p.op("act", (lambda t0, tw, bank: lambda e: e.activation(
                    tfr[0:1, t0:t0 + tw], bank[0:1, 0:tw], AF.Sigmoid, bias=cb["scl"][0:1, 0:1]))(t0, tw, bank),
                    reads=(btag, ("scl", 0)), writes=(("tfr", 0),), label="tf")
            proj_sweep(p, PWcn4, h3b, "h3b", TB, sw_banks, sw_bts, epi_h4)
            for t0 in range(0, TB, 512):
                tw = min(512, TB - t0)
                p.op("dve", (lambda t0, tw: lambda e: e.tensor_scalar(
                    cpr[0:1, t0:t0 + tw], tfr[0:1, t0:t0 + tw], rowb[0:1, 1:2], rowb[0:1, 2:3],
                    ALU.mult, ALU.add))(t0, tw),
                    reads=(("tfr", 0), ("rowb", 1), ("rowb", 2)), writes=(("cpr", 0),), label="coup")
                p.op("pe", (lambda t0, tw: lambda e: e.matmul(
                    cp_bank[:, 0:tw], c.onesr[:], cpr[0:1, t0:t0 + tw], start=True, stop=True))(t0, tw),
                    reads=(("cpr", 0), ("ln_onesr", 0)), writes=(cp_bt,), label="coupb")
                for k in range(NCH):
                    src, sgn = TWIST_SRC[k]
                    p.op("dve", (lambda k, src, t0, tw, cur: lambda e: e.tensor_tensor(
                        mtmp[:, 0:tw], ob[cur][src][:, t0:t0 + tw], cp_bank[:, 0:tw], ALU.mult))(k, src, t0, tw, cur),
                        reads=((f"o{cur}", src), cp_bt), writes=(("mtmp", 0),), label=f"tw{k}")
                    p.op("dve", (lambda k, sgn, t0, tw, cur, nxt: lambda e: e.scalar_tensor_tensor(
                        ob[nxt][k][:, t0:t0 + tw], mtmp[:, 0:tw], sgn, ob[cur][k][:, t0:t0 + tw],
                        ALU.mult, ALU.add))(k, sgn, t0, tw, cur, nxt),
                        reads=(("mtmp", 0), (f"o{cur}", k)), writes=((f"o{nxt}", k),), label=f"twa{k}")
        fin = 3 % 2
        for k in range(NCH):
            p.op("dve", (lambda k, fin: lambda e: e.scalar_tensor_tensor(
                xc[k][:, 0:TB], ob[fin][k][:, 0:TB], scale, xc[k][:, 0:TB], ALU.mult, ALU.add))(k, fin),
                reads=((f"o{fin}", k), ("x", k)), writes=(("x", k),), label=f"res{k}")
        out_keys = store_fm(p, xout, xc, "x", TB, dcol0=blk * TB)
    p.finalize(final_waits=out_keys)
    return nc


def stage_l2ffn(x_fm_batch, h, inp, Tloc=4096):
    t0 = h * Tloc
    return {
        "x_full": x_fm_batch,
        "x_own": np.ascontiguousarray(x_fm_batch[:, t0:t0 + Tloc]),
        "w_cn1": np.ascontiguousarray(inp["L2_ffn_cn_w1"]),
        "w_cn2": np.ascontiguousarray(inp["L2_ffn_cn_w2"]),
        "w_cn3": np.ascontiguousarray(inp["L2_ffn_cn_w3"]),
        "w_cn4": np.ascontiguousarray(inp["L2_ffn_cn_w4"]),
        "w_gc1": np.ascontiguousarray(inp["L2_ffn_gc_w1"]),
        "w_gc2": np.ascontiguousarray(inp["L2_ffn_gc_w2"]),
        "w_gc3": np.ascontiguousarray(inp["L2_ffn_gc_w3"]),
        "cnb1": chunk_cols(inp["L2_ffn_cn_b1"]),
        "cnb2": np.ascontiguousarray(inp["L2_ffn_cn_b2"].reshape(4, 128).T),
        "cnb3": np.ascontiguousarray(inp["L2_ffn_cn_b3"].reshape(2, 128).T),
        "gcb1": np.ascontiguousarray(inp["L2_ffn_gc_b1"].reshape(4, 128).T),
        "gcb2": np.ascontiguousarray(inp["L2_ffn_gc_b2"].reshape(2, 128).T),
        "scl": np.array([[inp["L2_ffn_cn_b4"][0], inp["L2_ffn_gc_b3"][0],
                          inp["L2_ffn_adaptive_range"][0], 0.0]], np.float32),
    }



def build_l1ffn(Tloc=4096, TB=512, scale=0.5, E=16, I=512):
    nc = bass.Bass("TRN2", target_bir_lowering=False)
    x_own = nc.dram_tensor("x_own", [1024, Tloc], F32, kind="ExternalInput").ap()
    gwT = nc.dram_tensor("gwT", [1024, E], F32, kind="ExternalInput").ap()
    ebias = nc.dram_tensor("ebias", [P, E], F32, kind="ExternalInput").ap()
    iota16 = nc.dram_tensor("iota16", [P, E], F32, kind="ExternalInput").ap()
    identf = nc.dram_tensor("identf", [P, P], F32, kind="ExternalInput").ap()
    gpT = nc.dram_tensor("gpT", [E, 1024, I], F32, kind="ExternalInput").ap()
    upT = nc.dram_tensor("upT", [E, 1024, I], F32, kind="ExternalInput").ap()
    dpT = nc.dram_tensor("dpT", [E, I, 1024], F32, kind="ExternalInput").ap()
    selm = nc.dram_tensor("selm", [E, E * P], F32, kind="ExternalInput").ap()
    xout = nc.dram_tensor("xout", [1024, Tloc], F32, kind="ExternalOutput").ap()

    p = Prog(nc)
    banks = [p.psum(f"pb{i}", [P, 512], F32) for i in range(8)]
    bts = [("pb", i) for i in range(8)]
    c = LNCtx(p, banks=[(banks[i], bts[i]) for i in range(4)])
    xc = [p.sbuf(f"x{k}", [P, TB], F32) for k in range(NCH)]
    zb = [p.sbuf(f"z{k}", [P, TB], BF16) for k in range(NCH)]
    zf = [p.sbuf(f"zf{k}", [P, TB], F32) for k in range(NCH)]
    gs = [p.sbuf(f"gs{k}", [P, TB], BF16) for k in range(4)]
    us = [p.sbuf(f"us{k}", [P, TB], BF16) for k in range(4)]
    actb = [[p.sbuf(f"act{e}_{k}", [P, TB], BF16) for k in range(4)] for e in range(E)]
    gwsb = p.sbuf("gwsb", [P, NCH * E], F32)
    ebs = p.sbuf("ebs", [P, E], F32)
    iot = p.sbuf("iot", [P, E], F32)
    idf = p.sbuf("idf", [P, P], F32)
    sig = p.sbuf("sig", [P, E], F32)
    sfc = p.sbuf("sfc", [P, E], F32)
    tmp16 = p.sbuf("tmp16", [P, E], F32)
    m1s = p.sbuf("m1s", [P, 4], F32)
    gsc4 = p.sbuf("gsc4", [P, 4], F32)
    gsc16 = p.sbuf("gsc16", [P, E], F32)
    grow = p.sbuf("grow", [P, 4], F32)
    ge4 = p.sbuf("ge4", [P, 4], F32)
    msk4 = p.sbuf("msk4", [P, 4], F32)
    oh = p.sbuf("oh", [P, E], F32)
    wsel = p.sbuf("wsel", [P, E], F32)
    wk = p.sbuf("wk", [P, 1], F32)
    mx8 = p.sbuf("mx8", [P, 8], F32)
    ix8 = p.sbuf("ix8", [P, 8], mybir.dt.uint32)
    ixf = p.sbuf("ixf", [P, 8], F32)
    cmb = p.sbuf("cmb", [P, E], F32)
    ones4 = p.sbuf("ones4", [P, 4], F32)
    comb_fm = p.sbuf("comb_fm", [E, TB], F32)
    selsb = p.sbuf("selsb", [E, E * P], F32)
    mtmp = p.sbuf("mtmp", [P, TB], F32)
    PWg = ProjWeights(p, gpT[0], 1024, I, "gp", nbuf=2, max_mw=I)
    PWu = ProjWeights(p, upT[0], 1024, I, "up", nbuf=2, max_mw=I)
    PWd = ProjWeights(p, dpT[0], I, 1024, "dp", nbuf=2, max_mw=1024)

    p.op("pool", lambda e: e.memset(ones4[:], 1.0), writes=(("ones4", 0),))
    p.op("sp", lambda e: e.dma_start(selsb[:], selm[:]), writes=(("selsb", 0),), dma="Lsel")
    for nm, dst, src_ in (("gw", gwsb, None), ("ebs", ebs, ebias), ("iot", iot, iota16), ("idf", idf, identf)):
        if nm == "gw":
            src = gwT.rearrange("(kc q) m -> q kc m", q=P)
            dstv = gwsb[:, :].rearrange("q (kc m) -> q kc m", m=E)
            p.op("sp", (lambda dstv, src: lambda e: e.dma_start(dstv, src))(dstv, src),
                 writes=(("gw", 0),), dma="Lgw")
        else:
            p.op("sp", (lambda dst, src_: lambda e: e.dma_start(dst[:], src_[:]))(dst, src_),
                 writes=((nm, 0),), dma="L" + nm)

    out_keys = []
    for blk in range(Tloc // TB):
        load_fm(p, x_own, xc, "x", TB, dcol0=blk * TB)
        ln_fm(p, c, xc, "x", TB, z_chunks=zb, z_name="z", zf_chunks=zf, zf_name="zf")
        for ti in range(TB // P):
            t0 = ti * P
            lbank, lbt = banks[6], bts[6]
            for kc in range(NCH):
                p.op("pe", (lambda kc, t0: lambda e: e.matmul(
                    lbank[:, 0:E], zf[kc][:, t0:t0 + P], gwsb[:, kc * E:(kc + 1) * E],
                    start=(kc == 0), stop=(kc == NCH - 1)))(kc, t0),
                    reads=(("zf", kc), ("gw", 0)), writes=(lbt,), label=f"lg{kc}")
            p.op("act", lambda e: e.activation(sig[:], lbank[:, 0:E], AF.Sigmoid),
                 reads=(lbt,), writes=(("sig", 0),), label="sigm")
            p.op("dve", lambda e: e.tensor_tensor(sfc[:], sig[:], ebs[:], ALU.add),
                 reads=(("sig", 0), ("ebs", 0)), writes=(("sfc", 0),), label="sfc")
            for g in range(4):
                sl = (4 * g, 4 * g + 4)
                p.op("dve", (lambda g, sl: lambda e: e.reduce_max(
                    m1s[:, g:g + 1], sfc[:, sl[0]:sl[1]], axis=mybir.AxisListType.X))(g, sl),
                    reads=(("sfc", 0),), writes=(("m1s", g),), label=f"m1{g}")
                p.op("dve", (lambda g, sl: lambda e: e.tensor_scalar(
                    ge4[:], sfc[:, sl[0]:sl[1]], m1s[:, g:g + 1], None, ALU.is_ge))(g, sl),
                    reads=(("sfc", 0), ("m1s", g)), writes=(("ge4", 0),), label=f"ge{g}")
                p.op("dve", (lambda g, sl: lambda e: e.scalar_tensor_tensor(
                    msk4[:], ge4[:], -1e30, sfc[:, sl[0]:sl[1]], ALU.mult, ALU.add))(g, sl),
                    reads=(("ge4", 0), ("sfc", 0)), writes=(("msk4", 0),), label=f"mk{g}")
                p.op("dve", (lambda g: lambda e: e.reduce_max(
                    gsc4[:, g:g + 1], msk4[:], axis=mybir.AxisListType.X))(g),
                    reads=(("msk4", 0),), writes=(("gsc4", g),), label=f"m2{g}")
                p.op("dve", (lambda g: lambda e: e.tensor_tensor(
                    gsc4[:, g:g + 1], gsc4[:, g:g + 1], m1s[:, g:g + 1], ALU.add))(g),
                    reads=(("gsc4", g), ("m1s", g)), writes=(("gsc4", g),), label=f"gs{g}")
            p.op("dve", lambda e: e.reduce_max(grow[:, 0:1], gsc4[:], axis=mybir.AxisListType.X),
                 reads=tuple(("gsc4", g) for g in range(4)), writes=(("grow", 0),), label="gm1")
            p.op("dve", lambda e: e.tensor_scalar(ge4[:], gsc4[:], grow[:, 0:1], None, ALU.is_ge),
                 reads=tuple(("gsc4", g) for g in range(4)) + (("grow", 0),), writes=(("ge4", 0),), label="gge")
            p.op("dve", lambda e: e.scalar_tensor_tensor(msk4[:], ge4[:], -1e30, gsc4[:], ALU.mult, ALU.add),
                 reads=(("ge4", 0),) + tuple(("gsc4", g) for g in range(4)), writes=(("msk4", 0),), label="gmk")
            p.op("dve", lambda e: e.reduce_max(grow[:, 1:2], msk4[:], axis=mybir.AxisListType.X),
                 reads=(("msk4", 0),), writes=(("grow", 1),), label="gm2")
            for g in range(4):
                p.op("dve", (lambda g: lambda e: e.tensor_scalar(
                    gsc16[:, 4 * g:4 * g + 4], ones4[:], gsc4[:, g:g + 1], None, ALU.mult))(g),
                    reads=(("ones4", 0), ("gsc4", g)), writes=(("gsc16", g),), label=f"ex{g}")
            p.op("dve", lambda e: e.tensor_scalar(gsc16[:], gsc16[:], grow[:, 1:2], None, ALU.is_ge),
                 reads=tuple(("gsc16", g) for g in range(4)) + (("grow", 1),),
                 writes=tuple(("gsc16", g) for g in range(4)), label="smask")
            p.op("dve", lambda e: e.tensor_tensor(tmp16[:], gsc16[:], sfc[:], ALU.mult),
                 reads=tuple(("gsc16", g) for g in range(4)) + (("sfc", 0),), writes=(("tmp16", 0),), label="tmp16")
            p.op("dve", lambda e: e.max(mx8[:], tmp16[:]),
                 reads=(("tmp16", 0),), writes=(("mx8", 0),), label="mx8")
            p.op("dve", lambda e: e.max_index(ix8[:], mx8[:], tmp16[:]),
                 reads=(("tmp16", 0), ("mx8", 0)), writes=(("ix8", 0),), label="mwi")
            p.op("dve", lambda e: e.tensor_copy(ixf[:], ix8[:]),
                 reads=(("ix8", 0),), writes=(("ixf", 0),), label="ixf")
            p.op("pool", lambda e: e.memset(cmb[:], 0.0), writes=(("cmb", 0),))
            p.op("pool", lambda e: e.memset(grow[:, 2:3], 1e-20), writes=(("grow", 2),))
            for j in range(4):
                p.op("dve", (lambda j: lambda e: e.tensor_scalar(
                    oh[:], iot[:], ixf[:, j:j + 1], None, ALU.is_equal))(j),
                    reads=(("iot", 0), ("ixf", 0)), writes=(("oh", 0),), label=f"oh{j}")
                p.op("dve", lambda e: e.tensor_tensor(wsel[:], oh[:], sig[:], ALU.mult),
                     reads=(("oh", 0), ("sig", 0)), writes=(("wsel", 0),), label=f"ws{j}")
                p.op("dve", lambda e: e.reduce_sum(wk[:], wsel[:], axis=mybir.AxisListType.X),
                     reads=(("wsel", 0),), writes=(("wk", 0),), label=f"wk{j}")
                p.op("dve", lambda e: e.scalar_tensor_tensor(cmb[:], oh[:], wk[:, 0:1], cmb[:], ALU.mult, ALU.add),
                     reads=(("oh", 0), ("wk", 0), ("cmb", 0)), writes=(("cmb", 0),), label=f"cm{j}")
                p.op("dve", lambda e: e.tensor_tensor(grow[:, 2:3], grow[:, 2:3], wk[:], ALU.add),
                     reads=(("wk", 0), ("grow", 2)), writes=(("grow", 2),), label=f"ds{j}")
            p.op("dve", lambda e: e.reciprocal(grow[:, 3:4], grow[:, 2:3]),
                 reads=(("grow", 2),), writes=(("grow", 3),), label="dinv")
            p.op("dve", lambda e: e.tensor_scalar(cmb[:], cmb[:], grow[:, 3:4], None, ALU.mult),
                 reads=(("cmb", 0), ("grow", 3)), writes=(("cmb", 0),), label="cnorm")
            tb_, tbt = banks[5], bts[5]
            p.op("pe", (lambda t0: lambda e: e.transpose(
                tb_[0:E, 0:P], cmb[:], idf[:]))(t0),
                reads=(("cmb", 0), ("idf", 0)), writes=(tbt,), label="ctr")
            p.op("dve", (lambda t0: lambda e: e.tensor_copy(
                comb_fm[0:E, t0:t0 + P], tb_[0:E, 0:P]))(t0),
                reads=(tbt,), writes=(("comb_fm", ti),), label="ctc")
        for e_ in range(E):
            bg, tg, _ = PWg.load(0, m_width=I, dram=gpT[e_])
            bu, tu, _ = PWu.load(0, m_width=I, dram=upT[e_])
            cbank, cbt = banks[4], bts[4]
            p.op("pe", (lambda e_: lambda e: e.matmul(
                cbank[:, 0:TB], selsb[0:E, e_ * P:(e_ + 1) * P], comb_fm[0:E, 0:TB],
                start=True, stop=True))(e_),
                reads=tuple(("comb_fm", ti) for ti in range(TB // P)) + (("selsb", 0),),
                writes=(cbt,), label=f"cbb{e_}")
            for m in range(4):
                bank, bt = banks[m], bts[m]
                for kc in range(NCH):
                    p.op("pe", (lambda bg, kc, m, bank: lambda e: e.matmul(
                        bank[:, 0:TB], PWg.tile2(bg, kc, m * P, P, I), zb[kc][:, 0:TB],
                        start=(kc == 0), stop=(kc == NCH - 1)))(bg, kc, m, bank),
                        reads=(tg, ("z", kc)), writes=(bt,), label=f"gmm{e_}_{m}_{kc}")
                p.op("act", (lambda m, bank: lambda e: e.activation(
                    gs[m][:, 0:TB], bank[:, 0:TB], AF.Silu))(m, bank),
                    reads=(bt,), writes=(("gs", m),), label=f"silu{e_}_{m}")
                for kc in range(NCH):
                    p.op("pe", (lambda bu, kc, m, bank: lambda e: e.matmul(
                        bank[:, 0:TB], PWu.tile2(bu, kc, m * P, P, I), zb[kc][:, 0:TB],
                        start=(kc == 0), stop=(kc == NCH - 1)))(bu, kc, m, bank),
                        reads=(tu, ("z", kc)), writes=(bt,), label=f"umm{e_}_{m}_{kc}")
                p.op("dve", (lambda m, bank: lambda e: e.tensor_copy(
                    us[m][:, 0:TB], bank[:, 0:TB]))(m, bank),
                    reads=(bt,), writes=(("us", m),), label=f"ucp{e_}_{m}")
                p.op("dve", (lambda m: lambda e: e.tensor_tensor(
                    mtmp[:, 0:TB], gs[m][:, 0:TB], us[m][:, 0:TB], ALU.mult))(m),
                    reads=(("gs", m), ("us", m)), writes=(("mtmp", 0),), label=f"gu{e_}_{m}")
                p.op("dve", (lambda e_, m: lambda e: e.tensor_tensor(
                    actb[e_][m][:, 0:TB], mtmp[:, 0:TB], cbank[:, 0:TB], ALU.mult))(e_, m),
                    reads=(("mtmp", 0), cbt), writes=(("act", e_, m),), label=f"ac{e_}_{m}")
        for e_ in range(E):
            bd, td, _ = PWd.load(0, m_width=1024, dram=dpT[e_])
            for m in range(NCH):
                for kc in range(4):
                    p.op("pe", (lambda bd, kc, m, e_: lambda e: e.matmul(
                        banks[m][:, 0:TB], PWd.tile2(bd, kc, m * P, P, 1024), actb[e_][kc][:, 0:TB],
                        start=(e_ == 0 and kc == 0), stop=(e_ == E - 1 and kc == 3)))(bd, kc, m, e_),
                        reads=(td, ("act", e_, kc)), writes=(bts[m],), label=f"y{e_}_{m}_{kc}")
        for m in range(NCH):
            p.op("dve", (lambda m: lambda e: e.scalar_tensor_tensor(
                xc[m][:, 0:TB], banks[m][:, 0:TB], scale, xc[m][:, 0:TB], ALU.mult, ALU.add))(m),
                reads=(bts[m], ("x", m)), writes=(("x", m),), label=f"res{m}")
        out_keys = store_fm(p, xout, xc, "x", TB, dcol0=blk * TB)
    p.finalize(final_waits=out_keys)
    return nc


def stage_l1ffn(x_fm_batch, h, inp, Tloc=4096):
    t0 = h * Tloc
    E = 16
    gpT = np.ascontiguousarray(np.transpose(inp["L1_ffn_gate_proj"], (0, 2, 1)))
    upT = np.ascontiguousarray(np.transpose(inp["L1_ffn_up_proj"], (0, 2, 1)))
    dpT = np.ascontiguousarray(np.transpose(inp["L1_ffn_down_proj"], (0, 2, 1)))
    return {
        "x_own": np.ascontiguousarray(x_fm_batch[:, t0:t0 + Tloc]),
        "gwT": wT(inp["L1_ffn_gate_weight"]),
        "ebias": np.ascontiguousarray(np.tile(inp["L1_ffn_e_bias"][None, :], (128, 1)).astype(np.float32)),
        "iota16": np.ascontiguousarray(np.tile(np.arange(E, dtype=np.float32)[None, :], (128, 1))),
        "identf": np.eye(128, dtype=np.float32),
        "gpT": gpT, "upT": upT, "dpT": dpT,
        "selm": np.ascontiguousarray(
            np.stack([np.tile(np.eye(E, dtype=np.float32)[:, e:e + 1], (1, 128)) for e in range(E)],
                     axis=0).reshape(E, E * 128, order="F").reshape(E, E * 128) * 0 +
            np.concatenate([np.tile(np.eye(E, dtype=np.float32)[:, e:e + 1], (1, 128)) for e in range(E)], axis=1)),
    }



def rms_rope_fm(p, c: LNCtx, xf, xf_name, dst, dst_name, nch, T, gam, gam_name, gcolof,
                tabC, tabS, tabname, pswap_sb, identb_sb, bank_s, tg_s, bank_rb, tg_rb,
                bank_sw, tg_sw, scr, scrb, abs0, ropeC, ropeS, eps_tile, eps_tag,
                dcol0=0, dstcol0=0, hd=256):
    nparity = hd // P
    heads = nch // nparity
    for t0 in range(0, T, 512):
        tw = min(512, T - t0)
        for par in range(nparity):
            p.op("sp", (lambda par, t0, tw: lambda e: e.dma_start(
                tabC[par][:, 0:tw], ropeC[P * par:P * (par + 1), abs0 + dcol0 + t0:abs0 + dcol0 + t0 + tw]))(par, t0, tw),
                writes=((tabname + "C", par),), dma=f"T{tabname}C{par}", label="tabC")
            p.op("sp", (lambda par, t0, tw: lambda e: e.dma_start(
                tabS[par][:, 0:tw], ropeS[P * par:P * (par + 1), abs0 + dcol0 + t0:abs0 + dcol0 + t0 + tw]))(par, t0, tw),
                writes=((tabname + "S", par),), dma=f"T{tabname}S{par}", label="tabS")
        for h in range(heads):
            chs = [h * nparity + par for par in range(nparity)]
            for ci, ch in enumerate(chs):
                sq = ci % 2
                p.op("act", (lambda ch, sq, t0, tw: lambda e: e.activation(
                    c.xsq[sq][:, 0:tw], xf[ch][:, t0:t0 + tw], AF.Square))(ch, sq, t0, tw),
                    reads=((xf_name, ch),), writes=(("ln_xsq", sq),), label="rrsq")
                p.op("pe", (lambda ci, sq, tw, n=len(chs): lambda e: e.matmul(
                    bank_s[0:1, 0:tw], c.onesc[:], c.xsq[sq][:, 0:tw],
                    start=(ci == 0), stop=(ci == n - 1)))(ci, sq, tw),
                    reads=(("ln_xsq", sq), ("ln_onesc", 0)), writes=(tg_s,), label="rrS")
            R = c.rows
            p.op("dve", (lambda tw: lambda e: e.tensor_scalar(
                R[0:1, 0:tw], bank_s[0:1, 0:tw], 1.0 / hd, None, ALU.mult))(tw),
                reads=(tg_s,), writes=(("ln_rows", 0),), label="rr_ms")
            p.op("act", (lambda tw: lambda e: e.activation(
                R[0:1, 512:512 + tw], R[0:1, 0:tw], AF.Sqrt, bias=eps_tile[0:1, 0:1]))(tw),
                reads=(("ln_rows", 0), eps_tag), writes=(("ln_rows", 1),), label="rr_sd")
            p.op("dve", (lambda tw: lambda e: e.reciprocal(
                R[0:1, 512:512 + tw], R[0:1, 512:512 + tw]))(tw),
                reads=(("ln_rows", 1),), writes=(("ln_rows", 1),), label="rr_rstd")
            p.op("pe", (lambda tw: lambda e: e.matmul(
                bank_rb[:, 0:tw], c.onesr[:], R[0:1, 512:512 + tw], start=True, stop=True))(tw),
                reads=(("ln_rows", 1), ("ln_onesr", 0)), writes=(tg_rb,), label="rr_b")
            for ci, ch in enumerate(chs):
                par = ch % nparity
                p.op("dve", (lambda ch, t0, tw, gc: lambda e: e.scalar_tensor_tensor(
                    scr[:, 0:tw], xf[ch][:, t0:t0 + tw], gam[:, gc:gc + 1], bank_rb[:, 0:tw],
                    ALU.mult, ALU.mult))(ch, t0, tw, gcolof + par),
                    reads=((xf_name, ch), (gam_name, 0), tg_rb), writes=(("rr_scr", 0),), label="rr_xn")
                p.op("pool", (lambda tw: lambda e: e.tensor_copy(
                    scrb[:, 0:tw], scr[:, 0:tw]))(tw),
                    reads=(("rr_scr", 0),), writes=(("rr_scrb", 0),), label="rr_cb")
                p.op("pe", (lambda tw: lambda e: e.matmul(
                    bank_sw[:, 0:tw], pswap_sb[:], scrb[:, 0:tw], start=True, stop=True))(tw),
                    reads=(("rr_scrb", 0), ("pswap", 0)), writes=(tg_sw,), label="rr_sw")
                p.op("dve", (lambda par, tw: lambda e: e.tensor_tensor(
                    scr[:, 0:tw], scr[:, 0:tw], tabC[par][:, 0:tw], ALU.mult))(par, tw),
                    reads=(("rr_scr", 0), (tabname + "C", par)), writes=(("rr_scr", 0),), label="rr_c")
                p.op("dve", (lambda par, tw: lambda e: e.tensor_tensor(
                    c.tmp[:, 0:tw], tabS[par][:, 0:tw], bank_sw[:, 0:tw], ALU.mult))(par, tw),
                    reads=((tabname + "S", par), tg_sw), writes=(("ln_tmp", 0),), label="rr_s")
                p.op("dve", (lambda ch, t0, tw: lambda e: e.tensor_tensor(
                    dst[ch][:, dstcol0 + t0:dstcol0 + t0 + tw], scr[:, 0:tw], c.tmp[:, 0:tw], ALU.add))(ch, t0, tw),
                    reads=(("rr_scr", 0), ("ln_tmp", 0)), writes=((dst_name, ch),), label="rr_o")


def build_l0attn(Tloc=4096, HALO=512):
    Text = HALO + Tloc
    nc = bass.Bass("TRN2", target_bir_lowering=False)
    x_ext = nc.dram_tensor("x_ext", [1024, Text], F32, kind="ExternalInput").ap()
    wqT = nc.dram_tensor("wqT", [1024, 1024], F32, kind="ExternalInput").ap()
    wkT = nc.dram_tensor("wkT", [1024, 256], F32, kind="ExternalInput").ap()
    wvT = nc.dram_tensor("wvT", [1024, 256], F32, kind="ExternalInput").ap()
    woT = nc.dram_tensor("woT", [1024, 1024], F32, kind="ExternalInput").ap()
    qg = nc.dram_tensor("qg", [P, 2], F32, kind="ExternalInput").ap()
    kg = nc.dram_tensor("kg", [P, 2], F32, kind="ExternalInput").ap()
    ropeC = nc.dram_tensor("ropeC", [256, Text], F32, kind="ExternalInput").ap()
    ropeS = nc.dram_tensor("ropeS", [256, Text], F32, kind="ExternalInput").ap()
    pswap = nc.dram_tensor("pswap", [P, P], BF16, kind="ExternalInput").ap()
    identb = nc.dram_tensor("identb", [P, P], BF16, kind="ExternalInput").ap()
    masks = nc.dram_tensor("masks", [5, P, 640], F32, kind="ExternalInput").ap()
    xout = nc.dram_tensor("xout", [1024, Tloc], F32, kind="ExternalOutput").ap()

    p = Prog(nc)
    banks = [p.psum(f"pb{i}", [P, 512], F32) for i in range(7)]
    bts = [("pb", i) for i in range(7)]
    btr = p.psum("ptr", [P, 1024], BF16)
    btrt = ("ptr", 0)
    c = LNCtx(p, banks=[(banks[i], bts[i]) for i in range(4)])

    Qr = [p.sbuf(f"Qr{k}", [P, Tloc], BF16) for k in range(NCH)]
    Kr = [p.sbuf(f"Kr{k}", [P, Text], BF16) for k in range(2)]
    Vtm = p.sbuf("Vtm", [P, (Text // P) * 256], BF16)
    TBL = 512
    xc = [p.sbuf(f"x{k}", [P, TBL], F32) for k in range(NCH)]
    zb = [p.sbuf(f"z{k}", [P, TBL], BF16) for k in range(NCH)]
    pf = [p.sbuf(f"pf{k}", [P, TBL], F32) for k in range(NCH)]
    vf = [p.sbuf(f"vf{k}", [P, TBL], BF16) for k in range(2)]
    tabC = [p.sbuf(f"tabC{i}", [P, 512], F32) for i in range(2)]
    tabS = [p.sbuf(f"tabS{i}", [P, 512], F32) for i in range(2)]
    scr = p.sbuf("scr", [P, 512], F32)
    scrb = p.sbuf("scrb", [P, 512], BF16)
    gqs = p.sbuf("gqs", [P, 2], F32)
    gks = p.sbuf("gks", [P, 2], F32)
    psw = p.sbuf("psw", [P, P], BF16)
    idb = p.sbuf("idb", [P, P], BF16)
    msks = p.sbuf("msks", [P, 5 * 640], F32)
    P1 = p.sbuf("P1", [P, 512], BF16)
    P2 = p.sbuf("P2", [P, 128], BF16)
    PT = p.sbuf("PTs", [P, 640], BF16)
    rws = p.sbuf("rws", [P, 8], F32)
    Ofm = [p.sbuf(f"Ofm{k}", [P, 512], BF16) for k in range(NCH)]
    epsr = p.sbuf("epsr", [1, 1], F32)
    p.op("pool", lambda e: e.memset(epsr[:], 1e-6), writes=(("epsr", 0),))

    PWq = ProjWeights(p, wqT, 1024, 1024, "wq")
    PWk = ProjWeights(p, wkT, 1024, 256, "wk")
    PWv = ProjWeights(p, wvT, 1024, 256, "wv")
    PWo = ProjWeights(p, woT, 1024, 1024, "wo")

    for nm, dst, src_ in (("gq", gqs, qg), ("gk", gks, kg), ("pswap", psw, pswap),
                          ("idb", idb, identb)):
        p.op("sp", (lambda dst, src_: lambda e: e.dma_start(dst[:], src_[:]))(dst, src_),
             writes=((nm, 0),), dma="L" + nm)
    msrc = masks.rearrange("s q m -> q s m")
    p.op("sp", lambda e: e.dma_start(msks[:, :].rearrange("q (s m) -> q s m", m=640), msrc),
         writes=(("msks", 0),), dma="Lmsk")

    nblk = Text // TBL + (1 if Text % TBL else 0)
    for blk in range(nblk):
        b0 = blk * TBL
        bw = min(TBL, Text - b0)
        load_fm(p, x_ext, xc, "x", bw, dcol0=b0)
        ln_fm(p, c, xc, "x", bw, z_chunks=zb, z_name="z")
        def epi_kv(dst, dname):
            def epi(mi, t0, tw, bank, btag):
                p.op("dve", (lambda mi, t0, tw, bank: lambda e: e.tensor_copy(
                    dst[mi][:, t0:t0 + tw], bank[:, 0:tw]))(mi, t0, tw, bank),
                    reads=(btag,), writes=((dname, mi),), label=f"ekv{mi}")
            return epi
        proj_sweep(p, PWk, zb, "z", bw, banks[0:4], bts[0:4], epi_kv(pf, "pf"))
        rms_rope_fm(p, c, pf, "pf", Kr, "kr", 2, bw, gks, "gk", 0, tabC, tabS, "tb",
                    psw, idb, banks[4], bts[4], banks[5], bts[5], banks[6], bts[6],
                    scr, scrb, 0, ropeC, ropeS, epsr, ("epsr", 0), dcol0=b0, dstcol0=b0)
        def epi_v(mi, t0, tw, bank, btag):
            p.op("dve", (lambda mi, t0, tw, bank: lambda e: e.tensor_copy(
                vf[mi][:, t0:t0 + tw], bank[:, 0:tw]))(mi, t0, tw, bank),
                reads=(btag,), writes=(("vf", mi),), label=f"ev{mi}")
        proj_sweep(p, PWv, zb, "z", bw, banks[0:4], bts[0:4], epi_v)
        for j in range(bw // P):
            for cvd in range(2):
                slot = (2 * j + cvd) % 8
                p.op("pe", (lambda j, cvd, slot: lambda e: e.transpose(
                    btr[:, slot * P:(slot + 1) * P], vf[cvd][:, j * P:(j + 1) * P], idb[:]))(j, cvd, slot),
                    reads=(("vf", cvd), ("idb", 0)), writes=(btrt,), label="vtr")
                jj = (b0 // P) + j
                p.op("dve", (lambda jj, cvd, slot: lambda e: e.tensor_copy(
                    Vtm[:, jj * 256 + cvd * P:jj * 256 + (cvd + 1) * P],
                    btr[:, slot * P:(slot + 1) * P]))(jj, cvd, slot),
                    reads=(btrt,), writes=(("vtm", jj),), label="vtc")

    for blk in range(Tloc // TBL):
        b0 = HALO + blk * TBL
        load_fm(p, x_ext, xc, "x", TBL, dcol0=b0)
        ln_fm(p, c, xc, "x", TBL, z_chunks=zb, z_name="z")
        def epi_q(mi, t0, tw, bank, btag):
            p.op("dve", (lambda mi, t0, tw, bank: lambda e: e.tensor_copy(
                pf[mi][:, t0:t0 + tw], bank[:, 0:tw]))(mi, t0, tw, bank),
                reads=(btag,), writes=(("pf", mi),), label=f"eq{mi}")
        proj_sweep(p, PWq, zb, "z", TBL, banks[0:4], bts[0:4], epi_q)
        rms_rope_fm(p, c, pf, "pf", Qr, "qr", NCH, TBL, gqs, "gq", 0, tabC, tabS, "tb",
                    psw, idb, banks[4], bts[4], banks[5], bts[5], banks[6], bts[6],
                    scr, scrb, HALO, ropeC, ropeS, epsr, ("epsr", 0),
                    dcol0=blk * TBL, dstcol0=blk * TBL)

    QB = 512
    for qb in range(Tloc // QB):
        for ti in range(QB // P):
            qt = qb * (QB // P) + ti
            win0 = P * qt
            slot = min(qt, 4)
            for h in range(4):
                sA, sAt = banks[0 + (h % 2) * 2], bts[0 + (h % 2) * 2]
                sB, sBt = banks[1 + (h % 2) * 2], bts[1 + (h % 2) * 2]
                for ci in range(2):
                    ch = 2 * h + ci
                    p.op("pe", (lambda ch, ci, qt, win0, sA: lambda e: e.matmul(
                        sA[:, 0:512], Qr[ch][:, qt * P:(qt + 1) * P], Kr[ci][:, win0:win0 + 512],
                        start=(ci == 0), stop=(ci == 1)))(ch, ci, qt, win0, sA),
                        reads=(("qr", ch), ("kr", ci)), writes=(sAt,), label="qk1")
                    p.op("pe", (lambda ch, ci, qt, win0, sB: lambda e: e.matmul(
                        sB[:, 0:128], Qr[ch][:, qt * P:(qt + 1) * P], Kr[ci][:, win0 + 512:win0 + 640],
                        start=(ci == 0), stop=(ci == 1)))(ch, ci, qt, win0, sB),
                        reads=(("qr", ch), ("kr", ci)), writes=(sBt,), label="qk2")
                p.op("dve", (lambda sA, slot: lambda e: e.scalar_tensor_tensor(
                    sA[:, 0:512], sA[:, 0:512], 1.0 / 16.0, msks[:, slot * 640:slot * 640 + 512],
                    ALU.mult, ALU.add))(sA, slot),
                    reads=(sAt, ("msks", 0)), writes=(sAt,), label="msk1")
                p.op("dve", (lambda sB, slot: lambda e: e.scalar_tensor_tensor(
                    sB[:, 0:128], sB[:, 0:128], 1.0 / 16.0, msks[:, slot * 640 + 512:slot * 640 + 640],
                    ALU.mult, ALU.add))(sB, slot),
                    reads=(sBt, ("msks", 0)), writes=(sBt,), label="msk2")
                p.op("dve", (lambda sA: lambda e: e.reduce_max(
                    rws[:, 0:1], sA[:, 0:512], axis=mybir.AxisListType.X))(sA),
                    reads=(sAt,), writes=(("rws", 0),), label="fm1")
                p.op("dve", (lambda sB: lambda e: e.reduce_max(
                    rws[:, 1:2], sB[:, 0:128], axis=mybir.AxisListType.X))(sB),
                    reads=(sBt,), writes=(("rws", 1),), label="fm2")
                p.op("dve", lambda e: e.tensor_tensor(rws[:, 2:3], rws[:, 0:1], rws[:, 1:2], ALU.max),
                     reads=(("rws", 0), ("rws", 1)), writes=(("rws", 2),), label="fmm")
                p.op("dve", lambda e: e.tensor_scalar(rws[:, 3:4], rws[:, 2:3], -1.0, None, ALU.mult),
                     reads=(("rws", 2),), writes=(("rws", 3),), label="fnm")
                p.op("act", (lambda sA: lambda e: e.activation(
                    P1[:, 0:512], sA[:, 0:512], AF.Exp, bias=rws[:, 3:4], accum_out=rws[:, 4:5]))(sA),
                    reads=(sAt, ("rws", 3)), writes=(("P1", 0), ("rws", 4)), label="fe1")
                p.op("act", (lambda sB: lambda e: e.activation(
                    P2[:, 0:128], sB[:, 0:128], AF.Exp, bias=rws[:, 3:4], accum_out=rws[:, 5:6]))(sB),
                    reads=(sBt, ("rws", 3)), writes=(("P2", 0), ("rws", 5)), label="fe2")
                p.op("dve", lambda e: e.tensor_tensor(rws[:, 6:7], rws[:, 4:5], rws[:, 5:6], ALU.add),
                     reads=(("rws", 4), ("rws", 5)), writes=(("rws", 6),), label="frs")
                p.op("dve", lambda e: e.reciprocal(rws[:, 7:8], rws[:, 6:7]),
                     reads=(("rws", 6),), writes=(("rws", 7),), label="fri")
                p.op("dve", lambda e: e.tensor_scalar(P1[:, 0:512], P1[:, 0:512], rws[:, 7:8], None, ALU.mult),
                     reads=(("P1", 0), ("rws", 7)), writes=(("P1", 0),), label="fp1")
                p.op("dve", lambda e: e.tensor_scalar(P2[:, 0:128], P2[:, 0:128], rws[:, 7:8], None, ALU.mult),
                     reads=(("P2", 0), ("rws", 7)), writes=(("P2", 0),), label="fp2")
                for j in range(5):
                    srcP = P1 if j < 4 else P2
                    scol = j * P if j < 4 else 0
                    ptag = ("P1", 0) if j < 4 else ("P2", 0)
                    p.op("pe", (lambda j, srcP, scol: lambda e: e.transpose(
                        btr[:, j * P:(j + 1) * P], srcP[:, scol:scol + P], idb[:]))(j, srcP, scol),
                        reads=(ptag, ("idb", 0)), writes=(btrt,), label="ptr")
                    p.op("dve", (lambda j: lambda e: e.tensor_copy(
                        PT[:, j * P:(j + 1) * P], btr[:, j * P:(j + 1) * P]))(j),
                        reads=(btrt,), writes=(("PT", j),), label="ptc")
                for cvd in range(2):
                    ob, obt = banks[4 + cvd], bts[4 + cvd]
                    for j in range(5):
                        jj = qt + j
                        p.op("pe", (lambda jj, cvd, j, ob: lambda e: e.matmul(
                            ob[:, 0:128], Vtm[:, jj * 256 + cvd * P:jj * 256 + (cvd + 1) * P],
                            PT[:, j * P:(j + 1) * P], start=(j == 0), stop=(j == 4)))(jj, cvd, j, ob),
                            reads=(("vtm", jj), ("PT", j)), writes=(obt,), label="pv")
                    p.op("act", (lambda h, cvd, ti, ob: lambda e: e.activation(
                        Ofm[2 * h + cvd][:, ti * P:(ti + 1) * P], ob[:, 0:128], AF.Copy))(h, cvd, ti, ob),
                        reads=(obt,), writes=(("ofm", 2 * h + cvd),), label="oev")
        load_fm(p, x_ext, xc, "x", QB, dcol0=HALO + qb * QB)
        def epi_o(mi, t0, tw, bank, btag):
            p.op("dve", (lambda mi, t0, tw, bank: lambda e: e.tensor_tensor(
                xc[mi][:, t0:t0 + tw], xc[mi][:, t0:t0 + tw], bank[:, 0:tw], ALU.add))(mi, t0, tw, bank),
                reads=(btag, ("x", mi)), writes=(("x", mi),), label=f"acc{mi}")
        proj_sweep(p, PWo, Ofm, "ofm", QB, banks[0:4], bts[0:4], epi_o)
        out_keys = store_fm(p, xout, xc, "x", QB, dcol0=qb * QB)
    p.finalize(final_waits=out_keys)
    return nc


def stage_l0attn(x_fm_batch, h, inp, Tloc=4096, HALO=512):
    t0 = h * Tloc
    Text = HALO + Tloc
    x_ext = np.zeros((1024, Text), np.float32)
    x_ext[:, HALO:] = x_fm_batch[:, t0:t0 + Tloc]
    if t0 > 0:
        x_ext[:, 0:HALO] = x_fm_batch[:, t0 - HALO:t0]
    hd = 256
    inv = 1.0 / (1000000.0 ** (np.arange(0, hd, 2, dtype=np.float32) / hd))
    pos = (np.arange(Text, dtype=np.float32) + (t0 - HALO))
    pos = np.maximum(pos, 0.0)
    fr = np.outer(inv, pos)
    cos = np.cos(fr).astype(np.float32)
    sin = np.sin(fr).astype(np.float32)
    ropeC = np.zeros((256, Text), np.float32)
    ropeS = np.zeros((256, Text), np.float32)
    ropeC[0::2] = cos; ropeC[1::2] = cos
    ropeS[0::2] = -sin; ropeS[1::2] = sin
    sw = np.zeros((128, 128), BF16NP)
    for j in range(0, 128, 2):
        sw[j + 1, j] = 1.0
        sw[j, j + 1] = 1.0
    base = np.zeros((128, 640), np.float32)
    r = np.arange(128)[:, None]; cc = np.arange(640)[None, :]
    base[~((cc >= r) & (cc <= r + 512))] = -1e30
    masks = np.stack([base] * 5, axis=0)
    if h == 0:
        for qt in range(4):
            m = base.copy()
            keyabs = P * qt - HALO + cc
            m[(keyabs < 0) & np.ones_like(r, bool)] = -1e30
            masks[qt] = m
    return {
        "x_ext": x_ext,
        "wqT": wT(inp["L0_attn_Wq"]), "wkT": wT(inp["L0_attn_Wk"]),
        "wvT": wT(inp["L0_attn_Wv"]), "woT": wT(inp["L0_attn_Wo"]),
        "qg": np.ascontiguousarray(inp["L0_attn_q_gamma"].reshape(2, 128).T),
        "kg": np.ascontiguousarray(inp["L0_attn_k_gamma"].reshape(2, 128).T),
        "ropeC": ropeC, "ropeS": ropeS,
        "pswap": sw, "identb": np.eye(128, dtype=BF16NP),
        "masks": masks,
    }



def build_fullattn(variant, Tloc=4096, Tfull=8192):
    nc = bass.Bass("TRN2", target_bir_lowering=False)
    x_full = nc.dram_tensor("x_full", [1024, Tfull], F32, kind="ExternalInput").ap()
    x_own = nc.dram_tensor("x_own", [1024, Tloc], F32, kind="ExternalInput").ap()
    if variant == "l1":
        wqT = nc.dram_tensor("wqT", [1024, 1024], F32, kind="ExternalInput").ap()
        wkT = nc.dram_tensor("wkT", [1024, 1024], F32, kind="ExternalInput").ap()
        bqv = nc.dram_tensor("bqv", [P, 16], F32, kind="ExternalInput").ap()
    else:
        wqkvT = nc.dram_tensor("wqkvT", [1024, 3072], F32, kind="ExternalInput").ap()
        bqv = nc.dram_tensor("bqv", [P, 24], F32, kind="ExternalInput").ap()
    identb = nc.dram_tensor("identb", [P, P], BF16, kind="ExternalInput").ap()
    xout = nc.dram_tensor("xout", [1024, Tloc], F32, kind="ExternalOutput").ap()
    Ks = nc.dram_tensor("Ks", [1024, Tfull], BF16, kind="Internal").ap()
    Vs = nc.dram_tensor("Vs", [Tfull, 1024], BF16, kind="Internal").ap()

    p = Prog(nc)
    banks = [p.psum(f"pb{i}", [P, 512], F32) for i in range(7)]
    bts = [("pb", i) for i in range(7)]
    btr = p.psum("ptr", [P, 1024], BF16)
    btrt = ("ptr", 0)
    c = LNCtx(p, banks=[(banks[i], bts[i]) for i in range(4)])

    TBL = 512
    KB = 1024
    QBT = 8
    xc = [p.sbuf(f"x{k}", [P, P * QBT], F32) for k in range(NCH)]
    zb = [p.sbuf(f"z{k}", [P, TBL], BF16) for k in range(NCH)]
    kb_ = [p.sbuf(f"kb{k}", [P, TBL], BF16) for k in range(NCH)]
    Qb = [p.sbuf(f"Qb{k}", [P, P * QBT], BF16) for k in range(NCH)]
    zloc = [p.sbuf(f"zl{k}", [P, P * QBT], BF16) for k in range(NCH)]
    Kblk = [[p.sbuf(f"K{s}_{k}", [P, KB], BF16) for k in range(NCH)] for s in range(2)]
    Vblk = [p.sbuf(f"V{s}", [P, (KB // P) * 1024], BF16) for s in range(2)]
    Ot = [p.sbuf(f"Ot{t}", [P, 1024], F32) for t in range(QBT)]
    PT = p.sbuf("PTs", [P, KB], BF16)
    P1 = p.sbuf("P1", [P, KB], BF16)
    rws = p.sbuf("rws", [P, 16], F32)
    mrow = [p.sbuf(f"mr{t}", [P, 16], F32) for t in range(QBT)]
    bqs = p.sbuf("bqs", [P, 24 if variant == "l2" else 16], F32)
    idb = p.sbuf("idb", [P, P], BF16)
    macc = p.sbuf("macc", [P, NCH], F32)
    rtmp = p.sbuf("rtmp", [P, 1], F32)

    if variant == "l1":
        PWq = ProjWeights(p, wqT, 1024, 1024, "wq")
        PWk = ProjWeights(p, wkT, 1024, 1024, "wk")
        PWv = None
    else:
        PWq = ProjWeights(p, wqkvT, 1024, 3072, "wq")
        PWk = PWq
        PWv = PWq
    p.op("sp", lambda e: e.dma_start(bqs[:], bqv[:]), writes=(("bqs", 0),), dma="Lbq")
    p.op("sp", lambda e: e.dma_start(idb[:], identb[:]), writes=(("idb", 0),), dma="Lid")
    p.op("pool", lambda e: e.memset(macc[:], 0.0), writes=(("macc", 0),))

    for blk in range(Tfull // TBL):
        b0 = blk * TBL
        load_fm(p, x_full, xc, "x", TBL, dcol0=b0)
        if variant == "l1":
            ln_fm(p, c, xc, "x", TBL, z_chunks=zb, z_name="z")
            for k in range(NCH):
                p.op("dve", (lambda k: lambda e: e.reduce_sum(
                    rtmp[:], zb[k][:, 0:TBL], axis=mybir.AxisListType.X))(k),
                    reads=(("z", k),), writes=(("rtmp", 0),), label=f"cs{k}")
                p.op("dve", (lambda k: lambda e: e.tensor_tensor(
                    macc[:, k:k + 1], macc[:, k:k + 1], rtmp[:], ALU.add))(k),
                    reads=(("rtmp", 0), ("macc", 0)), writes=(("macc", 0),), label=f"csa{k}")
        else:
            ln_fm(p, c, xc, "x", TBL, z_chunks=zb, z_name="z")
        def epi_k(mi, t0, tw, bank, btag):
            mo = mi - 8 if variant == "l2" else mi
            p.op("dve", (lambda mo, t0, tw, bank: lambda e: e.tensor_scalar(
                kb_[mo][:, t0:t0 + tw], bank[:, 0:tw], bqs[:, 8 + mo:8 + mo + 1], None, ALU.add))(mo, t0, tw, bank),
                reads=(btag, ("bqs", 0)), writes=(("kb", mo),), label=f"ek{mo}")
        kcols = list(range(8, 16)) if variant == "l2" else None
        proj_sweep(p, PWk, zb, "z", TBL, banks[0:4], bts[0:4], epi_k, m_chunks=kcols)
        for k in range(NCH):
            p.op("sp", (lambda k, b0: lambda e: e.dma_start(
                Ks[P * k:P * (k + 1), b0:b0 + TBL], kb_[k][:, 0:TBL]))(k, b0),
                reads=(("kb", k),), dma=f"SK{k}", label="stK")
        if variant == "l2":
            def epi_v(mi, t0, tw, bank, btag):
                mo = mi - 16
                p.op("dve", (lambda mo, t0, tw, bank: lambda e: e.tensor_scalar(
                    kb_[mo][:, t0:t0 + tw], bank[:, 0:tw], bqs[:, 16 + mo:16 + mo + 1], None, ALU.add))(mo, t0, tw, bank),
                    reads=(btag, ("bqs", 0)), writes=(("kb", mo),), label=f"ev{mo}")
            proj_sweep(p, PWv, zb, "z", TBL, banks[0:4], bts[0:4], epi_v,
                       m_chunks=list(range(16, 24)))
            vsrc, vtag = kb_, "kb"
        else:
            vsrc, vtag = zb, "z"
        for j in range(TBL // P):
            for ch in range(NCH):
                slot = ch % 8
                p.op("pe", (lambda j, ch, slot, vsrc: lambda e: e.transpose(
                    btr[:, slot * P:(slot + 1) * P], vsrc[ch][:, j * P:(j + 1) * P], idb[:]))(j, ch, slot, vsrc),
                    reads=((vtag, ch), ("idb", 0)), writes=(btrt,), label="vtr")
                p.op("dve", (lambda ch, slot: lambda e: e.tensor_copy(
                    PT[:, ch * P:(ch + 1) * P], btr[:, slot * P:(slot + 1) * P]))(ch, slot),
                    reads=(btrt,), writes=(("PTx", ch),), label="vtc")
            p.op("sp", (lambda j, b0: lambda e: e.dma_start(
                Vs[b0 + j * P:b0 + (j + 1) * P, :], PT[:, 0:1024]))(j, b0),
                reads=tuple(("PTx", ch) for ch in range(NCH)), dma="SV", label="stV")

    if variant == "l1":
        p.op("dve", lambda e: e.tensor_scalar(macc[:], macc[:], -2.0 / Tfull, None, ALU.mult),
             reads=(("macc", 0),), writes=(("macc", 0),), label="negcs")

    QB = P * QBT
    for qb in range(Tloc // QB):
        for sub in range(QB // TBL):
            b0 = qb * QB + sub * TBL
            load_fm(p, x_own, xc, "x", TBL, dcol0=b0)
            ln_fm(p, c, xc, "x", TBL, z_chunks=zb, z_name="z")
            def epi_q(mi, t0, tw, bank, btag):
                p.op("dve", (lambda mi, t0, tw, bank, sub: lambda e: e.tensor_scalar(
                    Qb[mi][:, sub * TBL + t0:sub * TBL + t0 + tw], bank[:, 0:tw],
                    bqs[:, mi:mi + 1], None, ALU.add))(mi, t0, tw, bank, sub),
                    reads=(btag, ("bqs", 0)), writes=(("Qb", mi),), label=f"eq{mi}")
            proj_sweep(p, PWq, zb, "z", TBL, banks[0:4], bts[0:4], epi_q,
                       m_chunks=(list(range(0, 8)) if variant == "l2" else None))
            if variant == "l1":
                for k in range(NCH):
                    p.op("pool", (lambda k, sub: lambda e: e.tensor_copy(
                        zloc[k][:, sub * TBL:(sub + 1) * TBL], zb[k][:, 0:TBL]))(k, sub),
                        reads=(("z", k),), writes=(("zloc", k),), label=f"zl{k}")
        for kblk in range(Tfull // KB):
            s = kblk % 2
            kb0 = kblk * KB
            for k in range(NCH):
                p.op("sp", (lambda s, k, kb0: lambda e: e.dma_start(
                    Kblk[s][k][:, 0:KB], Ks[P * k:P * (k + 1), kb0:kb0 + KB]))(s, k, kb0),
                    writes=(("Kb", s, k),), dma=f"LK{s}_{k}", label="ldK")
            for j in range(KB // P):
                p.op("sp", (lambda s, j, kb0: lambda e: e.dma_start(
                    Vblk[s][:, j * 1024:(j + 1) * 1024], Vs[kb0 + j * P:kb0 + (j + 1) * P, :]))(s, j, kb0),
                    writes=(("Vb", s, j),), dma=f"LV{s}", label="ldV")
            for t in range(QBT):
                sA, sAt = banks[0 + (t % 2) * 2], bts[0 + (t % 2) * 2]
                sB, sBt = banks[1 + (t % 2) * 2], bts[1 + (t % 2) * 2]
                for kc in range(NCH):
                    p.op("pe", (lambda t, kc, s, sA: lambda e: e.matmul(
                        sA[:, 0:512], Qb[kc][:, t * P:(t + 1) * P], Kblk[s][kc][:, 0:512],
                        start=(kc == 0), stop=(kc == NCH - 1)))(t, kc, s, sA),
                        reads=(("Qb", kc), ("Kb", s, kc)), writes=(sAt,), label="qkA")
                    p.op("pe", (lambda t, kc, s, sB: lambda e: e.matmul(
                        sB[:, 0:512], Qb[kc][:, t * P:(t + 1) * P], Kblk[s][kc][:, 512:1024],
                        start=(kc == 0), stop=(kc == NCH - 1)))(t, kc, s, sB),
                        reads=(("Qb", kc), ("Kb", s, kc)), writes=(sBt,), label="qkB")
                first = (kblk == 0)
                p.op("dve", (lambda sA: lambda e: e.reduce_max(
                    rws[:, 1:2], sA[:, 0:512], axis=mybir.AxisListType.X))(sA),
                    reads=(sAt,), writes=(("rws", 1),), label="fmx1")
                p.op("dve", (lambda sB: lambda e: e.reduce_max(
                    rws[:, 2:3], sB[:, 0:512], axis=mybir.AxisListType.X))(sB),
                    reads=(sBt,), writes=(("rws", 2),), label="fmx2")
                p.op("dve", lambda e: e.tensor_tensor(rws[:, 1:2], rws[:, 1:2], rws[:, 2:3], ALU.max),
                     reads=(("rws", 1), ("rws", 2)), writes=(("rws", 1),), label="fmx")
                if first:
                    p.op("dve", (lambda t: lambda e: e.tensor_scalar(
                        mrow[t][:, 0:1], rws[:, 1:2], 1.0 / 32.0, None, ALU.mult))(t),
                        reads=(("rws", 1),), writes=(("mr", t, 0),), label="fm0")
                else:
                    p.op("dve", (lambda t: lambda e: e.tensor_scalar(
                        rws[:, 3:4], rws[:, 1:2], 1.0 / 32.0, None, ALU.mult))(t),
                        reads=(("rws", 1),), writes=(("rws", 3),), label="fmsc")
                    p.op("dve", (lambda t: lambda e: e.tensor_tensor(
                        rws[:, 3:4], rws[:, 3:4], mrow[t][:, 0:1], ALU.max))(t),
                        reads=(("rws", 3), ("mr", t, 0)), writes=(("rws", 3),), label="fmN")
                    p.op("dve", (lambda t: lambda e: e.tensor_scalar(
                        rws[:, 4:5], rws[:, 3:4], -1.0, None, ALU.mult))(t),
                        reads=(("rws", 3),), writes=(("rws", 4),), label="fng")
                    p.op("act", (lambda t: lambda e: e.activation(
                        rws[:, 5:6], mrow[t][:, 0:1], AF.Exp, bias=rws[:, 4:5]))(t),
                        reads=(("mr", t, 0), ("rws", 4)), writes=(("rws", 5),), label="falf")
                    p.op("dve", (lambda t: lambda e: e.tensor_copy(
                        mrow[t][:, 0:1], rws[:, 3:4]))(t),
                        reads=(("rws", 3),), writes=(("mr", t, 0),), label="fmup")
                p.op("dve", (lambda t: lambda e: e.tensor_scalar(
                    rws[:, 4:5], mrow[t][:, 0:1], -1.0, None, ALU.mult))(t),
                    reads=(("mr", t, 0),), writes=(("rws", 4),), label="fnm")
                p.op("act", (lambda sA: lambda e: e.activation(
                    P1[:, 0:512], sA[:, 0:512], AF.Exp, bias=rws[:, 4:5], scale=1.0 / 32.0,
                    accum_out=rws[:, 6:7]))(sA),
                    reads=(sAt, ("rws", 4)), writes=(("P1", 0), ("rws", 6)), label="fe1")
                p.op("act", (lambda sB: lambda e: e.activation(
                    P1[:, 512:1024], sB[:, 0:512], AF.Exp, bias=rws[:, 4:5], scale=1.0 / 32.0,
                    accum_out=rws[:, 7:8]))(sB),
                    reads=(sBt, ("rws", 4)), writes=(("P1", 1), ("rws", 7)), label="fe2")
                p.op("dve", lambda e: e.tensor_tensor(rws[:, 6:7], rws[:, 6:7], rws[:, 7:8], ALU.add),
                     reads=(("rws", 6), ("rws", 7)), writes=(("rws", 6),), label="frs")
                if first:
                    p.op("dve", (lambda t: lambda e: e.tensor_copy(
                        mrow[t][:, 1:2], rws[:, 6:7]))(t),
                        reads=(("rws", 6),), writes=(("mr", t, 1),), label="fl0")
                else:
                    p.op("dve", (lambda t: lambda e: e.scalar_tensor_tensor(
                        mrow[t][:, 1:2], mrow[t][:, 1:2], rws[:, 5:6], rws[:, 6:7],
                        ALU.mult, ALU.add))(t),
                        reads=(("mr", t, 1), ("rws", 5), ("rws", 6)), writes=(("mr", t, 1),), label="flu")
                for j in range(KB // P):
                    p.op("pe", (lambda j: lambda e: e.transpose(
                        btr[:, (j % 8) * P:(j % 8 + 1) * P], P1[:, j * P:(j + 1) * P], idb[:]))(j),
                        reads=(("P1", j // 4), ("idb", 0)), writes=(btrt,), label="ptr")
                    p.op("dve", (lambda j: lambda e: e.tensor_copy(
                        PT[:, j * P:(j + 1) * P], btr[:, (j % 8) * P:(j % 8 + 1) * P]))(j),
                        reads=(btrt,), writes=(("PTx", j),), label="ptc")
                for vh in range(2):
                    ob, obt = banks[4 + vh], bts[4 + vh]
                    for j in range(KB // P):
                        p.op("pe", (lambda j, vh, s, ob: lambda e: e.matmul(
                            ob[:, 0:512], PT[:, j * P:(j + 1) * P],
                            Vblk[s][:, j * 1024 + vh * 512:j * 1024 + (vh + 1) * 512],
                            start=(j == 0), stop=(j == KB // P - 1)))(j, vh, s, ob),
                            reads=(("PTx", j), ("Vb", s, j)), writes=(obt,), label="pv")
                    if first:
                        p.op("dve", (lambda t, vh, ob: lambda e: e.tensor_copy(
                            Ot[t][:, vh * 512:(vh + 1) * 512], ob[:, 0:512]))(t, vh, ob),
                            reads=(obt,), writes=(("Ot", t, vh),), label="fo0")
                    else:
                        p.op("dve", (lambda t, vh, ob: lambda e: e.scalar_tensor_tensor(
                            Ot[t][:, vh * 512:(vh + 1) * 512], Ot[t][:, vh * 512:(vh + 1) * 512],
                            rws[:, 5:6], ob[:, 0:512], ALU.mult, ALU.add))(t, vh, ob),
                            reads=(obt, ("rws", 5), ("Ot", t, vh)), writes=(("Ot", t, vh),), label="fou")
        load_fm(p, x_own, xc, "x", QB, dcol0=qb * QB)
        for t in range(QBT):
            p.op("dve", (lambda t: lambda e: e.reciprocal(
                mrow[t][:, 2:3], mrow[t][:, 1:2]))(t),
                reads=(("mr", t, 1),), writes=(("mr", t, 2),), label="flinv")
            p.op("dve", (lambda t: lambda e: e.tensor_scalar(
                Ot[t][:, 0:1024], Ot[t][:, 0:1024], mrow[t][:, 2:3], None, ALU.mult))(t),
                reads=(("Ot", t, 0), ("Ot", t, 1), ("mr", t, 2)),
                writes=(("Ot", t, 0), ("Ot", t, 1)), label="fnrm")
            p.op("pool", (lambda t: lambda e: e.tensor_copy(P1[:, 0:1024], Ot[t][:, 0:1024]))(t),
                 reads=(("Ot", t, 0), ("Ot", t, 1)), writes=(("P1", 0), ("P1", 1)), label="focb")
            for ch in range(NCH):
                p.op("pe", (lambda ch: lambda e: e.transpose(
                    btr[:, (ch % 8) * P:(ch % 8 + 1) * P], P1[:, ch * P:(ch + 1) * P], idb[:]))(ch),
                    reads=(("P1", ch // 4), ("idb", 0)), writes=(btrt,), label="otr")
                p.op("dve", (lambda ch, t: lambda e: e.tensor_copy(
                    PT[:, ch * P:(ch + 1) * P], btr[:, (ch % 8) * P:(ch % 8 + 1) * P]))(ch, t),
                    reads=(btrt,), writes=(("PTx", ch),), label="otc")
            for ch in range(NCH):
                if variant == "l1":
                    p.op("dve", (lambda ch, t: lambda e: e.tensor_tensor(
                        c.tmp[:, 0:P], PT[:, ch * P:(ch + 1) * P],
                        zloc[ch][:, t * P:(t + 1) * P], ALU.add))(ch, t),
                        reads=(("PTx", ch), ("zloc", ch)), writes=(("ln_tmp", 0),), label="l1a")
                    p.op("dve", (lambda ch, t: lambda e: e.tensor_scalar(
                        c.tmp[:, 0:P], c.tmp[:, 0:P], macc[:, ch:ch + 1], None, ALU.add))(ch, t),
                        reads=(("ln_tmp", 0), ("macc", 0)), writes=(("ln_tmp", 0),), label="l1b")
                    p.op("dve", (lambda ch, t: lambda e: e.tensor_tensor(
                        xc[ch][:, t * P:(t + 1) * P], xc[ch][:, t * P:(t + 1) * P],
                        c.tmp[:, 0:P], ALU.add))(ch, t),
                        reads=(("ln_tmp", 0), ("x", ch)), writes=(("x", ch),), label="l1c")
                else:
                    p.op("dve", (lambda ch, t: lambda e: e.tensor_tensor(
                        xc[ch][:, t * P:(t + 1) * P], xc[ch][:, t * P:(t + 1) * P],
                        PT[:, ch * P:(ch + 1) * P], ALU.add))(ch, t),
                        reads=(("PTx", ch), ("x", ch)), writes=(("x", ch),), label="l2c")
        out_keys = store_fm(p, xout, xc, "x", QB, dcol0=qb * QB)
    p.finalize(final_waits=out_keys)
    return nc


def stage_fullattn(variant, x_fm_batch, h, inp, Tloc=4096):
    t0 = h * Tloc
    d = {
        "x_full": x_fm_batch,
        "x_own": np.ascontiguousarray(x_fm_batch[:, t0:t0 + Tloc]),
        "identb": np.eye(128, dtype=BF16NP),
    }
    if variant == "l1":
        d["wqT"] = wT(inp["L1_attn_Wq"])
        d["wkT"] = wT(inp["L1_attn_Wk"])
        d["bqv"] = np.concatenate([chunk_cols(inp["L1_attn_bq"]), chunk_cols(inp["L1_attn_bk"])], axis=1)
    else:
        d["wqkvT"] = wT(inp["L2_attn_W_qkv"])
        b = inp["L2_attn_b_qkv"]
        d["bqv"] = np.concatenate([chunk_cols(b[0:1024]), chunk_cols(b[1024:2048]),
                                   chunk_cols(b[2048:3072])], axis=1)
    return d



def build_l3attn(Tfull=8192, L=64):
    nc = bass.Bass("TRN2", target_bir_lowering=False)
    x_full = nc.dram_tensor("x_full", [1024, Tfull], F32, kind="ExternalInput").ap()
    wA = nc.dram_tensor("wA", [1024, 1024], F32, kind="ExternalInput").ap()
    wAT = nc.dram_tensor("wAT", [1024, 1024], F32, kind="ExternalInput").ap()
    wB = nc.dram_tensor("wB", [1024, 1024], F32, kind="ExternalInput").ap()
    wBT = nc.dram_tensor("wBT", [1024, 1024], F32, kind="ExternalInput").ap()
    wC = nc.dram_tensor("wC", [1024, 1024], F32, kind="ExternalInput").ap()
    wCT = nc.dram_tensor("wCT", [1024, 1024], F32, kind="ExternalInput").ap()
    identb = nc.dram_tensor("identb", [P, P], BF16, kind="ExternalInput").ap()
    xout = nc.dram_tensor("xout", [1024, Tfull], F32, kind="ExternalOutput").ap()
    xbs = nc.dram_tensor("xbs", [1024, Tfull], BF16, kind="Internal").ap()

    NC_ = Tfull // L
    p = Prog(nc)
    banks = [p.psum(f"pb{i}", [P, 512], F32) for i in range(7)]
    bts = [("pb", i) for i in range(7)]
    btr = p.psum("ptr", [P, 1024], BF16)
    btrt = ("ptr", 0)
    c = LNCtx(p, banks=[(banks[i], bts[i]) for i in range(4)])
    TBL = 512
    xc = [p.sbuf(f"x{k}", [P, TBL], F32, side="right") for k in range(NCH)]
    zb = [p.sbuf(f"z{k}", [P, TBL], BF16, side="right") for k in range(NCH)]
    xbb = [p.sbuf(f"xbb{k}", [P, TBL], BF16, side="right") for k in range(NCH)]
    idb = p.sbuf("idb", [P, P], BF16)
    svecs = p.sbuf("svecs", [P, 4], F32)
    vrow = p.sbuf("vrow", [1, 8], F32)
    ones2 = p.sbuf("ones2", [1, 2], F32)
    uv = p.sbuf("uv", [P, 16], F32)
    vv = p.sbuf("vv", [P, 16], F32)
    wv = p.sbuf("wv", [P, 16], F32)
    vsq = p.sbuf("vsq", [P, 16], F32)
    PWf1 = ProjWeights(p, wA, 1024, 1024, "pf1", dtype=F32, side="right")
    PWf2 = ProjWeights(p, wAT, 1024, 1024, "pf2", dtype=F32, side="right")
    PWb = ProjWeights(p, wBT, 1024, 1024, "wb", side="right")
    p.op("sp", lambda e: e.dma_start(idb[:], identb[:]), writes=(("idb", 0),), dma="Lid")
    p.op("pool", lambda e: e.memset(ones2[:], 1.0), writes=(("ones2", 0),))

    def rownorm_inv(tagpref, eps):
        p.op("act", lambda e: e.activation(vrow[0:1, 0:2], vrow[0:1, 2:4], AF.Sqrt),
             reads=(("vrow", 1),), writes=(("vrow", 0),), label=tagpref + "sq")
        p.op("dve", (lambda eps: lambda e: e.tensor_scalar(
            vrow[0:1, 0:2], vrow[0:1, 0:2], eps, None, ALU.add))(eps),
            reads=(("vrow", 0),), writes=(("vrow", 0),), label=tagpref + "ad")
        p.op("dve", lambda e: e.reciprocal(vrow[0:1, 0:2], vrow[0:1, 0:2]),
             reads=(("vrow", 0),), writes=(("vrow", 0),), label=tagpref + "rc")

    def matvec(dst, dst_tag, src, src_tag, PWx, dram, nm):
        for mi in range(8):
            b, wtag, mw = PWx.load(mi, dram=dram)
            bank, bt = banks[4], bts[4]
            for k in range(NCH):
                p.op("pe", (lambda b, k, PWx: lambda e: e.matmul(
                    bank[0:P, 0:2], PWx.tile(b, k, P), src[:, 2 * k:2 * k + 2],
                    start=(k == 0), stop=(k == NCH - 1)))(b, k, PWx),
                    reads=(wtag, (src_tag, 0)), writes=(bt,), label=nm + "mv")
            p.op("dve", (lambda mi, bank: lambda e: e.tensor_copy(
                dst[:, 2 * mi:2 * mi + 2], bank[0:P, 0:2]))(mi, bank),
                reads=(bt,), writes=((dst_tag, 0),), label=nm + "ev")

    def vnorm2_to_row(src, src_tag, col, nm):
        p.op("act", (lambda: lambda e: e.activation(vsq[:], src[:], AF.Square))(),
             reads=((src_tag, 0),), writes=(("vsq", 0),), label=nm + "sq")
        bank, bt = banks[5], bts[5]
        p.op("pe", lambda e: e.matmul(bank[0:1, 0:16], c.onesc[:], vsq[:], start=True, stop=True),
             reads=(("vsq", 0), ("ln_onesc", 0)), writes=(bt,), label=nm + "st")
        p.op("dve", lambda e: e.reduce_sum(vrow[0:1, 4:5], bank[0:1, 0:16], axis=mybir.AxisListType.X),
             reads=(bt,), writes=(("vrow", 2),), label=nm + "rs")
        p.op("dve", (lambda col: lambda e: e.tensor_scalar(
            vrow[0:1, col:col + 2], ones2[0:1, 0:2], vrow[0:1, 4:5], 0.5,
            ALU.mult, ALU.mult))(col),
            reads=(("vrow", 2), ("ones2", 0)), writes=(("vrow", 1),), label=nm + "dp")

    def scale_vec(vec, vec_tag, nm):
        bank, bt = banks[5], bts[5]
        p.op("pe", lambda e: e.matmul(bank[:, 0:2], c.onesr[:], vrow[0:1, 0:2], start=True, stop=True),
             reads=(("vrow", 0), ("ln_onesr", 0)), writes=(bt,), label=nm + "bc")
        for k in range(NCH):
            p.op("dve", (lambda k, bank: lambda e: e.tensor_tensor(
                vec[:, 2 * k:2 * k + 2], vec[:, 2 * k:2 * k + 2], bank[:, 0:2], ALU.mult))(k, bank),
                reads=((vec_tag, 0), bt), writes=((vec_tag, 0),), label=nm + f"sc{k}")

    def sigma(dram_nat, dram_t, svec_col, nm):
        p.op("pool", lambda e: e.memset(uv[:], 1.0 / 32.0), writes=(("uv", 0),), label=nm + "ui")
        for it in range(10):
            matvec(vv, "vv", uv, "uv", PWf1, dram_nat, nm + f"v{it}")
            vnorm2_to_row(vv, "vv", 2, nm + f"vn{it}")
            rownorm_inv(nm + f"vr{it}", 1e-12)
            scale_vec(vv, "vv", nm + f"vs{it}")
            matvec(uv, "uv", vv, "vv", PWf2, dram_t, nm + f"u{it}")
            vnorm2_to_row(uv, "uv", 2, nm + f"un{it}")
            rownorm_inv(nm + f"ur{it}", 1e-12)
            scale_vec(uv, "uv", nm + f"us{it}")
        matvec(vv, "vv", uv, "uv", PWf1, dram_nat, nm + "vf")
        vnorm2_to_row(vv, "vv", 2, nm + "vnf")
        rownorm_inv(nm + "vrf", 1e-12)
        scale_vec(vv, "vv", nm + "vsf")
        matvec(wv, "wv", vv, "vv", PWf2, dram_t, nm + "wv")
        p.op("dve", lambda e: e.tensor_tensor(vsq[:], uv[:], wv[:], ALU.mult),
             reads=(("uv", 0), ("wv", 0)), writes=(("vsq", 0),), label=nm + "uw")
        bank, bt = banks[5], bts[5]
        p.op("pe", lambda e: e.matmul(bank[0:1, 0:16], c.onesc[:], vsq[:], start=True, stop=True),
             reads=(("vsq", 0), ("ln_onesc", 0)), writes=(bt,), label=nm + "ss")
        p.op("dve", lambda e: e.reduce_sum(vrow[0:1, 4:5], bank[0:1, 0:16], axis=mybir.AxisListType.X),
             reads=(bt,), writes=(("vrow", 2),), label=nm + "sr")
        p.op("dve", lambda e: e.tensor_scalar(vrow[0:1, 4:5], vrow[0:1, 4:5], 0.5, None, ALU.mult),
             reads=(("vrow", 2),), writes=(("vrow", 2),), label=nm + "sh")
        p.op("dve", lambda e: e.reciprocal(vrow[0:1, 4:5], vrow[0:1, 4:5]),
             reads=(("vrow", 2),), writes=(("vrow", 2),), label=nm + "si")
        p.op("pe", lambda e: e.matmul(bank[:, 0:2], c.onesr[:], vrow[0:1, 4:6], start=True, stop=True),
             reads=(("vrow", 2), ("ln_onesr", 0)), writes=(bt,), label=nm + "sb")
        p.op("dve", (lambda svec_col, bank: lambda e: e.tensor_copy(
            svecs[:, svec_col:svec_col + 1], bank[:, 0:1]))(svec_col, bank),
            reads=(bt,), writes=(("svecs", svec_col),), label=nm + "sv")


    sigma(wA, wAT, 0, "sA")
    sigma(wB, wBT, 1, "sB")
    sigma(wC, wCT, 2, "sC")

    for blk in range(Tfull // TBL):
        b0 = blk * TBL
        load_fm(p, x_full, xc, "x", TBL, dcol0=b0)
        ln_fm(p, c, xc, "x", TBL, z_chunks=zb, z_name="z")
        def epi_b(mi, t0, tw, bank, btag):
            p.op("dve", (lambda mi, t0, tw, bank: lambda e: e.tensor_scalar(
                xbb[mi][:, t0:t0 + tw], bank[:, 0:tw], svecs[:, 1:2], None, ALU.mult))(mi, t0, tw, bank),
                reads=(btag, ("svecs", 1)), writes=(("xbb", mi),), label=f"exb{mi}")
        proj_sweep(p, PWb, zb, "z", TBL, banks[0:4], bts[0:4], epi_b)
        for k in range(NCH):
            p.op("sp", (lambda k, b0: lambda e: e.dma_start(
                xbs[P * k:P * (k + 1), b0:b0 + TBL], xbb[k][:, 0:TBL]))(k, b0),
                reads=(("xbb", k),), dma=f"SXB{k}", label="stxb")

    At = [p.sbuf(f"At{k}", [P, 1024], BF16) for k in range(NCH)]
    fstrip = p.sbuf("fstrip", [P, 1024], F32, side="right")
    An = [p.sbuf(f"An{k}", [P, 1024], BF16, side="right") for k in range(NCH)]
    for (dst, dname, srcd) in ((At, "At", wAT), (An, "An", wA)):
        for k in range(NCH):
            p.op("sp", (lambda k, srcd: lambda e: e.dma_start(
                fstrip[:], srcd[P * k:P * (k + 1), :]))(k, srcd),
                writes=(("fstrip", 0),), dma="Lfs", label="lfs")
            p.op("dve", (lambda k, dst: lambda e: e.tensor_scalar(
                dst[k][:], fstrip[:], svecs[:, 0:1], None, ALU.mult))(k, dst),
                reads=(("fstrip", 0), ("svecs", 0)), writes=((dname, k),), label=f"sc{dname}{k}")
    pw_n = [[p.sbuf(f"pwn{s}_{k}", [P, 1024], BF16, side="right") for k in range(NCH)] for s in range(2)]
    pw_t = [[p.sbuf(f"pwt{s}_{k}", [P, 1024], BF16, side="right") for k in range(NCH)] for s in range(2)]
    cur_n, cur_t, cur_nt, cur_tt = An, At, "An", "At"
    for s in range(6):
        dst_n, dst_t = pw_n[s % 2], pw_t[s % 2]
        dnt, dtt = f"pwn{s % 2}", f"pwt{s % 2}"
        for mi in range(NCH):
            bank, bt = banks[mi % 2], bts[mi % 2]
            for jt in range(2):
                for k in range(NCH):
                    p.op("pe", (lambda mi, jt, k, bank, cur_t, cur_n: lambda e: e.matmul(
                        bank[:, 0:512], cur_t[k][:, mi * P:(mi + 1) * P],
                        cur_n[k][:, jt * 512:(jt + 1) * 512],
                        start=(k == 0), stop=(k == NCH - 1)))(mi, jt, k, bank, cur_t, cur_n),
                        reads=((cur_tt, k), (cur_nt, k)), writes=(bt,), label=f"sq{s}")
                p.op("dve", (lambda mi, jt, bank, dst_n: lambda e: e.tensor_copy(
                    dst_n[mi][:, jt * 512:(jt + 1) * 512], bank[:, 0:512]))(mi, jt, bank, dst_n),
                    reads=(bt,), writes=((dnt, mi),), label=f"sqe{s}")
            for jt8 in range(8):
                p.op("pe", (lambda mi, jt8, dst_n: lambda e: e.transpose(
                    btr[:, (jt8 % 8) * P:(jt8 % 8 + 1) * P], dst_n[mi][:, jt8 * P:(jt8 + 1) * P],
                    idb[:]))(mi, jt8, dst_n),
                    reads=((dnt, mi), ("idb", 0)), writes=(btrt,), label=f"sqt{s}")
                p.op("dve", (lambda mi, jt8, dst_t: lambda e: e.tensor_copy(
                    dst_t[jt8][:, mi * P:(mi + 1) * P], btr[:, (jt8 % 8) * P:(jt8 % 8 + 1) * P]))(mi, jt8, dst_t),
                    reads=(btrt,), writes=((dtt, jt8),), label=f"sqc{s}")
        cur_n, cur_t, cur_nt, cur_tt = dst_n, dst_t, dnt, dtt
    A64 = [p.sbuf(f"A64_{k}", [P, 1024], BF16) for k in range(NCH)]
    for k in range(NCH):
        p.op("pool", (lambda k, cur_t: lambda e: e.tensor_copy(A64[k][:], cur_t[k][:]))(k, cur_t),
             reads=((cur_tt, k),), writes=(("A64", k),), label="a64c")

    p.phase()
    for s in (1, 0):
        for k in reversed(range(NCH)):
            p.free(f"pwt{s}_{k}")
    for s in (1, 0):
        for k in reversed(range(NCH)):
            p.free(f"pwn{s}_{k}")
    for k in reversed(range(NCH)):
        p.free(f"An{k}")
    p.free("fstrip")
    PWb.free(); PWf2.free(); PWf1.free()
    for k in reversed(range(NCH)):
        p.free(f"xbb{k}")
    for k in reversed(range(NCH)):
        p.free(f"z{k}")
    for k in reversed(range(NCH)):
        p.free(f"x{k}")

    xb = [p.sbuf(f"hs{k}", [P, Tfull], BF16) for k in range(NCH)]
    Uc = [[p.sbuf(f"U{s}_{k}", [P, NC_], BF16) for k in range(NCH)] for s in range(2)]
    Bi = [p.sbuf(f"Bi{k}", [P, NC_], BF16) for k in range(NCH)]
    G = [p.sbuf(f"G{k}", [P, NC_], BF16) for k in range(NCH)]
    gv = [p.sbuf(f"gv{s}", [P, NCH], BF16) for s in range(2)]
    for k in range(NCH):
        p.op("sp", (lambda k: lambda e: e.dma_start(
            xb[k][:], xbs[P * k:P * (k + 1), :]))(k),
            writes=(("hs", k),), dma=f"LXB{k}", label="lxb")
        p.op("pool", (lambda k: lambda e: e.memset(G[k][:], 0.0))(k), writes=(("G", k),))
        p.op("pool", (lambda k: lambda e: e.memset(Uc[0][k][:], 0.0))(k), writes=(("U0", k),))
    for i in range(L):
        par, nxt = i % 2, (i + 1) % 2
        for k in range(NCH):
            p.op("pool", (lambda k, i: lambda e: e.tensor_copy(
                Bi[k][:], xb[k].rearrange("p (cc l) -> p l cc", l=L)[:, i:i + 1, :]))(k, i),
                reads=(("hs", k),), writes=(("Bi", k),), label=f"big{i}")
        for mi in range(NCH):
            bank, bt = banks[mi % 4], bts[mi % 4]
            for k in range(NCH):
                p.op("pe", (lambda mi, k, par, bank: lambda e: e.matmul(
                    bank[:, 0:NC_], At[k][:, mi * P:(mi + 1) * P], Uc[par][k][:],
                    start=(k == 0), stop=False))(mi, k, par, bank),
                    reads=(("At", k), (f"U{par}", k)), writes=(bt,), label=f"su{i}")
            p.op("pe", (lambda mi, bank: lambda e: e.matmul(
                bank[:, 0:NC_], idb[:], Bi[mi][:], start=False, stop=True))(mi, bank),
                reads=(("Bi", mi), ("idb", 0)), writes=(bt,), label=f"sb{i}")
            p.op("dve", (lambda mi, nxt, bank: lambda e: e.tensor_copy(
                Uc[nxt][mi][:], bank[:, 0:NC_]))(mi, nxt, bank),
                reads=(bt,), writes=((f"U{nxt}", mi),), label=f"se{i}")
            p.op("pool", (lambda mi, nxt, i: lambda e: e.tensor_copy(
                xb[mi].rearrange("p (cc l) -> p l cc", l=L)[:, i:i + 1, :], Uc[nxt][mi][:]))(mi, nxt, i),
                reads=((f"U{nxt}", mi),), writes=(("hs", mi),), label=f"sw{i}")
    p.op("pool", lambda e: e.memset(gv[0][:], 0.0), writes=(("gv0", 0),))
    for cc in range(NC_):
        par, nxt = cc % 2, (cc + 1) % 2
        for mi in range(NCH):
            bank, bt = banks[4 + mi % 2], bts[4 + mi % 2]
            for k in range(NCH):
                p.op("pe", (lambda mi, k, par, cc, bank: lambda e: e.matmul(
                    bank[:, 0:1], A64[k][:, mi * P:(mi + 1) * P], gv[par][:, k:k + 1],
                    start=(k == 0), stop=False))(mi, k, par, cc, bank),
                    reads=(("A64", k), (f"gv{par}", 0)), writes=(bt,), label=f"cg{cc}")
            p.op("pe", (lambda mi, cc, bank: lambda e: e.matmul(
                bank[:, 0:1], idb[:],
                xb[mi].rearrange("p (cc l) -> p cc l", l=L)[:, cc:cc + 1, L - 1:L],
                start=False, stop=True))(mi, cc, bank),
                reads=(("hs", mi), ("idb", 0)), writes=(bt,), label=f"cu{cc}")
            p.op("dve", (lambda mi, nxt, bank: lambda e: e.tensor_copy(
                gv[nxt][:, mi:mi + 1], bank[:, 0:1]))(mi, nxt, bank),
                reads=(bt,), writes=((f"gv{nxt}", 0),), label=f"ce{cc}")
            if cc + 1 < NC_:
                p.op("pool", (lambda mi, nxt, cc: lambda e: e.tensor_copy(
                    G[mi][:, cc + 1:cc + 2], gv[nxt][:, mi:mi + 1]))(mi, nxt, cc),
                    reads=((f"gv{nxt}", 0),), writes=(("G", mi),), label=f"cw{cc}")
    Pp = [[p.sbuf(f"Pp{s}_{k}", [P, NC_], BF16) for k in range(NCH)] for s in range(2)]
    for k in range(NCH):
        p.op("pool", (lambda k: lambda e: e.tensor_copy(Pp[1][k][:], G[k][:]))(k),
             reads=(("G", k),), writes=(("Pp1", k),), label="ppi")
    for i in range(L):
        par, nxt = (i + 1) % 2, i % 2
        for mi in range(NCH):
            bank, bt = banks[mi % 4], bts[mi % 4]
            for k in range(NCH):
                p.op("pe", (lambda mi, k, par, bank: lambda e: e.matmul(
                    bank[:, 0:NC_], At[k][:, mi * P:(mi + 1) * P], Pp[par][k][:],
                    start=(k == 0), stop=(k == NCH - 1)))(mi, k, par, bank),
                    reads=(("At", k), (f"Pp{par}", k)), writes=(bt,), label=f"pp{i}")
            p.op("dve", (lambda mi, nxt, bank: lambda e: e.tensor_copy(
                Pp[nxt][mi][:], bank[:, 0:NC_]))(mi, nxt, bank),
                reads=(bt,), writes=((f"Pp{nxt}", mi),), label=f"ppe{i}")
            p.op("pool", (lambda mi, nxt, i: lambda e: e.tensor_tensor(
                xb[mi].rearrange("p (cc l) -> p l cc", l=L)[:, i:i + 1, :],
                xb[mi].rearrange("p (cc l) -> p l cc", l=L)[:, i:i + 1, :],
                Pp[nxt][mi][:], ALU.add))(mi, nxt, i),
                reads=((f"Pp{nxt}", mi), ("hs", mi)), writes=(("hs", mi),), label=f"ppa{i}")

    p.phase()
    xc2 = [p.sbuf(f"x2{k}", [P, TBL], F32, side="right") for k in range(NCH)]
    PWc = ProjWeights(p, wCT, 1024, 1024, "wc", side="right")
    out_keys = []
    for blk in range(Tfull // TBL):
        b0 = blk * TBL
        load_fm(p, x_full, xc2, "x2", TBL, dcol0=b0)
        def epi_c(mi, t0, tw, bank, btag):
            p.op("dve", (lambda mi, t0, tw, bank: lambda e: e.scalar_tensor_tensor(
                xc2[mi][:, t0:t0 + tw], bank[:, 0:tw], svecs[:, 2:3],
                xc2[mi][:, t0:t0 + tw], ALU.mult, ALU.add))(mi, t0, tw, bank),
                reads=(btag, ("svecs", 2), ("x2", mi)), writes=(("x2", mi),), label=f"eyc{mi}")
        proj_sweep(p, PWc, xb, "hs", TBL, banks[0:4], bts[0:4], epi_c, z_col0=b0)
        out_keys = store_fm(p, xout, xc2, "x2", TBL, dcol0=b0)
    p.finalize(final_waits=out_keys)
    return nc


def stage_l3attn(x_fm_batch, inp):
    return {
        "x_full": x_fm_batch,
        "wA": np.ascontiguousarray(inp["L3_attn_W_A"]), "wAT": wT(inp["L3_attn_W_A"]),
        "wB": np.ascontiguousarray(inp["L3_attn_W_B"]), "wBT": wT(inp["L3_attn_W_B"]),
        "wC": np.ascontiguousarray(inp["L3_attn_W_C"]), "wCT": wT(inp["L3_attn_W_C"]),
        "identb": np.eye(128, dtype=BF16NP),
    }



def build_lninit(Text=7168):
    nc = bass.Bass("TRN2", target_bir_lowering=False)
    x_ext = nc.dram_tensor("x_ext", [1024, Text], F32, kind="ExternalInput").ap()
    yout = nc.dram_tensor("yout", [1024, Text], F32, kind="ExternalOutput").ap()
    p = Prog(nc)
    banks = [p.psum(f"pb{i}", [P, 512], F32) for i in range(4)]
    bts = [("pb", i) for i in range(4)]
    c = LNCtx(p, banks=[(banks[i], bts[i]) for i in range(4)])
    TBL = 1024
    xc = [p.sbuf(f"x{k}", [P, TBL], F32) for k in range(NCH)]
    zf = [p.sbuf(f"z{k}", [P, TBL], F32) for k in range(NCH)]
    for blk in range(Text // TBL):
        load_fm(p, x_ext, xc, "x", TBL, dcol0=blk * TBL)
        ln_fm(p, c, xc, "x", TBL, zf_chunks=zf, zf_name="z")
        keys = store_fm(p, yout, zf, "z", TBL, dcol0=blk * TBL)
    p.finalize(final_waits=keys)
    return nc


def build_l3fin(Tloc=4096, scale=0.5):
    nc = bass.Bass("TRN2", target_bir_lowering=False)
    x_own = nc.dram_tensor("x_own", [1024, Tloc], F32, kind="ExternalInput").ap()
    y_own = nc.dram_tensor("y_own", [1024, Tloc], F32, kind="ExternalInput").ap()
    xout = nc.dram_tensor("xout", [1024, Tloc], F32, kind="ExternalOutput").ap()
    p = Prog(nc)
    TBL = 2048
    xc = [p.sbuf(f"x{k}", [P, TBL], F32) for k in range(NCH)]
    yc = [p.sbuf(f"y{k}", [P, TBL], F32) for k in range(NCH)]
    for blk in range(Tloc // TBL):
        load_fm(p, x_own, xc, "x", TBL, dcol0=blk * TBL)
        load_fm(p, y_own, yc, "y", TBL, dcol0=blk * TBL, eng="pool")
        for k in range(NCH):
            p.op("dve", (lambda k: lambda e: e.scalar_tensor_tensor(
                xc[k][:, 0:TBL], yc[k][:, 0:TBL], scale, xc[k][:, 0:TBL], ALU.mult, ALU.add))(k),
                reads=(("y", k), ("x", k)), writes=(("x", k),), label=f"fin{k}")
        keys = store_fm(p, xout, xc, "x", TBL, dcol0=blk * TBL)
    p.finalize(final_waits=keys)
    return nc


def build_l3ffn_layer(li, last=False, Text0=7168, RING=11):
    b_i = 512 * li
    Li = Text0 - b_i
    Ltiles = Li // P
    nc = bass.Bass("TRN2", target_bir_lowering=False)
    yext = nc.dram_tensor("yext", [1024, 2 + Li], F32, kind="ExternalInput").ap()
    qkvw = nc.dram_tensor("qkvw", [1024, 3072], F32, kind="ExternalInput").ap()
    wow = nc.dram_tensor("wow", [1024, 1024], F32, kind="ExternalInput").ap()
    w1w = nc.dram_tensor("w1w", [1024, 5460], F32, kind="ExternalInput").ap()
    w2w = nc.dram_tensor("w2w", [2730, 1024], F32, kind="ExternalInput").ap()
    qsc = nc.dram_tensor("qsc", [P, 2], F32, kind="ExternalInput").ap()
    lngb = nc.dram_tensor("lngb", [P, 16], F32, kind="ExternalInput").ap()
    b2c = nc.dram_tensor("b2c", [P, NCH], F32, kind="ExternalInput").ap()
    b1a = nc.dram_tensor("b1a", [P, 22], F32, kind="ExternalInput").ap()
    b1g = nc.dram_tensor("b1g", [P, 22], F32, kind="ExternalInput").ap()
    tabQC = nc.dram_tensor("tabQC", [P, Li], F32, kind="ExternalInput").ap()
    tabQS = nc.dram_tensor("tabQS", [P, Li], F32, kind="ExternalInput").ap()
    tabKC = nc.dram_tensor("tabKC", [P, Li], F32, kind="ExternalInput").ap()
    tabKS = nc.dram_tensor("tabKS", [P, Li], F32, kind="ExternalInput").ap()
    p32 = nc.dram_tensor("p32", [P, P], BF16, kind="ExternalInput").ap()
    identb = nc.dram_tensor("identb", [P, P], BF16, kind="ExternalInput").ap()
    masks = nc.dram_tensor("masks", [4, P, 512], F32, kind="ExternalInput").ap()
    hones = nc.dram_tensor("hones", [P, 2], F32, kind="ExternalInput").ap()
    hexp = nc.dram_tensor("hexp", [2, P], F32, kind="ExternalInput").ap()
    ysc = nc.dram_tensor("ysc", [1024, 2 + Li], F32, kind="Internal").ap()
    if last:
        x_own = nc.dram_tensor("x_own", [1024, 4096], F32, kind="ExternalInput").ap()
        xout = nc.dram_tensor("xout", [1024, 4096], F32, kind="ExternalOutput").ap()
        y2 = nc.dram_tensor("yfin", [1024, 4096], F32, kind="Internal").ap()
    else:
        yout = nc.dram_tensor("yout", [1024, Li - 511], F32, kind="ExternalOutput").ap()

    p = Prog(nc)
    banks = [p.psum(f"pb{i}", [P, 512], F32) for i in range(7)]
    bts = [("pb", i) for i in range(7)]
    btr = p.psum("ptr", [P, 1024], BF16)
    btrt = ("ptr", 0)
    c = LNCtx(p, banks=[(banks[i], bts[i]) for i in range(4)])
    TB = 512
    yb = [p.sbuf(f"y{k}", [P, TB + 2], F32) for k in range(NCH)]
    ybs = [p.sbuf(f"ys{k}", [P, TB + 2], F32) for k in range(4)]
    wb = [p.sbuf(f"w{k}", [P, TB], BF16) for k in range(NCH)]
    pf = [p.sbuf(f"pf{k}", [P, TB], F32) for k in range(NCH)]
    vb = [p.sbuf(f"vbk{k}", [P, TB], BF16) for k in range(NCH)]
    Qr = [p.sbuf(f"Qr{k}", [P, TB], BF16) for k in range(NCH)]
    Kring = [p.sbuf(f"Kr{k}", [P, RING * P], BF16) for k in range(NCH)]
    Vring = p.sbuf("Vring", [P, RING * 1024], BF16)
    Ofm = [p.sbuf(f"Of{k}", [P, TB], BF16) for k in range(NCH)]
    Ab = [p.sbuf(f"Ab{k}", [P, TB], BF16) for k in range(22)]
    tabs = [p.sbuf(f"tab{i}", [P, TB], F32) for i in range(4)]
    scr = p.sbuf("scr", [P, TB], F32)
    scrb = p.sbuf("scrb", [P, TB], BF16)
    P1 = p.sbuf("P1s", [P, 512], BF16)
    PT = p.sbuf("PTs", [P, 512], BF16)
    rws = p.sbuf("rws", [P, 8], F32)
    r2 = p.sbuf("r2", [2, 512], F32)
    hob = p.sbuf("hob", [P, 2], F32)
    heb = p.sbuf("heb", [2, P], F32)
    qss = p.sbuf("qss", [P, 2], F32)
    lgb = p.sbuf("lgb", [P, 16], F32)
    b2s = p.sbuf("b2s", [P, NCH], F32)
    b1as = p.sbuf("b1as", [P, 22], F32)
    b1gs = p.sbuf("b1gs", [P, 22], F32)
    p32s = p.sbuf("p32s", [P, P], BF16)
    idb = p.sbuf("idb", [P, P], BF16)
    msks = p.sbuf("msks", [P, 4 * 512], F32)
    PWqkv = ProjWeights(p, qkvw, 1024, 3072, "wqkv")
    PWo = ProjWeights(p, wow, 1024, 1024, "wo")
    PW1a = ProjWeights(p, w1w, 1024, 2730, "w1a")
    PW1g = ProjWeights(p, w1w, 1024, 2730, "w1g")
    PW2 = ProjWeights(p, w2w, 2730, 1024, "w2")

    for nm, dst, sr in (("hob", hob, hones), ("heb", heb, hexp), ("qss", qss, qsc),
                        ("lgb", lgb, lngb), ("b2s", b2s, b2c), ("b1as", b1as, b1a),
                        ("b1gs", b1gs, b1g), ("p32s", p32s, p32), ("idb", idb, identb)):
        p.op("sp", (lambda dst, sr: lambda e: e.dma_start(dst[:], sr[:]))(dst, sr),
             writes=((nm, 0),), dma="L" + nm)
    msrc = masks.rearrange("s q m -> q s m")
    p.op("sp", lambda e: e.dma_start(msks[:, :].rearrange("q (s m) -> q s m", m=512), msrc),
         writes=(("msks", 0),), dma="Lmsk")
    for k in range(NCH):
        p.op("pool", (lambda k: lambda e: e.memset(Kring[k][:], 0.0))(k), writes=(("KR", k),))
    p.op("pool", lambda e: e.memset(Vring[:], 0.0), writes=(("VR", -1),))

    sp0 = 24 - 4 * li
    def mslot(t):
        if t == sp0:
            return 2
        if t == sp0 + 1:
            return 3
        return t % 2

    def l2norm_rope(src, src_name, dst, dst_name, tw, gcol, tabC, tabCt, tabS, tabSt, dcol0):
        for ch in range(NCH):
            bank_s, bst = banks[4], bts[4]
            bank_rb, brt = banks[5], bts[5]
            bank_sw, bwt = banks[6], bts[6]
            sq = ch % 2
            p.op("act", (lambda ch, sq, tw: lambda e: e.activation(
                c.xsq[sq][:, 0:tw], src[ch][:, 0:tw], AF.Square))(ch, sq, tw),
                reads=((src_name, ch),), writes=(("ln_xsq", sq),), label="l2sq")
            p.op("pe", (lambda sq, tw: lambda e: e.matmul(
                bank_s[0:2, 0:tw], hob[:], c.xsq[sq][:, 0:tw], start=True, stop=True))(sq, tw),
                reads=(("ln_xsq", sq), ("hob", 0)), writes=(bst,), label="l2ss")
            p.op("act", (lambda tw: lambda e: e.activation(
                r2[0:2, 0:tw], bank_s[0:2, 0:tw], AF.Sqrt))(tw),
                reads=(bst,), writes=(("r2", 0),), label="l2sd")
            p.op("dve", (lambda tw: lambda e: e.tensor_scalar(
                r2[0:2, 0:tw], r2[0:2, 0:tw], 1e-12, None, ALU.max))(tw),
                reads=(("r2", 0),), writes=(("r2", 0),), label="l2mx")
            p.op("dve", (lambda tw: lambda e: e.reciprocal(
                r2[0:2, 0:tw], r2[0:2, 0:tw]))(tw),
                reads=(("r2", 0),), writes=(("r2", 0),), label="l2rc")
            p.op("pe", (lambda tw: lambda e: e.matmul(
                bank_rb[:, 0:tw], heb[:], r2[0:2, 0:tw], start=True, stop=True))(tw),
                reads=(("r2", 0), ("heb", 0)), writes=(brt,), label="l2bc")
            p.op("dve", (lambda ch, tw, gcol: lambda e: e.scalar_tensor_tensor(
                scr[:, 0:tw], src[ch][:, 0:tw], qss[:, gcol:gcol + 1], bank_rb[:, 0:tw],
                ALU.mult, ALU.mult))(ch, tw, gcol),
                reads=((src_name, ch), ("qss", 0), brt), writes=(("scr", 0),), label="l2xn")
            p.op("pool", (lambda tw: lambda e: e.tensor_copy(scrb[:, 0:tw], scr[:, 0:tw]))(tw),
                 reads=(("scr", 0),), writes=(("scrb", 0),), label="l2cb")
            p.op("pe", (lambda tw: lambda e: e.matmul(
                bank_sw[:, 0:tw], p32s[:], scrb[:, 0:tw], start=True, stop=True))(tw),
                reads=(("scrb", 0), ("p32s", 0)), writes=(bwt,), label="l2sw")
            p.op("dve", (lambda tw, tabC, tabCt: lambda e: e.tensor_tensor(
                scr[:, 0:tw], scr[:, 0:tw], tabC[:, 0:tw], ALU.mult))(tw, tabC, tabCt),
                reads=(("scr", 0), tabCt), writes=(("scr", 0),), label="l2c")
            p.op("dve", (lambda tw, tabS, tabSt: lambda e: e.tensor_tensor(
                c.tmp[:, 0:tw], tabS[:, 0:tw], bank_sw[:, 0:tw], ALU.mult))(tw, tabS, tabSt),
                reads=(tabSt, bwt), writes=(("ln_tmp", 0),), label="l2s")
            p.op("dve", (lambda ch, tw, dcol0: lambda e: e.tensor_tensor(
                dst[ch][:, dcol0:dcol0 + tw], scr[:, 0:tw], c.tmp[:, 0:tw], ALU.add))(ch, tw, dcol0),
                reads=(("scr", 0), ("ln_tmp", 0)), writes=((dst_name, ch),), label="l2o")

    blocks = [(0, 2, False), (2, 2, True)]
    t = 4
    while t < Ltiles:
        nt = min(4, Ltiles - t)
        blocks.append((t, nt, True))
        t += nt
    for (t0, nt, do_q) in blocks:
        g0 = P * t0 + 2
        bw = P * nt
        load_fm(p, yext, yb, "yb", bw + 2, dcol0=g0 - 2)
        for k in range(NCH):
            ofs = 2 if k < 4 else 1
            p.op("pool", (lambda k, ofs, bw: lambda e: e.tensor_copy(
                wb[k][:, 0:bw], yb[k][:, ofs:ofs + bw]))(k, ofs, bw),
                reads=(("yb", k),), writes=(("wb", k),), label="shft")
        tcol = P * t0
        for (ti, tdr) in ((0, tabQC), (1, tabQS), (2, tabKC), (3, tabKS)):
            p.op("sp", (lambda ti, tdr, tcol, bw: lambda e: e.dma_start(
                tabs[ti][:, 0:bw], tdr[:, tcol:tcol + bw]))(ti, tdr, tcol, bw),
                writes=(("tab", ti),), dma=f"Ltab{ti}", label="ltab")
        if do_q:
            def epi_q(mi, tt0, tw, bank, btag):
                p.op("dve", (lambda mi, tt0, tw, bank: lambda e: e.tensor_copy(
                    pf[mi][:, tt0:tt0 + tw], bank[:, 0:tw]))(mi, tt0, tw, bank),
                    reads=(btag,), writes=(("pf", mi),), label="eq")
            proj_sweep(p, PWqkv, wb, "wb", bw, banks[0:4], bts[0:4], epi_q,
                       m_chunks=list(range(0, 8)))
            l2norm_rope(pf, "pf", Qr, "Qr", bw, 0, tabs[0], ("tab", 0), tabs[1], ("tab", 1), 0)
        def epi_k(mi, tt0, tw, bank, btag):
            mo = mi - 8
            p.op("dve", (lambda mo, tt0, tw, bank: lambda e: e.tensor_copy(
                pf[mo][:, tt0:tt0 + tw], bank[:, 0:tw]))(mo, tt0, tw, bank),
                reads=(btag,), writes=(("pf", mo),), label="ek")
        proj_sweep(p, PWqkv, wb, "wb", bw, banks[0:4], bts[0:4], epi_k,
                   m_chunks=list(range(8, 16)))
        l2norm_rope(pf, "pf", vb, "kw", bw, 1, tabs[2], ("tab", 2), tabs[3], ("tab", 3), 0)
        for tt in range(nt):
            slot = (t0 + tt) % RING
            for k in range(NCH):
                p.op("pool", (lambda k, tt, slot: lambda e: e.tensor_copy(
                    Kring[k][:, slot * P:(slot + 1) * P], vb[k][:, tt * P:(tt + 1) * P]))(k, tt, slot),
                    reads=(("kw", k),), writes=(("KR", k),), label="krw")
        def epi_v(mi, tt0, tw, bank, btag):
            mo = mi - 16
            p.op("dve", (lambda mo, tt0, tw, bank: lambda e: e.tensor_copy(
                vb[mo][:, tt0:tt0 + tw], bank[:, 0:tw]))(mo, tt0, tw, bank),
                reads=(btag,), writes=(("kw", mo),), label="evv")
        proj_sweep(p, PWqkv, wb, "wb", bw, banks[0:4], bts[0:4], epi_v,
                   m_chunks=list(range(16, 24)))
        for tt in range(nt):
            slot = (t0 + tt) % RING
            for ch in range(NCH):
                p.op("pe", (lambda tt, ch: lambda e: e.transpose(
                    btr[:, (ch % 8) * P:(ch % 8 + 1) * P], vb[ch][:, tt * P:(tt + 1) * P],
                    idb[:]))(tt, ch),
                    reads=(("kw", ch), ("idb", 0)), writes=(btrt,), label="vtr")
                p.op("dve", (lambda slot, ch: lambda e: e.tensor_copy(
                    Vring[:, slot * 1024 + ch * P:slot * 1024 + (ch + 1) * P],
                    btr[:, (ch % 8) * P:(ch % 8 + 1) * P]))(slot, ch),
                    reads=(btrt,), writes=(("VR", slot),), label="vtc")
        if not do_q:
            continue
        for tt in range(nt):
            tl = t0 + tt
            nk = 3 if tl % 2 == 0 else 4
            kt0 = tl - nk + 1
            slotL = [(kt0 + j) % RING for j in range(nk)]
            sl = mslot(tl)
            for ch in range(NCH):
                bA, bAt = banks[0], bts[0]
                bB, bBt = banks[1], bts[1]
                for (bank, bt2, hof) in ((bA, bAt, 0), (bB, bBt, 64)):
                    for j in range(nk):
                        p.op("pe", (lambda ch, tt, hof, j, slotL, bank, nk: lambda e: e.matmul(
                            bank[0:P, j * P:(j + 1) * P],
                            Qr[ch][hof:hof + 64, tt * P:(tt + 1) * P],
                            Kring[ch][hof:hof + 64, slotL[j] * P:(slotL[j] + 1) * P],
                            start=(j == 0), stop=(j == nk - 1),
                            tile_position=(hof, 0)))(ch, tt, hof, j, slotL, bank, nk),
                            reads=(("Qr", ch), ("KR", ch)), writes=(bt2,),
                            label="qk")
                for (bank, bt2, hof) in ((bA, bAt, 0), (bB, bBt, 64)):
                    p.op("dve", (lambda bank, sl, nk: lambda e: e.scalar_tensor_tensor(
                        bank[:, 0:nk * P], bank[:, 0:nk * P], 8.0,
                        msks[:, sl * 512:sl * 512 + nk * P], ALU.mult, ALU.add))(bank, sl, nk),
                        reads=(bt2, ("msks", 0)), writes=(bt2,), label="fmsk")
                    p.op("dve", (lambda bank, nk: lambda e: e.reduce_max(
                        rws[:, 0:1], bank[:, 0:nk * P], axis=mybir.AxisListType.X))(bank, nk),
                        reads=(bt2,), writes=(("rws", 0),), label="fmx")
                    p.op("dve", lambda e: e.tensor_scalar(rws[:, 1:2], rws[:, 0:1], -1.0, None, ALU.mult),
                         reads=(("rws", 0),), writes=(("rws", 1),), label="fng")
                    p.op("act", (lambda bank, nk: lambda e: e.activation(
                        P1[:, 0:nk * P], bank[:, 0:nk * P], AF.Exp, bias=rws[:, 1:2],
                        accum_out=rws[:, 2:3]))(bank, nk),
                        reads=(bt2, ("rws", 1)), writes=(("P1", 0), ("rws", 2)), label="fex")
                    p.op("dve", lambda e: e.reciprocal(rws[:, 3:4], rws[:, 2:3]),
                         reads=(("rws", 2),), writes=(("rws", 3),), label="fri")
                    p.op("dve", (lambda nk: lambda e: e.tensor_scalar(
                        P1[:, 0:nk * P], P1[:, 0:nk * P], rws[:, 3:4], None, ALU.mult))(nk),
                        reads=(("P1", 0), ("rws", 3)), writes=(("P1", 0),), label="fps")
                    for j in range(nk):
                        p.op("pe", (lambda j: lambda e: e.transpose(
                            btr[:, j * P:(j + 1) * P], P1[:, j * P:(j + 1) * P], idb[:]))(j),
                            reads=(("P1", 0), ("idb", 0)), writes=(btrt,), label="ptr")
                        p.op("dve", (lambda j: lambda e: e.tensor_copy(
                            PT[:, j * P:(j + 1) * P], btr[:, j * P:(j + 1) * P]))(j),
                            reads=(btrt,), writes=(("PTx", j),), label="ptc")
                    ob, obt = (banks[2], bts[2]) if hof == 0 else (banks[3], bts[3])
                    for j in range(nk):
                        p.op("pe", (lambda j, ch, hof, slotL, ob: lambda e: e.matmul(
                            ob[0:64, 0:P],
                            Vring[:, slotL[j] * 1024 + ch * P + hof:slotL[j] * 1024 + ch * P + hof + 64],
                            PT[:, j * P:(j + 1) * P], start=(j == 0), stop=(j == nk - 1)))(j, ch, hof, slotL, ob),
                            reads=(("PTx", j), ("VR", slotL[j])), writes=(obt,), label="pv")
                    p.op("dve", (lambda ch, tt, hof, ob: lambda e: e.tensor_copy(
                        Ofm[ch][hof:hof + 64, tt * P:(tt + 1) * P], ob[0:64, 0:P]))(ch, tt, hof, ob),
                        reads=(obt,), writes=(("Of", ch),), label="oev")
        q0 = 2 * P
        def epi_o(mi, tt0, tw, bank, btag):
            p.op("dve", (lambda mi, tt0, tw, bank: lambda e: e.tensor_tensor(
                yb[mi][:, 2 + tt0:2 + tt0 + tw], yb[mi][:, 2 + tt0:2 + tt0 + tw],
                bank[:, 0:tw], ALU.add))(mi, tt0, tw, bank),
                reads=(btag, ("yb", mi)), writes=(("yb", mi),), label="accA")
        proj_sweep(p, PWo, Ofm, "Of", bw, banks[0:4], bts[0:4], epi_o)
        store_fm(p, ysc, yb, "yb", bw, dcol0=g0, col0=2)

    g_ap = [lgb[:, k:k + 1] for k in range(NCH)]
    b_ap = [lgb[:, 8 + k:8 + k + 1] for k in range(NCH)]
    fstart = 512
    fblocks = []
    t = fstart
    while t < 2 + Li:
        bwf = min(TB, 2 + Li - t)
        fblocks.append((t, bwf))
        t += bwf
    out_keys = []
    for (g0, bwf) in fblocks:
        load_fm(p, ysc, yb, "yb", bwf + 2, dcol0=g0 - 2)
        for k in range(4):
            p.op("sp", (lambda k, g0, bwf: lambda e: e.dma_start(
                ybs[k][:, 0:bwf + 2], ysc[P * (k + 4):P * (k + 5), g0 - 3:g0 - 1 + bwf]))(k, g0, bwf),
                writes=(("ybs", k),), dma=f"Lybs{k}", label="lybs")
        lnx = yb[0:4] + ybs
        ln_fm(p, c, lnx, "yb", bwf, z_chunks=wb, z_name="wb",
              col0s=[2] * 8, gain_ap=g_ap, gain_tag=("lgb", 0), bias_ap=b_ap,
              x_col_tags=lambda k: ("yb", k) if k < 4 else ("ybs", k - 4))
        def epi_a(mi, tt0, tw, bank, btag):
            mw2 = min(P, 2730 - mi * P)
            p.op("dve", (lambda mi, tt0, tw, bank, mw2: lambda e: e.tensor_scalar(
                Ab[mi][0:mw2, tt0:tt0 + tw], bank[0:mw2, 0:tw], b1as[0:mw2, mi:mi + 1],
                None, ALU.add))(mi, tt0, tw, bank, mw2),
                reads=(btag, ("b1as", 0)), writes=(("Ab", mi),), label="ea")
        proj_sweep(p, PW1a, wb, "wb", bwf, banks[0:4], bts[0:4], epi_a)
        def epi_g(mi, tt0, tw, bank, btag):
            mw2 = min(P, 2730 - mi * P)
            p.op("act", (lambda mi, tt0, tw, bank, mw2: lambda e: e.activation(
                scrb[0:mw2, 0:tw], bank[0:mw2, 0:tw], AF.Gelu, bias=b1gs[0:mw2, mi:mi + 1]))(mi, tt0, tw, bank, mw2),
                reads=(btag, ("b1gs", 0)), writes=(("scrb", 0),), label="eg")
            p.op("dve", (lambda mi, tt0, tw, mw2: lambda e: e.tensor_tensor(
                Ab[mi][0:mw2, tt0:tt0 + tw], Ab[mi][0:mw2, tt0:tt0 + tw],
                scrb[0:mw2, 0:tw], ALU.mult))(mi, tt0, tw, mw2),
                reads=(("scrb", 0), ("Ab", mi)), writes=(("Ab", mi),), label="eag")
        proj_sweep(p, PW1g, wb, "wb", bwf, banks[0:4], bts[0:4], epi_g, w_m_ofs=2730)
        def epi_2(mi, tt0, tw, bank, btag):
            p.op("dve", (lambda mi, tt0, tw, bank: lambda e: e.scalar_tensor_tensor(
                yb[mi][:, 2 + tt0:2 + tt0 + tw], bank[:, 0:tw], b2s[:, mi:mi + 1],
                yb[mi][:, 2 + tt0:2 + tt0 + tw], ALU.add, ALU.add))(mi, tt0, tw, bank),
                reads=(btag, ("b2s", 0), ("yb", mi)), writes=(("yb", mi),), label="e2")
        proj_sweep(p, PW2, Ab, "Ab", bwf, banks[0:4], bts[0:4], epi_2)
        o0 = max(g0, 514 if last else 513)
        if o0 < g0 + bwf:
            if last:
                keys = store_fm(p, y2, yb, "yb", g0 + bwf - o0, dcol0=o0 - 514,
                                col0=2 + o0 - g0)
            else:
                keys = store_fm(p, yout, yb, "yb", g0 + bwf - o0, dcol0=o0 - 513,
                                col0=2 + o0 - g0)
            out_keys = keys
    if last:
        TBF = 512
        for blk in range(4096 // TBF):
            load_fm(p, x_own, yb, "yb", TBF, dcol0=blk * TBF)
            load_fm(p, y2, pf, "pf", TBF, dcol0=blk * TBF, eng="pool")
            for k in range(NCH):
                p.op("dve", (lambda k: lambda e: e.scalar_tensor_tensor(
                    yb[k][:, 0:TBF], pf[k][:, 0:TBF], 0.5, yb[k][:, 0:TBF],
                    ALU.mult, ALU.add))(k),
                    reads=(("pf", k), ("yb", k)), writes=(("yb", k),), label="finx")
            out_keys = store_fm(p, xout, yb, "yb", TBF, dcol0=blk * TBF)
    p.finalize(final_waits=out_keys)
    return nc


def l3ffn_tables(h, li, Text0=7168):
    b_i = 512 * li
    Li = Text0 - b_i
    pos = np.arange(Li, dtype=np.float64) + (h * 4096 - 3072 + b_i)
    half = np.arange(0, 64, 2, dtype=np.float64)
    inv_freq = 1.0 / 10000.0 ** (half / 64)
    fr = np.outer(inv_freq, pos)
    base = (half + 0.4 * 64) / (1.4 * 64)
    power = (pos - 4096.0) / 128.0
    scale = base[:, None] ** power[None, :]
    cos, sin = np.cos(fr), np.sin(fr)
    qC64 = np.concatenate([cos * scale, cos * scale], axis=0)
    qS64 = np.concatenate([sin * scale, sin * scale], axis=0)
    kC64 = np.concatenate([cos / scale, cos / scale], axis=0)
    kS64 = np.concatenate([sin / scale, sin / scale], axis=0)
    def dup(t):
        return np.ascontiguousarray(np.concatenate([t, t], axis=0).astype(np.float32))
    return dup(qC64), dup(qS64), dup(kC64), dup(kS64)


def l3ffn_masks(h, li):
    r = np.arange(128)[:, None]
    cc = np.arange(512)[None, :]
    NEG = -1e30
    m = np.full((4, 128, 512), NEG, np.float32)
    ok0 = (cc >= r) & (cc <= r + 256) & (cc < 384)
    ok1 = (cc >= r + 128) & (cc <= r + 384)
    m[0][ok0] = 0.0
    m[1][ok1] = 0.0
    if h == 0:
        m[2][ok0 & (cc >= 256)] = 0.0
        m[3][ok1 & (cc >= 256)] = 0.0
    else:
        m[2][ok0] = 0.0
        m[3][ok1] = 0.0
    return m


def _p32mat():
    m = np.zeros((128, 128), BF16NP)
    for b in range(2):
        o = 64 * b
        for mm in range(32):
            m[o + mm + 32, o + mm] = -1.0
        for mm in range(32, 64):
            m[o + mm - 32, o + mm] = 1.0
    return m


def _pad_chunkcols(v, nch):
    padded = np.zeros(nch * 128, np.float32)
    padded[:len(v)] = v
    return np.ascontiguousarray(padded.reshape(nch, 128).T)


def stage_l3ffn_layer(li, h, yext, inp, x_own=None):
    qC, qS, kC, kS = l3ffn_tables(h, li)
    qs = np.tile(inp["L3_ffn_q_scale"][li], 2).astype(np.float32)
    ks = np.tile(inp["L3_ffn_k_scale"][li], 2).astype(np.float32)
    d = {
        "yext": yext,
        "qkvw": np.ascontiguousarray(inp["L3_ffn_Wqkv"][li]),
        "wow": np.ascontiguousarray(inp["L3_ffn_Wo"][li]),
        "w1w": np.ascontiguousarray(inp["L3_ffn_W1"][li]),
        "w2w": np.ascontiguousarray(inp["L3_ffn_W2"][li]),
        "qsc": np.ascontiguousarray(np.stack([qs, ks], axis=1)),
        "lngb": np.concatenate([chunk_cols(inp["L3_ffn_ln_g"][li]),
                                chunk_cols(inp["L3_ffn_ln_b"][li])], axis=1),
        "b2c": chunk_cols(inp["L3_ffn_b2"][li]),
        "b1a": _pad_chunkcols(inp["L3_ffn_b1"][li][:2730], 22),
        "b1g": _pad_chunkcols(inp["L3_ffn_b1"][li][2730:], 22),
        "tabQC": qC, "tabQS": qS, "tabKC": kC, "tabKS": kS,
        "p32": _p32mat(),
        "identb": np.eye(128, dtype=BF16NP),
        "masks": l3ffn_masks(h, li),
        "hones": np.ascontiguousarray(np.stack(
            [np.concatenate([np.ones(64), np.zeros(64)]),
             np.concatenate([np.zeros(64), np.ones(64)])], axis=1).astype(np.float32)),
        "hexp": np.ascontiguousarray(np.stack(
            [np.concatenate([np.ones(64), np.zeros(64)]),
             np.concatenate([np.zeros(64), np.ones(64)])], axis=0).astype(np.float32)),
    }
    if x_own is not None:
        d["x_own"] = x_own
    return d


def run_l3ffn(x_np, inp, bass_run=run_bass_kernel_spmd):
    Text0 = 7168
    nci = build_lninit()
    ims = []
    xfm = [to_fm(x_np[b]) for b in range(4)]
    for core in range(8):
        b, h = core // 2, core % 2
        t0 = h * 4096
        xe = np.zeros((1024, Text0), np.float32)
        lo = max(0, t0 - 3072)
        xe[:, 3072 - (t0 - lo):] = xfm[b][:, lo:t0 + 4096]
        ims.append({"x_ext": xe})
    res = bass_run(nci, ims, core_ids=list(range(8)))
    ys = [None] * 8
    for core in range(8):
        y = res.results[core]["yout"]
        ye = np.zeros((1024, 2 + Text0), np.float32)
        ye[:, 2:] = y
        ys[core] = ye
    for li in range(6):
        last = li == 5
        ncl = build_l3ffn_layer(li, last=last)
        ims = []
        for core in range(8):
            b, h = core // 2, core % 2
            xo = np.ascontiguousarray(xfm[b][:, h * 4096:(h + 1) * 4096]) if last else None
            ims.append(stage_l3ffn_layer(li, h, ys[core], inp, x_own=xo))
        res = bass_run(ncl, ims, core_ids=list(range(8)))
        if not last:
            for core in range(8):
                y = res.results[core]["yout"]
                ye = np.zeros((1024, 1 + y.shape[1]), np.float32)
                ye[:, 1:] = y
                ys[core] = ye
        else:
            out = np.zeros((4, 8192, 1024), np.float32)
            for core in range(8):
                b, h = core // 2, core % 2
                out[b, h * 4096:(h + 1) * 4096] = res.results[core]["xout"].T
            return out

"""Fused megakernel: all sublayers in one program.
x stream lives in two Internal DRAM buffers (ping-pong); pair AllGather for cross-core data.
"""
import numpy as np
import concourse.bass as bass

TLOC = 4096


class Shared:
    def __init__(self, p: Prog, identb_dram):
        self.banks = [p.psum(f"pb{i}", [P, 512], F32) for i in range(7)]
        self.bts = [("pb", i) for i in range(7)]
        self.btr = p.psum("ptr", [P, 1024], BF16)
        self.btrt = ("ptr", 0)
        self.c = LNCtx(p, banks=[(self.banks[i], self.bts[i]) for i in range(4)])
        self.idb = p.sbuf("sh_idb", [P, P], BF16)
        p.op("sp", lambda e: e.dma_start(self.idb[:], identb_dram[:]), writes=(("sh_idb", 0),),
             dma="Lidb")
        self.idt = ("sh_idb", 0)


def emit_copy_dram(p, dst, dst_name, src, src_name, T, bounce, bounce_name, eng="sp"):
    for k in range(NCH):
        p.op(eng, (lambda k: lambda e: e.dma_start(
            bounce[:, 0:T], src[P * k:P * (k + 1), 0:T]))(k),
            reads=((src_name, k),), writes=((bounce_name, 0),), dma=f"cpL{dst_name}{k}")
        p.op(eng, (lambda k: lambda e: e.dma_start(
            dst[P * k:P * (k + 1), 0:T], bounce[:, 0:T]))(k),
            reads=((bounce_name, 0),), writes=((dst_name, k),), dma=f"cpS{dst_name}{k}")


def emit_ag(p, src_dram, src_name, agin, agout, nrows, T, key, fence=None,
            src_col0=0):
    nch = nrows // P
    for k in range(nch):
        p.op("pool", (lambda k: lambda e: e.dma_start(
            agin[P * k:P * (k + 1), 0:T], src_dram[P * k:P * (k + 1), src_col0:src_col0 + T]))(k),
            reads=((src_name, k),), writes=((key + "in", k),), dma=f"ag{key}{k}")
    p.op("pool", lambda e: e.collective_compute(
        "AllGather", mybir.AluOpType.bypass,
        replica_groups=[[0, 1], [2, 3], [4, 5], [6, 7]],
        ins=[agin.opt()], outs=[agout.opt()]),
        reads=tuple((key + "in", k) for k in range(nch)),
        writes=((key + "pre", 0),), label=f"AG_{key}")
    fin, fout = fence
    p.op("pool", lambda e: e.collective_compute(
        "AllReduce", mybir.AluOpType.add,
        replica_groups=[[0, 1], [2, 3], [4, 5], [6, 7]],
        ins=[fin.opt()], outs=[fout.opt()]),
        reads=((key + "pre", 0),), writes=((key + "out", 0),), label=f"AGF_{key}")



def emit_l0ffn(p, sh, d, xsrc, xsn, xdst, xdn, agout2, agt, pre, TB=1024, scale=0.5):
    c = sh.c
    p.push_scope()
    banks, bts = sh.banks[0:4], sh.bts[0:4]
    xc = [p.sbuf(f"{pre}x{k}", [P, TB + 2], F32, side="right") for k in range(NCH)]
    zc = [p.sbuf(f"{pre}z{k}", [P, TB + 2], BF16, side="right") for k in range(NCH)]
    xr = [p.sbuf(f"{pre}xr{k}", [P, TB], BF16, side="right") for k in range(NCH)]
    xk = [p.sbuf(f"{pre}xk{k}", [P, TB], BF16, side="right") for k in range(NCH)]
    rr = [p.sbuf(f"{pre}r{k}", [P, TB], F32, side="right") for k in range(NCH)]
    krel = [p.sbuf(f"{pre}kr{k}", [P, TB], BF16, side="right") for k in range(NCH)]
    kact = [p.sbuf(f"{pre}ka{k}", [P, TB], BF16, side="right") for k in range(NCH)]
    mus = p.sbuf(f"{pre}mus", [P, 32], F32, side="right")
    mtmp = p.sbuf(f"{pre}mtmp", [P, TB], F32, side="right")
    mlt = p.sbuf(f"{pre}mlt", [P, 1], F32, side="right")
    Wr = ProjWeights(p, d["l0f_wrT"], 1024, 1024, pre + "wr", side="right")
    Wk = ProjWeights(p, d["l0f_wkT"], 1024, 1024, pre + "wk", side="right")
    Wv = ProjWeights(p, d["l0f_wvT"], 1024, 1024, pre + "wv", side="right")

    p.op("sp", lambda e: e.dma_start(mus[:, 0:16], d["l0f_mu"][:, 0:16]),
         writes=((pre + "mus", 0),), dma=pre + "mus")
    p.op("sp", lambda e: e.dma_start(mlt[:], d["maskL"][:, 0:1]),
         writes=((pre + "mlt", 0),), dma=pre + "mlt")
    p.op("dve", lambda e: e.tensor_scalar(mus[:, 16:32], mus[:, 0:16], -1.0, 1.0, ALU.mult, ALU.add),
         reads=((pre + "mus", 0),), writes=((pre + "mus", 1),))

    out_keys = []
    for blk in range(TLOC // TB):
        if blk == 0:
            for k in range(NCH):
                p.op("pool", (lambda k: lambda e: e.dma_start(
                    xc[k][:, 0:2], agout2[P * k:P * (k + 1), 0:2]))(k),
                    reads=((agt, 0),), writes=((pre + "x", k),), dma=f"{pre}hl{k}")
                p.op("dve", (lambda k: lambda e: e.tensor_scalar(
                    xc[k][:, 0:2], xc[k][:, 0:2], mlt[:, 0:1], None, ALU.mult))(k),
                    reads=((pre + "x", k), (pre + "mlt", 0)), writes=((pre + "x", k),))
            load_fm(p, xsrc, xc, pre + "x", TB, dcol0=0, col0=2, rtag=xsn)
        else:
            load_fm(p, xsrc, xc, pre + "x", TB + 2, dcol0=blk * TB - 2, rtag=xsn)
        ln_fm(p, c, xc, pre + "x", TB + 2, z_chunks=zc, z_name=pre + "z")
        for k in range(NCH):
            for (dst_, dname, mcol) in ((xr, pre + "xr", 0), (xk, pre + "xk", 8)):
                p.op("dve", (lambda k, mcol: lambda e: e.tensor_scalar(
                    mtmp[:, 0:TB], zc[k][:, 1:TB + 1], mus[:, 16 + mcol + k:17 + mcol + k],
                    None, ALU.mult))(k, mcol),
                    reads=((pre + "z", k), (pre + "mus", 1)), writes=((pre + "mtmp", 0),))
                p.op("dve", (lambda k, dst_, mcol: lambda e: e.scalar_tensor_tensor(
                    dst_[k][:, 0:TB], zc[k][:, 2:TB + 2], mus[:, mcol + k:mcol + k + 1],
                    mtmp[:, 0:TB], ALU.mult, ALU.add))(k, dst_, mcol),
                    reads=((pre + "z", k), (pre + "mus", 0), (pre + "mtmp", 0)),
                    writes=((dname, k),))
        def epi_r(mi, t0, tw, bank, btag):
            p.op("act", (lambda mi, t0, tw, bank: lambda e: e.activation(
                rr[mi][:, t0:t0 + tw], bank[:, 0:tw], AF.Sigmoid))(mi, t0, tw, bank),
                reads=(btag,), writes=((pre + "r", mi),))
        proj_sweep(p, Wr, xr, pre + "xr", TB, banks, bts, epi_r)
        def epi_k(mi, t0, tw, bank, btag):
            p.op("act", (lambda mi, t0, tw, bank: lambda e: e.activation(
                krel[mi][:, t0:t0 + tw], bank[:, 0:tw], AF.Relu))(mi, t0, tw, bank),
                reads=(btag,), writes=((pre + "kr", mi),))
        proj_sweep(p, Wk, xk, pre + "xk", TB, banks, bts, epi_k)
        for k in range(NCH):
            p.op("pool", (lambda k: lambda e: e.tensor_mul(
                kact[k][:, 0:TB], krel[k][:, 0:TB], krel[k][:, 0:TB]))(k),
                reads=((pre + "kr", k),), writes=((pre + "ka", k),))
        def epi_v(mi, t0, tw, bank, btag):
            p.op("dve", (lambda mi, t0, tw, bank: lambda e: e.tensor_tensor(
                mtmp[:, 0:tw], rr[mi][:, t0:t0 + tw], bank[:, 0:tw], ALU.mult))(mi, t0, tw, bank),
                reads=((pre + "r", mi), btag), writes=((pre + "mtmp", 0),))
            p.op("dve", (lambda mi, t0, tw: lambda e: e.scalar_tensor_tensor(
                xc[mi][:, 2 + t0:2 + t0 + tw], mtmp[:, 0:tw], scale,
                xc[mi][:, 2 + t0:2 + t0 + tw], ALU.mult, ALU.add))(mi, t0, tw),
                reads=((pre + "mtmp", 0), (pre + "x", mi)), writes=((pre + "x", mi),))
        proj_sweep(p, Wv, kact, pre + "ka", TB, banks, bts, epi_v)
        out_keys += store_fm(p, xdst, xc, pre + "x", TB, dcol0=blk * TB, col0=2, wtag=xdn)
    p.pop_scope()
    return out_keys



def emit_l0attn(p, sh, d, xsrc, xsn, xdst, xdn, agout5, agt, pre, HALO=512, phases=("a1", "a2", "b")):
    Tloc, Text = TLOC, HALO + TLOC
    c = sh.c
    p.push_scope()
    banks, bts, btr, btrt = sh.banks, sh.bts, sh.btr, sh.btrt
    idb, idt = sh.idb, sh.idt

    Qr = [p.sbuf(f"{pre}Qr{k}", [P, Tloc], BF16, side="right") for k in range(NCH)]
    Kr = [p.sbuf(f"{pre}Kr{k}", [P, Text], BF16, side="right") for k in range(2)]
    Vtm = p.sbuf(f"{pre}Vtm", [P, (Text // P) * 256], BF16, side="right")
    TBL = 512
    xc = [p.sbuf(f"{pre}x{k}", [P, TBL], F32, side="right") for k in range(NCH)]
    zb = [p.sbuf(f"{pre}z{k}", [P, TBL], BF16, side="right") for k in range(NCH)]
    pf = [p.sbuf(f"{pre}pf{k}", [P, TBL], F32, side="right") for k in range(NCH)]
    vf = [p.sbuf(f"{pre}vf{k}", [P, TBL], BF16, side="right") for k in range(2)]
    tabC = [p.sbuf(f"{pre}tC{i}", [P, 512], F32, side="right") for i in range(2)]
    tabS = [p.sbuf(f"{pre}tS{i}", [P, 512], F32, side="right") for i in range(2)]
    scr = p.sbuf(f"{pre}scr", [P, 512], F32, side="right")
    scrb = p.sbuf(f"{pre}scrb", [P, 512], BF16, side="right")
    gqs = p.sbuf(f"{pre}gqs", [P, 2], F32, side="right")
    gks = p.sbuf(f"{pre}gks", [P, 2], F32, side="right")
    psw = p.sbuf(f"{pre}psw", [P, P], BF16, side="right")
    msks = p.sbuf(f"{pre}msks", [P, 5 * 640], F32, side="right")
    P1 = p.sbuf(f"{pre}P1", [P, 512], BF16, side="right")
    P2 = p.sbuf(f"{pre}P2", [P, 128], BF16, side="right")
    PT = p.sbuf(f"{pre}PT", [P, 640], BF16, side="right")
    rws = p.sbuf(f"{pre}rws", [P, 8], F32, side="right")
    Ofm = [p.sbuf(f"{pre}Of{k}", [P, 512], BF16, side="right") for k in range(NCH)]
    epsr = p.sbuf(f"{pre}epsr", [1, 1], F32, side="right")
    mlt = p.sbuf(f"{pre}mlt", [P, 1], F32, side="right")
    p.op("pool", lambda e: e.memset(epsr[:], 1e-6), writes=((pre + "epsr", 0),))
    PWq = ProjWeights(p, d["l0a_wqT"], 1024, 1024, pre + "wq", side="right")
    PWk = ProjWeights(p, d["l0a_wkT"], 1024, 256, pre + "wk", side="right")
    PWv = ProjWeights(p, d["l0a_wvT"], 1024, 256, pre + "wv", side="right")
    PWo = ProjWeights(p, d["l0a_woT"], 1024, 1024, pre + "wo", side="right")

    for nm, dst_, src_ in ((pre + "gq", gqs, d["l0a_qg"]), (pre + "gk", gks, d["l0a_kg"]),
                           (pre + "psw", psw, d["l0a_pswap"]), (pre + "mlt", mlt, d["maskL"])):
        p.op("sp", (lambda dst_, src_: lambda e: e.dma_start(dst_[:], src_[:, 0:dst_.shape[1]] if False else src_[:]))(dst_, src_),
             writes=((nm, 0),), dma="L" + nm)
    msrc = d["l0a_masks"].rearrange("s q m -> q s m")
    p.op("sp", lambda e: e.dma_start(msks[:, :].rearrange("q (s m) -> q s m", m=640), msrc),
         writes=((pre + "msks", 0),), dma=pre + "msk")

    ropeC, ropeS = d["l0a_ropeC"], d["l0a_ropeS"]

    nblk = Text // TBL
    for blk in range(nblk):
        b0 = blk * TBL
        if blk == 0:
            for k in range(NCH):
                p.op("pool", (lambda k: lambda e: e.dma_start(
                    xc[k][:, 0:TBL], agout5[P * k:P * (k + 1), 0:HALO]))(k),
                    reads=((agt, 0),), writes=((pre + "x", k),), dma=f"{pre}hx{k}")
                p.op("dve", (lambda k: lambda e: e.tensor_scalar(
                    xc[k][:, 0:TBL], xc[k][:, 0:TBL], mlt[:, 0:1], None, ALU.mult))(k),
                    reads=((pre + "x", k), (pre + "mlt", 0)), writes=((pre + "x", k),))
        else:
            load_fm(p, xsrc, xc, pre + "x", TBL, dcol0=b0 - HALO, rtag=xsn)
        ln_fm(p, c, xc, pre + "x", TBL, z_chunks=zb, z_name=pre + "z")
        def epi_kv(mi, t0, tw, bank, btag):
            p.op("dve", (lambda mi, t0, tw, bank: lambda e: e.tensor_copy(
                pf[mi][:, t0:t0 + tw], bank[:, 0:tw]))(mi, t0, tw, bank),
                reads=(btag,), writes=((pre + "pf", mi),))
        proj_sweep(p, PWk, zb, pre + "z", TBL, banks[0:4], bts[0:4], epi_kv)
        rms_rope_fm(p, c, pf, pre + "pf", Kr, pre + "Kr", 2, TBL, gks, pre + "gk", 0,
                    tabC, tabS, pre + "tb", psw, idb, banks[4], bts[4], banks[5], bts[5],
                    banks[6], bts[6], scr, scrb, 0, ropeC, ropeS, epsr, (pre + "epsr", 0),
                    dcol0=b0, dstcol0=b0)
        def epi_v(mi, t0, tw, bank, btag):
            p.op("dve", (lambda mi, t0, tw, bank: lambda e: e.tensor_copy(
                vf[mi][:, t0:t0 + tw], bank[:, 0:tw]))(mi, t0, tw, bank),
                reads=(btag,), writes=((pre + "vf", mi),))
        proj_sweep(p, PWv, zb, pre + "z", TBL, banks[0:4], bts[0:4], epi_v)
        for j in range(TBL // P):
            for cvd in range(2):
                slot = (2 * j + cvd) % 8
                p.op("pe", (lambda j, cvd, slot: lambda e: e.transpose(
                    btr[:, slot * P:(slot + 1) * P], vf[cvd][:, j * P:(j + 1) * P], idb[:]))(j, cvd, slot),
                    reads=((pre + "vf", cvd), idt), writes=(btrt,))
                jj = (b0 // P) + j
                p.op("dve", (lambda jj, cvd, slot: lambda e: e.tensor_copy(
                    Vtm[:, jj * 256 + cvd * P:jj * 256 + (cvd + 1) * P],
                    btr[:, slot * P:(slot + 1) * P]))(jj, cvd, slot),
                    reads=(btrt,), writes=((pre + "vtm", jj),))
    for blk in (range(Tloc // TBL) if "a2" in phases else []):
        b0 = blk * TBL
        load_fm(p, xsrc, xc, pre + "x", TBL, dcol0=b0, rtag=xsn)
        ln_fm(p, c, xc, pre + "x", TBL, z_chunks=zb, z_name=pre + "z")
        def epi_q(mi, t0, tw, bank, btag):
            p.op("dve", (lambda mi, t0, tw, bank: lambda e: e.tensor_copy(
                pf[mi][:, t0:t0 + tw], bank[:, 0:tw]))(mi, t0, tw, bank),
                reads=(btag,), writes=((pre + "pf", mi),))
        proj_sweep(p, PWq, zb, pre + "z", TBL, banks[0:4], bts[0:4], epi_q)
        rms_rope_fm(p, c, pf, pre + "pf", Qr, pre + "Qr", NCH, TBL, gqs, pre + "gq", 0,
                    tabC, tabS, pre + "tb", psw, idb, banks[4], bts[4], banks[5], bts[5],
                    banks[6], bts[6], scr, scrb, HALO, ropeC, ropeS, epsr, (pre + "epsr", 0),
                    dcol0=b0, dstcol0=b0)
    QB = 512
    out_keys = []
    for qb in (range(Tloc // QB) if "b" in phases else []):
        for ti in range(QB // P):
            qt = qb * (QB // P) + ti
            win0 = P * qt
            slot = min(qt, 4)
            for h in range(4):
                sA, sAt = banks[0 + (h % 2) * 2], bts[0 + (h % 2) * 2]
                sB, sBt = banks[1 + (h % 2) * 2], bts[1 + (h % 2) * 2]
                for ci in range(2):
                    ch = 2 * h + ci
                    p.op("pe", (lambda ch, ci, qt, win0, sA: lambda e: e.matmul(
                        sA[:, 0:512], Qr[ch][:, qt * P:(qt + 1) * P], Kr[ci][:, win0:win0 + 512],
                        start=(ci == 0), stop=(ci == 1)))(ch, ci, qt, win0, sA),
                        reads=((pre + "Qr", ch), (pre + "Kr", ci)), writes=(sAt,))
                    p.op("pe", (lambda ch, ci, qt, win0, sB: lambda e: e.matmul(
                        sB[:, 0:128], Qr[ch][:, qt * P:(qt + 1) * P],
                        Kr[ci][:, win0 + 512:win0 + 640],
                        start=(ci == 0), stop=(ci == 1)))(ch, ci, qt, win0, sB),
                        reads=((pre + "Qr", ch), (pre + "Kr", ci)), writes=(sBt,))
                p.op("dve", (lambda sA, slot: lambda e: e.scalar_tensor_tensor(
                    sA[:, 0:512], sA[:, 0:512], 1.0 / 16.0, msks[:, slot * 640:slot * 640 + 512],
                    ALU.mult, ALU.add))(sA, slot),
                    reads=(sAt, (pre + "msks", 0)), writes=(sAt,))
                p.op("dve", (lambda sB, slot: lambda e: e.scalar_tensor_tensor(
                    sB[:, 0:128], sB[:, 0:128], 1.0 / 16.0,
                    msks[:, slot * 640 + 512:slot * 640 + 640], ALU.mult, ALU.add))(sB, slot),
                    reads=(sBt, (pre + "msks", 0)), writes=(sBt,))
                p.op("dve", (lambda sA: lambda e: e.reduce_max(
                    rws[:, 0:1], sA[:, 0:512], axis=mybir.AxisListType.X))(sA),
                    reads=(sAt,), writes=((pre + "rws", 0),))
                p.op("dve", (lambda sB: lambda e: e.reduce_max(
                    rws[:, 1:2], sB[:, 0:128], axis=mybir.AxisListType.X))(sB),
                    reads=(sBt,), writes=((pre + "rws", 1),))
                p.op("dve", lambda e: e.tensor_tensor(rws[:, 2:3], rws[:, 0:1], rws[:, 1:2], ALU.max),
                     reads=((pre + "rws", 0), (pre + "rws", 1)), writes=((pre + "rws", 2),))
                p.op("dve", lambda e: e.tensor_scalar(rws[:, 3:4], rws[:, 2:3], -1.0, None, ALU.mult),
                     reads=((pre + "rws", 2),), writes=((pre + "rws", 3),))
                p.op("act", (lambda sA: lambda e: e.activation(
                    P1[:, 0:512], sA[:, 0:512], AF.Exp, bias=rws[:, 3:4], accum_out=rws[:, 4:5]))(sA),
                    reads=(sAt, (pre + "rws", 3)), writes=((pre + "P1", 0), (pre + "rws", 4)))
                p.op("act", (lambda sB: lambda e: e.activation(
                    P2[:, 0:128], sB[:, 0:128], AF.Exp, bias=rws[:, 3:4], accum_out=rws[:, 5:6]))(sB),
                    reads=(sBt, (pre + "rws", 3)), writes=((pre + "P2", 0), (pre + "rws", 5)))
                p.op("dve", lambda e: e.tensor_tensor(rws[:, 6:7], rws[:, 4:5], rws[:, 5:6], ALU.add),
                     reads=((pre + "rws", 4), (pre + "rws", 5)), writes=((pre + "rws", 6),))
                p.op("dve", lambda e: e.reciprocal(rws[:, 7:8], rws[:, 6:7]),
                     reads=((pre + "rws", 6),), writes=((pre + "rws", 7),))
                p.op("dve", lambda e: e.tensor_scalar(P1[:, 0:512], P1[:, 0:512], rws[:, 7:8], None, ALU.mult),
                     reads=((pre + "P1", 0), (pre + "rws", 7)), writes=((pre + "P1", 0),))
                p.op("dve", lambda e: e.tensor_scalar(P2[:, 0:128], P2[:, 0:128], rws[:, 7:8], None, ALU.mult),
                     reads=((pre + "P2", 0), (pre + "rws", 7)), writes=((pre + "P2", 0),))
                for j in range(5):
                    srcP = P1 if j < 4 else P2
                    scol = j * P if j < 4 else 0
                    ptag = (pre + "P1", 0) if j < 4 else (pre + "P2", 0)
                    p.op("pe", (lambda j, srcP, scol: lambda e: e.transpose(
                        btr[:, j * P:(j + 1) * P], srcP[:, scol:scol + P], idb[:]))(j, srcP, scol),
                        reads=(ptag, idt), writes=(btrt,))
                    p.op("dve", (lambda j: lambda e: e.tensor_copy(
                        PT[:, j * P:(j + 1) * P], btr[:, j * P:(j + 1) * P]))(j),
                        reads=(btrt,), writes=((pre + "PT", j),))
                for cvd in range(2):
                    ob, obt = banks[4 + cvd], bts[4 + cvd]
                    for j in range(5):
                        jj = qt + j
                        p.op("pe", (lambda jj, cvd, j, ob: lambda e: e.matmul(
                            ob[:, 0:128], Vtm[:, jj * 256 + cvd * P:jj * 256 + (cvd + 1) * P],
                            PT[:, j * P:(j + 1) * P], start=(j == 0), stop=(j == 4)))(jj, cvd, j, ob),
                            reads=((pre + "vtm", jj), (pre + "PT", j)), writes=(obt,))
                    p.op("act", (lambda h, cvd, ti, ob: lambda e: e.activation(
                        Ofm[2 * h + cvd][:, ti * P:(ti + 1) * P], ob[:, 0:128], AF.Copy))(h, cvd, ti, ob),
                        reads=(obt,), writes=((pre + "Of", 2 * h + cvd),))
        load_fm(p, xsrc, xc, pre + "x", QB, dcol0=qb * QB, rtag=xsn)
        def epi_o(mi, t0, tw, bank, btag):
            p.op("dve", (lambda mi, t0, tw, bank: lambda e: e.tensor_tensor(
                xc[mi][:, t0:t0 + tw], xc[mi][:, t0:t0 + tw], bank[:, 0:tw], ALU.add))(mi, t0, tw, bank),
                reads=(btag, (pre + "x", mi)), writes=((pre + "x", mi),))
        proj_sweep(p, PWo, Ofm, pre + "Of", QB, banks[0:4], bts[0:4], epi_o)
        out_keys += store_fm(p, xdst, xc, pre + "x", QB, dcol0=qb * QB, wtag=xdn)
    if "keep" not in phases:
        p.pop_scope()
    return out_keys



def build_fused(nparts=3):
    nc = bass.Bass("TRN2", target_bir_lowering=False)
    F = F32
    dr = {}
    def di(name, shape, dtype=F32):
        dr[name] = nc.dram_tensor(name, list(shape), dtype, kind="ExternalInput").ap()
        return dr[name]
    x0 = di("x0", [1024, TLOC])
    xout = nc.dram_tensor("xout", [1024, TLOC], F32, kind="ExternalOutput").ap()
    xA = nc.dram_tensor("xA", [1024, TLOC], F32, kind="Internal").ap()
    xB = nc.dram_tensor("xB", [1024, TLOC], F32, kind="Internal").ap()
    ag2in = nc.dram_tensor("ag2in", [1024, 2], F32, kind="Internal").ap()
    ag2out = nc.dram_tensor("ag2out", [2048, 2], F32, kind="Internal").ap()
    ag5in = nc.dram_tensor("ag5in", [1024, 512], F32, kind="Internal").ap()
    ag5out = nc.dram_tensor("ag5out", [2048, 512], F32, kind="Internal").ap()
    fin = nc.dram_tensor("fencein", [2, 2], F32, kind="Internal").ap()
    fout = nc.dram_tensor("fenceout", [2, 2], F32, kind="Internal").ap()
    di("identb", [P, P], BF16)
    di("maskL", [P, 1])
    di("l0f_wrT", [1024, 1024]); di("l0f_wkT", [1024, 1024]); di("l0f_wvT", [1024, 1024])
    di("l0f_mu", [P, 16])
    di("l0a_wqT", [1024, 1024]); di("l0a_wkT", [1024, 256]); di("l0a_wvT", [1024, 256])
    di("l0a_woT", [1024, 1024]); di("l0a_qg", [P, 2]); di("l0a_kg", [P, 2])
    di("l0a_ropeC", [256, 512 + TLOC]); di("l0a_ropeS", [256, 512 + TLOC])
    di("l0a_pswap", [P, P], BF16); di("l0a_masks", [5, P, 640])

    p = Prog(nc)
    sh = Shared(p, dr["identb"])

    chain = []
    full = [("l0f", "aa"), ("l0a", "ab"), ("l0f", "ac")]
    parts = full[:nparts]
    srcs = [x0] + [xA if i % 2 == 0 else xB for i in range(len(parts))]
    names = ["x0"] + ["xA" if i % 2 == 0 else "xB" for i in range(len(parts))]
    out_keys = []
    agn = [0]
    for i, (kind, pre) in enumerate(parts):
        xsrc, xsn = srcs[i], names[i]
        last = i == len(parts) - 1
        xdst = xout if last else srcs[i + 1]
        xdn = "xout" if last else names[i + 1]
        gk = f"g{agn[0]}"; agn[0] += 1
        if kind == "l0f":
            emit_ag(p, xsrc, xsn, ag2in, ag2out, 1024, 2, gk, fence=(fin, fout), src_col0=TLOC - 2)
            out_keys = emit_l0ffn(p, sh, dr, xsrc, xsn, xdst, xdn, ag2out, gk + "out", pre + "_")
        elif kind == "l0a":
            emit_ag(p, xsrc, xsn, ag5in, ag5out, 1024, 512, gk, fence=(fin, fout), src_col0=TLOC - 512)
            out_keys = emit_l0attn(p, sh, dr, xsrc, xsn, xdst, xdn, ag5out, gk + "out", pre + "_")
        p.phase()
    p.finalize(final_waits=out_keys)
    return nc


def stage_fused(inp, xfm_b, h):
    a = stage_l0attn(np.zeros((1024, 8192), np.float32), h, inp)
    d = {
        "x0": np.ascontiguousarray(xfm_b[:, h * TLOC:(h + 1) * TLOC]),
        "identb": np.eye(128, dtype=BF16NP),
        "maskL": np.full((128, 1), 1.0 if h == 1 else 0.0, np.float32),
        "l0f_wrT": wT(inp["L0_ffn_W_r"]), "l0f_wkT": wT(inp["L0_ffn_W_k"]),
        "l0f_wvT": wT(inp["L0_ffn_W_v"]),
        "l0f_mu": np.ascontiguousarray(np.concatenate(
            [chunk_cols(inp["L0_ffn_mu_r"]), chunk_cols(inp["L0_ffn_mu_k"])], axis=1).astype(np.float32)),
        "l0a_wqT": a["wqT"], "l0a_wkT": a["wkT"], "l0a_wvT": a["wvT"], "l0a_woT": a["woT"],
        "l0a_qg": a["qg"], "l0a_kg": a["kg"],
        "l0a_ropeC": a["ropeC"], "l0a_ropeS": a["ropeS"],
        "l0a_pswap": a["pswap"], "l0a_masks": a["masks"],
    }
    return d



def emit_l1ffn(p, sh, d, xsrc, xsn, xdst, xdn, pre, TB=512, scale=0.5, E=16, I=512):
    c = sh.c
    p.push_scope()
    banks, bts = sh.banks, sh.bts
    xc = [p.sbuf(f"{pre}x{k}", [P, TB], F32, side="right") for k in range(NCH)]
    zb = [p.sbuf(f"{pre}z{k}", [P, TB], BF16, side="right") for k in range(NCH)]
    zf = [p.sbuf(f"{pre}zf{k}", [P, TB], F32, side="right") for k in range(NCH)]
    gs = [p.sbuf(f"{pre}gs{k}", [P, TB], BF16, side="right") for k in range(4)]
    us = [p.sbuf(f"{pre}us{k}", [P, TB], BF16, side="right") for k in range(4)]
    actb = [[p.sbuf(f"{pre}act{e}_{k}", [P, TB], BF16, side="right") for k in range(4)] for e in range(E)]
    gwsb = p.sbuf(f"{pre}gw", [P, NCH * E], F32, side="right")
    ebs = p.sbuf(f"{pre}ebs", [P, E], F32, side="right")
    iot = p.sbuf(f"{pre}iot", [P, E], F32, side="right")
    idf = p.sbuf(f"{pre}idf", [P, P], F32, side="right")
    sig = p.sbuf(f"{pre}sig", [P, E], F32, side="right")
    sfc = p.sbuf(f"{pre}sfc", [P, E], F32, side="right")
    tmp16 = p.sbuf(f"{pre}tmp16", [P, E], F32, side="right")
    m1s = p.sbuf(f"{pre}m1s", [P, 4], F32, side="right")
    gsc4 = p.sbuf(f"{pre}gsc4", [P, 4], F32, side="right")
    gsc16 = p.sbuf(f"{pre}gsc16", [P, E], F32, side="right")
    grow = p.sbuf(f"{pre}grow", [P, 4], F32, side="right")
    ge4 = p.sbuf(f"{pre}ge4", [P, 4], F32, side="right")
    msk4 = p.sbuf(f"{pre}msk4", [P, 4], F32, side="right")
    oh = p.sbuf(f"{pre}oh", [P, E], F32, side="right")
    wsel = p.sbuf(f"{pre}wsel", [P, E], F32, side="right")
    wk = p.sbuf(f"{pre}wk", [P, 1], F32, side="right")
    mx8 = p.sbuf(f"{pre}mx8", [P, 8], F32, side="right")
    ix8 = p.sbuf(f"{pre}ix8", [P, 8], mybir.dt.uint32, side="right")
    ixf = p.sbuf(f"{pre}ixf", [P, 8], F32, side="right")
    cmb = p.sbuf(f"{pre}cmb", [P, E], F32, side="right")
    ones4 = p.sbuf(f"{pre}ones4", [P, 4], F32, side="right")
    comb_fm = p.sbuf(f"{pre}cfm", [E, TB], F32, side="right")
    selsb = p.sbuf(f"{pre}sel", [E, E * P], F32, side="right")
    mtmp = p.sbuf(f"{pre}mtmp", [P, TB], F32, side="right")
    PWg = ProjWeights(p, d["moe_gpT"][0], 1024, I, pre + "gp", nbuf=2, max_mw=I, side="right")
    PWu = ProjWeights(p, d["moe_upT"][0], 1024, I, pre + "up", nbuf=2, max_mw=I, side="right")
    PWd = ProjWeights(p, d["moe_dpT"][0], I, 1024, pre + "dp", nbuf=2, max_mw=1024, side="right")

    p.op("pool", lambda e: e.memset(ones4[:], 1.0), writes=((pre + "ones4", 0),))
    p.op("sp", lambda e: e.dma_start(selsb[:], d["moe_selm"][:]), writes=((pre + "sel", 0),), dma=pre + "sel")
    src = d["moe_gwT"].rearrange("(kc q) m -> q kc m", q=P)
    dstv = gwsb[:, :].rearrange("q (kc m) -> q kc m", m=E)
    p.op("sp", (lambda dstv, src: lambda e: e.dma_start(dstv, src))(dstv, src),
         writes=((pre + "gw", 0),), dma=pre + "gwL")
    for nm, dst_, sr in ((pre + "ebs", ebs, d["moe_ebias"]), (pre + "iot", iot, d["moe_iota"]),
                         (pre + "idf", idf, d["identf"])):
        p.op("sp", (lambda dst_, sr: lambda e: e.dma_start(dst_[:], sr[:]))(dst_, sr),
             writes=((nm, 0),), dma="L" + nm)

    out_keys = []
    for blk in range(TLOC // TB):
        load_fm(p, xsrc, xc, pre + "x", TB, dcol0=blk * TB, rtag=xsn)
        ln_fm(p, c, xc, pre + "x", TB, z_chunks=zb, z_name=pre + "z", zf_chunks=zf, zf_name=pre + "zf")
        for ti in range(TB // P):
            t0 = ti * P
            lbank, lbt = banks[6], bts[6]
            for kc in range(NCH):
                p.op("pe", (lambda kc, t0: lambda e: e.matmul(
                    lbank[:, 0:E], zf[kc][:, t0:t0 + P], gwsb[:, kc * E:(kc + 1) * E],
                    start=(kc == 0), stop=(kc == NCH - 1)))(kc, t0),
                    reads=((pre + "zf", kc), (pre + "gw", 0)), writes=(lbt,))
            p.op("act", lambda e: e.activation(sig[:], lbank[:, 0:E], AF.Sigmoid),
                 reads=(lbt,), writes=((pre + "sig", 0),))
            p.op("dve", lambda e: e.tensor_tensor(sfc[:], sig[:], ebs[:], ALU.add),
                 reads=((pre + "sig", 0), (pre + "ebs", 0)), writes=((pre + "sfc", 0),))
            for g in range(4):
                sl = (4 * g, 4 * g + 4)
                p.op("dve", (lambda g, sl: lambda e: e.reduce_max(
                    m1s[:, g:g + 1], sfc[:, sl[0]:sl[1]], axis=mybir.AxisListType.X))(g, sl),
                    reads=((pre + "sfc", 0),), writes=((pre + "m1s", g),))
                p.op("dve", (lambda g, sl: lambda e: e.tensor_scalar(
                    ge4[:], sfc[:, sl[0]:sl[1]], m1s[:, g:g + 1], None, ALU.is_ge))(g, sl),
                    reads=((pre + "sfc", 0), (pre + "m1s", g)), writes=((pre + "ge4", 0),))
                p.op("dve", (lambda g, sl: lambda e: e.scalar_tensor_tensor(
                    msk4[:], ge4[:], -1e30, sfc[:, sl[0]:sl[1]], ALU.mult, ALU.add))(g, sl),
                    reads=((pre + "ge4", 0), (pre + "sfc", 0)), writes=((pre + "msk4", 0),))
                p.op("dve", (lambda g: lambda e: e.reduce_max(
                    gsc4[:, g:g + 1], msk4[:], axis=mybir.AxisListType.X))(g),
                    reads=((pre + "msk4", 0),), writes=((pre + "gsc4", g),))
                p.op("dve", (lambda g: lambda e: e.tensor_tensor(
                    gsc4[:, g:g + 1], gsc4[:, g:g + 1], m1s[:, g:g + 1], ALU.add))(g),
                    reads=((pre + "gsc4", g), (pre + "m1s", g)), writes=((pre + "gsc4", g),))
            p.op("dve", lambda e: e.reduce_max(grow[:, 0:1], gsc4[:], axis=mybir.AxisListType.X),
                 reads=tuple((pre + "gsc4", g) for g in range(4)), writes=((pre + "grow", 0),))
            p.op("dve", lambda e: e.tensor_scalar(ge4[:], gsc4[:], grow[:, 0:1], None, ALU.is_ge),
                 reads=tuple((pre + "gsc4", g) for g in range(4)) + ((pre + "grow", 0),),
                 writes=((pre + "ge4", 0),))
            p.op("dve", lambda e: e.scalar_tensor_tensor(msk4[:], ge4[:], -1e30, gsc4[:], ALU.mult, ALU.add),
                 reads=((pre + "ge4", 0),) + tuple((pre + "gsc4", g) for g in range(4)),
                 writes=((pre + "msk4", 0),))
            p.op("dve", lambda e: e.reduce_max(grow[:, 1:2], msk4[:], axis=mybir.AxisListType.X),
                 reads=((pre + "msk4", 0),), writes=((pre + "grow", 1),))
            for g in range(4):
                p.op("dve", (lambda g: lambda e: e.tensor_scalar(
                    gsc16[:, 4 * g:4 * g + 4], ones4[:], gsc4[:, g:g + 1], None, ALU.mult))(g),
                    reads=((pre + "ones4", 0), (pre + "gsc4", g)), writes=((pre + "gsc16", g),))
            p.op("dve", lambda e: e.tensor_scalar(gsc16[:], gsc16[:], grow[:, 1:2], None, ALU.is_ge),
                 reads=tuple((pre + "gsc16", g) for g in range(4)) + ((pre + "grow", 1),),
                 writes=tuple((pre + "gsc16", g) for g in range(4)))
            p.op("dve", lambda e: e.tensor_tensor(tmp16[:], gsc16[:], sfc[:], ALU.mult),
                 reads=tuple((pre + "gsc16", g) for g in range(4)) + ((pre + "sfc", 0),),
                 writes=((pre + "tmp16", 0),))
            p.op("dve", lambda e: e.max(mx8[:], tmp16[:]),
                 reads=((pre + "tmp16", 0),), writes=((pre + "mx8", 0),))
            p.op("dve", lambda e: e.max_index(ix8[:], mx8[:], tmp16[:]),
                 reads=((pre + "tmp16", 0), (pre + "mx8", 0)), writes=((pre + "ix8", 0),))
            p.op("dve", lambda e: e.tensor_copy(ixf[:], ix8[:]),
                 reads=((pre + "ix8", 0),), writes=((pre + "ixf", 0),))
            p.op("pool", lambda e: e.memset(cmb[:], 0.0), writes=((pre + "cmb", 0),))
            p.op("pool", lambda e: e.memset(grow[:, 2:3], 1e-20), writes=((pre + "grow", 2),))
            for j in range(4):
                p.op("dve", (lambda j: lambda e: e.tensor_scalar(
                    oh[:], iot[:], ixf[:, j:j + 1], None, ALU.is_equal))(j),
                    reads=((pre + "iot", 0), (pre + "ixf", 0)), writes=((pre + "oh", 0),))
                p.op("dve", lambda e: e.tensor_tensor(wsel[:], oh[:], sig[:], ALU.mult),
                     reads=((pre + "oh", 0), (pre + "sig", 0)), writes=((pre + "wsel", 0),))
                p.op("dve", lambda e: e.reduce_sum(wk[:], wsel[:], axis=mybir.AxisListType.X),
                     reads=((pre + "wsel", 0),), writes=((pre + "wk", 0),))
                p.op("dve", lambda e: e.scalar_tensor_tensor(cmb[:], oh[:], wk[:, 0:1], cmb[:], ALU.mult, ALU.add),
                     reads=((pre + "oh", 0), (pre + "wk", 0), (pre + "cmb", 0)), writes=((pre + "cmb", 0),))
                p.op("dve", lambda e: e.tensor_tensor(grow[:, 2:3], grow[:, 2:3], wk[:], ALU.add),
                     reads=((pre + "wk", 0), (pre + "grow", 2)), writes=((pre + "grow", 2),))
            p.op("dve", lambda e: e.reciprocal(grow[:, 3:4], grow[:, 2:3]),
                 reads=((pre + "grow", 2),), writes=((pre + "grow", 3),))
            p.op("dve", lambda e: e.tensor_scalar(cmb[:], cmb[:], grow[:, 3:4], None, ALU.mult),
                 reads=((pre + "cmb", 0), (pre + "grow", 3)), writes=((pre + "cmb", 0),))
            tb_, tbt = banks[5], bts[5]
            p.op("pe", (lambda t0: lambda e: e.transpose(
                tb_[0:E, 0:P], cmb[:], idf[:]))(t0),
                reads=((pre + "cmb", 0), (pre + "idf", 0)), writes=(tbt,))
            p.op("dve", (lambda t0: lambda e: e.tensor_copy(
                comb_fm[0:E, t0:t0 + P], tb_[0:E, 0:P]))(t0),
                reads=(tbt,), writes=((pre + "cfm", ti),))
        for e_ in range(E):
            bg, tg, _ = PWg.load(0, m_width=I, dram=d["moe_gpT"][e_])
            bu, tu, _ = PWu.load(0, m_width=I, dram=d["moe_upT"][e_])
            cbank, cbt = banks[4], bts[4]
            p.op("pe", (lambda e_: lambda e: e.matmul(
                cbank[:, 0:TB], selsb[0:E, e_ * P:(e_ + 1) * P], comb_fm[0:E, 0:TB],
                start=True, stop=True))(e_),
                reads=tuple((pre + "cfm", ti) for ti in range(TB // P)) + ((pre + "sel", 0),),
                writes=(cbt,))
            for m in range(4):
                bank, bt = banks[m], bts[m]
                for kc in range(NCH):
                    p.op("pe", (lambda bg, kc, m, bank: lambda e: e.matmul(
                        bank[:, 0:TB], PWg.tile2(bg, kc, m * P, P, I), zb[kc][:, 0:TB],
                        start=(kc == 0), stop=(kc == NCH - 1)))(bg, kc, m, bank),
                        reads=(tg, (pre + "z", kc)), writes=(bt,))
                p.op("act", (lambda m, bank: lambda e: e.activation(
                    gs[m][:, 0:TB], bank[:, 0:TB], AF.Silu))(m, bank),
                    reads=(bt,), writes=((pre + "gs", m),))
                for kc in range(NCH):
                    p.op("pe", (lambda bu, kc, m, bank: lambda e: e.matmul(
                        bank[:, 0:TB], PWu.tile2(bu, kc, m * P, P, I), zb[kc][:, 0:TB],
                        start=(kc == 0), stop=(kc == NCH - 1)))(bu, kc, m, bank),
                        reads=(tu, (pre + "z", kc)), writes=(bt,))
                p.op("dve", (lambda m, bank: lambda e: e.tensor_copy(
                    us[m][:, 0:TB], bank[:, 0:TB]))(m, bank),
                    reads=(bt,), writes=((pre + "us", m),))
                p.op("dve", (lambda m: lambda e: e.tensor_tensor(
                    mtmp[:, 0:TB], gs[m][:, 0:TB], us[m][:, 0:TB], ALU.mult))(m),
                    reads=((pre + "gs", m), (pre + "us", m)), writes=((pre + "mtmp", 0),))
                p.op("dve", (lambda e_, m: lambda e: e.tensor_tensor(
                    actb[e_][m][:, 0:TB], mtmp[:, 0:TB], cbank[:, 0:TB], ALU.mult))(e_, m),
                    reads=((pre + "mtmp", 0), cbt), writes=((pre + "act", e_, m),))
        for half in range(2):
            for e_ in range(E):
                bd, td, _ = PWd.load(0, m_width=1024, dram=d["moe_dpT"][e_])
                for mh in range(4):
                    m = half * 4 + mh
                    for kc in range(4):
                        p.op("pe", (lambda bd, kc, m, mh, e_: lambda e: e.matmul(
                            banks[mh][:, 0:TB], PWd.tile2(bd, kc, m * P, P, 1024),
                            actb[e_][kc][:, 0:TB],
                            start=(e_ == 0 and kc == 0), stop=(e_ == E - 1 and kc == 3)))(bd, kc, m, mh, e_),
                            reads=(td, (pre + "act", e_, kc)), writes=(bts[mh],))
            for mh in range(4):
                m = half * 4 + mh
                p.op("dve", (lambda m, mh: lambda e: e.scalar_tensor_tensor(
                    xc[m][:, 0:TB], banks[mh][:, 0:TB], scale, xc[m][:, 0:TB], ALU.mult, ALU.add))(m, mh),
                    reads=(bts[mh], (pre + "x", m)), writes=((pre + "x", m),))
        out_keys += store_fm(p, xdst, xc, pre + "x", TB, dcol0=blk * TB, wtag=xdn)
    p.pop_scope()
    return out_keys



def emit_fullattn(p, sh, d, variant, xsrc, xsn, xdst, xdn, agout, agt, Ks, Vs, pre,
                  Tfull=8192):
    c = sh.c
    p.push_scope()
    banks, bts, btr, btrt = sh.banks, sh.bts, sh.btr, sh.btrt
    idb, idt = sh.idb, sh.idt
    TBL = 512
    KB = 1024
    QBT = 8
    Tloc = TLOC
    xc = [p.sbuf(f"{pre}x{k}", [P, P * QBT], F32, side="right") for k in range(NCH)]
    zb = [p.sbuf(f"{pre}z{k}", [P, TBL], BF16, side="right") for k in range(NCH)]
    kb_ = [p.sbuf(f"{pre}kb{k}", [P, TBL], BF16, side="right") for k in range(NCH)]
    Qb = [p.sbuf(f"{pre}Qb{k}", [P, P * QBT], BF16, side="right") for k in range(NCH)]
    zloc = [p.sbuf(f"{pre}zl{k}", [P, P * QBT], BF16, side="right") for k in range(NCH)]
    Kblk = [[p.sbuf(f"{pre}K{s}_{k}", [P, KB], BF16, side="right") for k in range(NCH)] for s in range(2)]
    Vblk = [p.sbuf(f"{pre}V{s}", [P, (KB // P) * 1024], BF16, side="right") for s in range(2)]
    Ot = [p.sbuf(f"{pre}Ot{t}", [P, 1024], F32, side="right") for t in range(QBT)]
    PT = p.sbuf(f"{pre}PT", [P, KB], BF16, side="right")
    P1 = p.sbuf(f"{pre}P1", [P, KB], BF16, side="right")
    rws = p.sbuf(f"{pre}rws", [P, 16], F32, side="right")
    mrow = [p.sbuf(f"{pre}mr{t}", [P, 16], F32, side="right") for t in range(QBT)]
    bqs = p.sbuf(f"{pre}bqs", [P, 24 if variant == "l2" else 16], F32, side="right")
    macc = p.sbuf(f"{pre}macc", [P, NCH], F32, side="right")
    rtmp = p.sbuf(f"{pre}rt", [P, 1], F32, side="right")
    if variant == "l1":
        PWq = ProjWeights(p, d["l1a_wqT"], 1024, 1024, pre + "wq", side="right")
        PWk = ProjWeights(p, d["l1a_wkT"], 1024, 1024, pre + "wk", side="right")
        PWv = None
        bqv = d["l1a_bqv"]
    else:
        PWq = ProjWeights(p, d["l2a_wqkvT"], 1024, 3072, pre + "wq", side="right")
        PWk = PWq
        PWv = PWq
        bqv = d["l2a_bqv"]
    p.op("sp", (lambda bqv: lambda e: e.dma_start(bqs[:], bqv[:]))(bqv),
         writes=((pre + "bqs", 0),), dma=pre + "bq")
    p.op("pool", lambda e: e.memset(macc[:], 0.0), writes=((pre + "macc", 0),))

    for blk in range(Tfull // TBL):
        b0 = blk * TBL
        half = b0 // Tloc
        cb = b0 % Tloc
        for k in range(NCH):
            p.op("pool", (lambda k, half, cb: lambda e: e.dma_start(
                xc[k][:, 0:TBL], agout[half * 1024 + P * k:half * 1024 + P * (k + 1), cb:cb + TBL]))(k, half, cb),
                reads=((agt, 0),), writes=((pre + "x", k),), dma=f"{pre}fx{k}")
        ln_fm(p, c, xc, pre + "x", TBL, z_chunks=zb, z_name=pre + "z")
        if variant == "l1":
            for k in range(NCH):
                p.op("dve", (lambda k: lambda e: e.reduce_sum(
                    rtmp[:], zb[k][:, 0:TBL], axis=mybir.AxisListType.X))(k),
                    reads=((pre + "z", k),), writes=((pre + "rt", 0),))
                p.op("dve", (lambda k: lambda e: e.tensor_tensor(
                    macc[:, k:k + 1], macc[:, k:k + 1], rtmp[:], ALU.add))(k),
                    reads=((pre + "rt", 0), (pre + "macc", 0)), writes=((pre + "macc", 0),))
        def epi_k(mi, t0, tw, bank, btag):
            mo = mi - 8 if variant == "l2" else mi
            p.op("dve", (lambda mo, t0, tw, bank: lambda e: e.tensor_scalar(
                kb_[mo][:, t0:t0 + tw], bank[:, 0:tw], bqs[:, 8 + mo:8 + mo + 1], None, ALU.add))(mo, t0, tw, bank),
                reads=(btag, (pre + "bqs", 0)), writes=((pre + "kb", mo),))
        kcols = list(range(8, 16)) if variant == "l2" else None
        proj_sweep(p, PWk, zb, pre + "z", TBL, banks[0:4], bts[0:4], epi_k, m_chunks=kcols)
        for k in range(NCH):
            p.op("sp", (lambda k, b0: lambda e: e.dma_start(
                Ks[P * k:P * (k + 1), b0:b0 + TBL], kb_[k][:, 0:TBL]))(k, b0),
                reads=((pre + "kb", k),), dma=f"{pre}SK{k}")
        if variant == "l2":
            def epi_v(mi, t0, tw, bank, btag):
                mo = mi - 16
                p.op("dve", (lambda mo, t0, tw, bank: lambda e: e.tensor_scalar(
                    kb_[mo][:, t0:t0 + tw], bank[:, 0:tw], bqs[:, 16 + mo:16 + mo + 1], None, ALU.add))(mo, t0, tw, bank),
                    reads=(btag, (pre + "bqs", 0)), writes=((pre + "kb", mo),))
            proj_sweep(p, PWv, zb, pre + "z", TBL, banks[0:4], bts[0:4], epi_v,
                       m_chunks=list(range(16, 24)))
            vtag = pre + "kb"
            vsrc = kb_
        else:
            vtag = pre + "z"
            vsrc = zb
        for j in range(TBL // P):
            for ch in range(NCH):
                slot = ch % 8
                p.op("pe", (lambda j, ch, slot, vsrc: lambda e: e.transpose(
                    btr[:, slot * P:(slot + 1) * P], vsrc[ch][:, j * P:(j + 1) * P], idb[:]))(j, ch, slot, vsrc),
                    reads=((vtag, ch), idt), writes=(btrt,))
                p.op("dve", (lambda ch, slot: lambda e: e.tensor_copy(
                    PT[:, ch * P:(ch + 1) * P], btr[:, slot * P:(slot + 1) * P]))(ch, slot),
                    reads=(btrt,), writes=((pre + "PTx", ch),))
            p.op("sp", (lambda j, b0: lambda e: e.dma_start(
                Vs[b0 + j * P:b0 + (j + 1) * P, :], PT[:, 0:1024]))(j, b0),
                reads=tuple((pre + "PTx", ch) for ch in range(NCH)), dma=f"{pre}SV")
    if variant == "l1":
        p.op("dve", (lambda: lambda e: e.tensor_scalar(
            macc[:], macc[:], -2.0 / Tfull, None, ALU.mult))(),
            reads=((pre + "macc", 0),), writes=((pre + "macc", 0),))

    QB = P * QBT
    out_keys = []
    for qb in range(Tloc // QB):
        for sub in range(QB // TBL):
            b0 = qb * QB + sub * TBL
            for k in range(NCH):
                p.op("sp", (lambda k, b0: lambda e: e.dma_start(
                    xc[k][:, 0:TBL], xsrc[P * k:P * (k + 1), b0:b0 + TBL]))(k, b0),
                    reads=((xsn, k),), writes=((pre + "x", k),), dma=f"{pre}qx{k}")
            ln_fm(p, c, xc, pre + "x", TBL, z_chunks=zb, z_name=pre + "z")
            def epi_q(mi, t0, tw, bank, btag):
                p.op("dve", (lambda mi, t0, tw, bank, sub: lambda e: e.tensor_scalar(
                    Qb[mi][:, sub * TBL + t0:sub * TBL + t0 + tw], bank[:, 0:tw],
                    bqs[:, mi:mi + 1], None, ALU.add))(mi, t0, tw, bank, sub),
                    reads=(btag, (pre + "bqs", 0)), writes=((pre + "Qb", mi),))
            proj_sweep(p, PWq, zb, pre + "z", TBL, banks[0:4], bts[0:4], epi_q,
                       m_chunks=(list(range(0, 8)) if variant == "l2" else None))
            if variant == "l1":
                for k in range(NCH):
                    p.op("pool", (lambda k, sub: lambda e: e.tensor_copy(
                        zloc[k][:, sub * TBL:(sub + 1) * TBL], zb[k][:, 0:TBL]))(k, sub),
                        reads=((pre + "z", k),), writes=((pre + "zl", k),))
        for kblk in range(Tfull // KB):
            s = kblk % 2
            kb0 = kblk * KB
            for k in range(NCH):
                p.op("sp", (lambda s, k, kb0: lambda e: e.dma_start(
                    Kblk[s][k][:, 0:KB], Ks[P * k:P * (k + 1), kb0:kb0 + KB]))(s, k, kb0),
                    writes=((pre + "Kb", s, k),), dma=f"{pre}LK{s}_{k}")
            for j in range(KB // P):
                p.op("sp", (lambda s, j, kb0: lambda e: e.dma_start(
                    Vblk[s][:, j * 1024:(j + 1) * 1024], Vs[kb0 + j * P:kb0 + (j + 1) * P, :]))(s, j, kb0),
                    writes=((pre + "Vb", s, j),), dma=f"{pre}LV{s}")
            for t in range(QBT):
                sA, sAt = banks[0 + (t % 2) * 2], bts[0 + (t % 2) * 2]
                sB, sBt = banks[1 + (t % 2) * 2], bts[1 + (t % 2) * 2]
                for kc in range(NCH):
                    p.op("pe", (lambda t, kc, s, sA: lambda e: e.matmul(
                        sA[:, 0:512], Qb[kc][:, t * P:(t + 1) * P], Kblk[s][kc][:, 0:512],
                        start=(kc == 0), stop=(kc == NCH - 1)))(t, kc, s, sA),
                        reads=((pre + "Qb", kc), (pre + "Kb", s, kc)), writes=(sAt,))
                    p.op("pe", (lambda t, kc, s, sB: lambda e: e.matmul(
                        sB[:, 0:512], Qb[kc][:, t * P:(t + 1) * P], Kblk[s][kc][:, 512:1024],
                        start=(kc == 0), stop=(kc == NCH - 1)))(t, kc, s, sB),
                        reads=((pre + "Qb", kc), (pre + "Kb", s, kc)), writes=(sBt,))
                first = (kblk == 0)
                p.op("dve", (lambda sA: lambda e: e.reduce_max(
                    rws[:, 1:2], sA[:, 0:512], axis=mybir.AxisListType.X))(sA),
                    reads=(sAt,), writes=((pre + "rws", 1),))
                p.op("dve", (lambda sB: lambda e: e.reduce_max(
                    rws[:, 2:3], sB[:, 0:512], axis=mybir.AxisListType.X))(sB),
                    reads=(sBt,), writes=((pre + "rws", 2),))
                p.op("dve", lambda e: e.tensor_tensor(rws[:, 1:2], rws[:, 1:2], rws[:, 2:3], ALU.max),
                     reads=((pre + "rws", 1), (pre + "rws", 2)), writes=((pre + "rws", 1),))
                if first:
                    p.op("dve", (lambda t: lambda e: e.tensor_scalar(
                        mrow[t][:, 0:1], rws[:, 1:2], 1.0 / 32.0, None, ALU.mult))(t),
                        reads=((pre + "rws", 1),), writes=((pre + "mr", t, 0),))
                else:
                    p.op("dve", (lambda t: lambda e: e.tensor_scalar(
                        rws[:, 3:4], rws[:, 1:2], 1.0 / 32.0, None, ALU.mult))(t),
                        reads=((pre + "rws", 1),), writes=((pre + "rws", 3),))
                    p.op("dve", (lambda t: lambda e: e.tensor_tensor(
                        rws[:, 3:4], rws[:, 3:4], mrow[t][:, 0:1], ALU.max))(t),
                        reads=((pre + "rws", 3), (pre + "mr", t, 0)), writes=((pre + "rws", 3),))
                    p.op("dve", (lambda t: lambda e: e.tensor_scalar(
                        rws[:, 4:5], rws[:, 3:4], -1.0, None, ALU.mult))(t),
                        reads=((pre + "rws", 3),), writes=((pre + "rws", 4),))
                    p.op("act", (lambda t: lambda e: e.activation(
                        rws[:, 5:6], mrow[t][:, 0:1], AF.Exp, bias=rws[:, 4:5]))(t),
                        reads=((pre + "mr", t, 0), (pre + "rws", 4)), writes=((pre + "rws", 5),))
                    p.op("dve", (lambda t: lambda e: e.tensor_copy(
                        mrow[t][:, 0:1], rws[:, 3:4]))(t),
                        reads=((pre + "rws", 3),), writes=((pre + "mr", t, 0),))
                p.op("dve", (lambda t: lambda e: e.tensor_scalar(
                    rws[:, 4:5], mrow[t][:, 0:1], -1.0, None, ALU.mult))(t),
                    reads=((pre + "mr", t, 0),), writes=((pre + "rws", 4),))
                p.op("act", (lambda sA: lambda e: e.activation(
                    P1[:, 0:512], sA[:, 0:512], AF.Exp, bias=rws[:, 4:5], scale=1.0 / 32.0,
                    accum_out=rws[:, 6:7]))(sA),
                    reads=(sAt, (pre + "rws", 4)), writes=((pre + "P1", 0), (pre + "rws", 6)))
                p.op("act", (lambda sB: lambda e: e.activation(
                    P1[:, 512:1024], sB[:, 0:512], AF.Exp, bias=rws[:, 4:5], scale=1.0 / 32.0,
                    accum_out=rws[:, 7:8]))(sB),
                    reads=(sBt, (pre + "rws", 4)), writes=((pre + "P1", 1), (pre + "rws", 7)))
                p.op("dve", lambda e: e.tensor_tensor(rws[:, 6:7], rws[:, 6:7], rws[:, 7:8], ALU.add),
                     reads=((pre + "rws", 6), (pre + "rws", 7)), writes=((pre + "rws", 6),))
                if first:
                    p.op("dve", (lambda t: lambda e: e.tensor_copy(
                        mrow[t][:, 1:2], rws[:, 6:7]))(t),
                        reads=((pre + "rws", 6),), writes=((pre + "mr", t, 1),))
                else:
                    p.op("dve", (lambda t: lambda e: e.scalar_tensor_tensor(
                        mrow[t][:, 1:2], mrow[t][:, 1:2], rws[:, 5:6], rws[:, 6:7],
                        ALU.mult, ALU.add))(t),
                        reads=((pre + "mr", t, 1), (pre + "rws", 5), (pre + "rws", 6)),
                        writes=((pre + "mr", t, 1),))
                for j in range(KB // P):
                    p.op("pe", (lambda j: lambda e: e.transpose(
                        btr[:, (j % 8) * P:(j % 8 + 1) * P], P1[:, j * P:(j + 1) * P], idb[:]))(j),
                        reads=((pre + "P1", j // 4), idt), writes=(btrt,))
                    p.op("dve", (lambda j: lambda e: e.tensor_copy(
                        PT[:, j * P:(j + 1) * P], btr[:, (j % 8) * P:(j % 8 + 1) * P]))(j),
                        reads=(btrt,), writes=((pre + "PTx", j),))
                for vh in range(2):
                    ob, obt = banks[4 + vh], bts[4 + vh]
                    for j in range(KB // P):
                        p.op("pe", (lambda j, vh, s, ob: lambda e: e.matmul(
                            ob[:, 0:512], PT[:, j * P:(j + 1) * P],
                            Vblk[s][:, j * 1024 + vh * 512:j * 1024 + (vh + 1) * 512],
                            start=(j == 0), stop=(j == KB // P - 1)))(j, vh, s, ob),
                            reads=((pre + "PTx", j), (pre + "Vb", s, j)), writes=(obt,))
                    if first:
                        p.op("dve", (lambda t, vh, ob: lambda e: e.tensor_copy(
                            Ot[t][:, vh * 512:(vh + 1) * 512], ob[:, 0:512]))(t, vh, ob),
                            reads=(obt,), writes=((pre + "Ot", t, vh),))
                    else:
                        p.op("dve", (lambda t, vh, ob: lambda e: e.scalar_tensor_tensor(
                            Ot[t][:, vh * 512:(vh + 1) * 512], Ot[t][:, vh * 512:(vh + 1) * 512],
                            rws[:, 5:6], ob[:, 0:512], ALU.mult, ALU.add))(t, vh, ob),
                            reads=(obt, (pre + "rws", 5), (pre + "Ot", t, vh)),
                            writes=((pre + "Ot", t, vh),))
        for k in range(NCH):
            p.op("sp", (lambda k, qb: lambda e: e.dma_start(
                xc[k][:, 0:QB], xsrc[P * k:P * (k + 1), qb * QB:(qb + 1) * QB]))(k, qb),
                reads=((xsn, k),), writes=((pre + "x", k),), dma=f"{pre}rx{k}")
        for t in range(QBT):
            p.op("dve", (lambda t: lambda e: e.reciprocal(
                mrow[t][:, 2:3], mrow[t][:, 1:2]))(t),
                reads=((pre + "mr", t, 1),), writes=((pre + "mr", t, 2),))
            p.op("dve", (lambda t: lambda e: e.tensor_scalar(
                Ot[t][:, 0:1024], Ot[t][:, 0:1024], mrow[t][:, 2:3], None, ALU.mult))(t),
                reads=((pre + "Ot", t, 0), (pre + "Ot", t, 1), (pre + "mr", t, 2)),
                writes=((pre + "Ot", t, 0), (pre + "Ot", t, 1)))
            p.op("pool", (lambda t: lambda e: e.tensor_copy(P1[:, 0:1024], Ot[t][:, 0:1024]))(t),
                 reads=((pre + "Ot", t, 0), (pre + "Ot", t, 1)),
                 writes=((pre + "P1", 0), (pre + "P1", 1)))
            for ch in range(NCH):
                p.op("pe", (lambda ch: lambda e: e.transpose(
                    btr[:, (ch % 8) * P:(ch % 8 + 1) * P], P1[:, ch * P:(ch + 1) * P], idb[:]))(ch),
                    reads=((pre + "P1", ch // 4), idt), writes=(btrt,))
                p.op("dve", (lambda ch, t: lambda e: e.tensor_copy(
                    PT[:, ch * P:(ch + 1) * P], btr[:, (ch % 8) * P:(ch % 8 + 1) * P]))(ch, t),
                    reads=(btrt,), writes=((pre + "PTx", ch),))
            for ch in range(NCH):
                if variant == "l1":
                    p.op("dve", (lambda ch, t: lambda e: e.tensor_tensor(
                        c.tmp[:, 0:P], PT[:, ch * P:(ch + 1) * P],
                        zloc[ch][:, t * P:(t + 1) * P], ALU.add))(ch, t),
                        reads=((pre + "PTx", ch), (pre + "zl", ch)), writes=(("ln_tmp", 0),))
                    p.op("dve", (lambda ch, t: lambda e: e.tensor_scalar(
                        c.tmp[:, 0:P], c.tmp[:, 0:P], macc[:, ch:ch + 1], None, ALU.add))(ch, t),
                        reads=(("ln_tmp", 0), (pre + "macc", 0)), writes=(("ln_tmp", 0),))
                    p.op("dve", (lambda ch, t: lambda e: e.tensor_tensor(
                        xc[ch][:, t * P:(t + 1) * P], xc[ch][:, t * P:(t + 1) * P],
                        c.tmp[:, 0:P], ALU.add))(ch, t),
                        reads=(("ln_tmp", 0), (pre + "x", ch)), writes=((pre + "x", ch),))
                else:
                    p.op("dve", (lambda ch, t: lambda e: e.tensor_tensor(
                        xc[ch][:, t * P:(t + 1) * P], xc[ch][:, t * P:(t + 1) * P],
                        PT[:, ch * P:(ch + 1) * P], ALU.add))(ch, t),
                        reads=((pre + "PTx", ch), (pre + "x", ch)), writes=((pre + "x", ch),))
        out_keys += store_fm(p, xdst, xc, pre + "x", QB, dcol0=qb * QB, wtag=xdn)
    p.pop_scope()
    return out_keys



def emit_l2ffn(p, sh, d, xsrc, xsn, xdst, xdn, agout, agt, pre, TB=1024, Tfull=8192, scale=0.5):
    c = sh.c
    p.push_scope()
    banks, bts = sh.banks, sh.bts
    sw_banks, sw_bts = banks[0:6], bts[0:6]
    cp_bank, cp_bt = banks[6], bts[6]
    xc = [p.sbuf(f"{pre}x{k}", [P, TB], F32, side="right") for k in range(NCH)]
    ob = [[p.sbuf(f"{pre}o{s}_{k}", [P, TB], F32, side="right") for k in range(NCH)] for s in range(2)]
    zbx = [p.sbuf(f"{pre}zb{k}", [P, TB], BF16, side="right") for k in range(NCH)]
    h1b = [p.sbuf(f"{pre}h1b{k}", [P, TB], BF16, side="right") for k in range(NCH)]
    h2b = [p.sbuf(f"{pre}h2b{k}", [P, TB], BF16, side="right") for k in range(4)]
    h3b = [p.sbuf(f"{pre}h3b{k}", [P, TB], BF16, side="right") for k in range(2)]
    tfr = p.sbuf(f"{pre}tfr", [1, TB], F32, side="right")
    cpr = p.sbuf(f"{pre}cpr", [1, TB], F32, side="right")
    mtmp = p.sbuf(f"{pre}mtmp", [P, 512], F32, side="right")
    macc = p.sbuf(f"{pre}macc", [P, NCH], F32, side="right")
    rtmp = p.sbuf(f"{pre}rt", [P, 1], F32, side="right")
    gcb = p.sbuf(f"{pre}gcb", [P, NCH], BF16, side="right")
    h1g = p.sbuf(f"{pre}h1g", [P, 4], BF16, side="right")
    h2g = p.sbuf(f"{pre}h2g", [P, 2], BF16, side="right")
    rowb = p.sbuf(f"{pre}rowb", [1, 8], F32, side="right")
    cbd = {}
    for nm, t, w in ((pre + "cnb1", d["l2f_cnb1"], 8), (pre + "cnb2", d["l2f_cnb2"], 4),
                     (pre + "cnb3", d["l2f_cnb3"], 2), (pre + "gcb1", d["l2f_gcb1"], 4),
                     (pre + "gcb2", d["l2f_gcb2"], 2), (pre + "scl", d["l2f_scl"], 4)):
        buf = p.sbuf(nm, [P if "scl" not in nm else 1, w], F32, side="right")
        p.op("sp", (lambda buf, t: lambda e: e.dma_start(buf[:], t[:]))(buf, t),
             writes=((nm, 0),), dma="L" + nm)
        cbd[nm] = buf
    PWcn1 = ProjWeights(p, d["l2f_w_cn1"], 1024, 1024, pre + "cn1", side="right")
    PWcn2 = ProjWeights(p, d["l2f_w_cn2"], 1024, 512, pre + "cn2", side="right")
    PWcn3 = ProjWeights(p, d["l2f_w_cn3"], 512, 256, pre + "cn3", side="right")
    PWcn4 = ProjWeights(p, d["l2f_w_cn4"], 256, 1, pre + "cn4", side="right")
    PWgc1 = ProjWeights(p, d["l2f_w_gc1"], 1024, 512, pre + "gc1", side="right")
    PWgc2 = ProjWeights(p, d["l2f_w_gc2"], 512, 256, pre + "gc2", side="right")
    PWgc3 = ProjWeights(p, d["l2f_w_gc3"], 256, 1, pre + "gc3", side="right")

    p.op("pool", lambda e: e.memset(macc[:], 0.0), writes=((pre + "macc", 0),))
    zs = ob[1]
    for blk in range(Tfull // TB):
        b0 = blk * TB
        half = b0 // TLOC
        cb0 = b0 % TLOC
        for k in range(NCH):
            p.op("pool", (lambda k, half, cb0: lambda e: e.dma_start(
                xc[k][:, 0:TB], agout[half * 1024 + P * k:half * 1024 + P * (k + 1), cb0:cb0 + TB]))(k, half, cb0),
                reads=((agt, 0),), writes=((pre + "x", k),), dma=f"{pre}fx{k}")
        ln_fm(p, c, xc, pre + "x", TB, zf_chunks=zs, zf_name=pre + "o1")
        for k in range(NCH):
            p.op("dve", (lambda k: lambda e: e.reduce_sum(
                rtmp[:], zs[k][:, 0:TB], axis=mybir.AxisListType.X))(k),
                reads=((pre + "o1", k),), writes=((pre + "rt", 0),))
            p.op("dve", (lambda k: lambda e: e.tensor_tensor(
                macc[:, k:k + 1], macc[:, k:k + 1], rtmp[:], ALU.add))(k),
                reads=((pre + "rt", 0), (pre + "macc", 0)), writes=((pre + "macc", 0),))
    p.op("dve", lambda e: e.tensor_scalar(gcb[:], macc[:], 1.0 / Tfull, None, ALU.mult),
         reads=((pre + "macc", 0),), writes=tuple((pre + "gcb", k) for k in range(NCH)))
    def epi_g1(mi, t0, tw, bank, btag):
        p.op("act", (lambda mi, bank: lambda e: e.activation(
            h1g[:, mi:mi + 1], bank[:, 0:1], AF.Gelu, bias=cbd[pre + "gcb1"][:, mi:mi + 1]))(mi, bank),
            reads=(btag, (pre + "gcb1", 0)), writes=((pre + "h1g", mi),))
    proj_sweep(p, PWgc1, [gcb[:, k:k + 1] for k in range(NCH)], pre + "gcb", 1, sw_banks, sw_bts, epi_g1)
    def epi_g2(mi, t0, tw, bank, btag):
        p.op("act", (lambda mi, bank: lambda e: e.activation(
            h2g[:, mi:mi + 1], bank[:, 0:1], AF.Gelu, bias=cbd[pre + "gcb2"][:, mi:mi + 1]))(mi, bank),
            reads=(btag, (pre + "gcb2", 0)), writes=((pre + "h2g", mi),))
    proj_sweep(p, PWgc2, [h1g[:, k:k + 1] for k in range(4)], pre + "h1g", 1, sw_banks, sw_bts, epi_g2)
    def epi_g3(mi, t0, tw, bank, btag):
        p.op("act", (lambda bank: lambda e: e.activation(
            rowb[0:1, 0:1], bank[0:1, 0:1], AF.Sigmoid, bias=cbd[pre + "scl"][0:1, 1:2]))(bank),
            reads=(btag, (pre + "scl", 0)), writes=((pre + "rowb", 0),))
    proj_sweep(p, PWgc3, [h2g[:, k:k + 1] for k in range(2)], pre + "h2g", 1, sw_banks, sw_bts, epi_g3)
    p.op("dve", lambda e: e.tensor_scalar(rowb[0:1, 3:4], cbd[pre + "scl"][0:1, 2:3], 2.0, None, ALU.mult),
         reads=((pre + "scl", 0),), writes=((pre + "rowb", 3),))
    p.op("dve", lambda e: e.tensor_scalar(rowb[0:1, 4:5], cbd[pre + "scl"][0:1, 2:3], -1.0, 0.1, ALU.mult, ALU.add),
         reads=((pre + "scl", 0),), writes=((pre + "rowb", 4),))
    p.op("dve", lambda e: e.tensor_scalar(rowb[0:1, 5:6], rowb[0:1, 0:1], 0.7, None, ALU.mult),
         reads=((pre + "rowb", 0),), writes=((pre + "rowb", 5),))
    p.op("dve", lambda e: e.tensor_tensor(rowb[0:1, 5:6], rowb[0:1, 5:6], rowb[0:1, 3:4], ALU.mult),
         reads=((pre + "rowb", 5), (pre + "rowb", 3)), writes=((pre + "rowb", 5),))
    p.op("dve", lambda e: e.tensor_tensor(rowb[0:1, 2:3], rowb[0:1, 5:6], rowb[0:1, 4:5], ALU.add),
         reads=((pre + "rowb", 5), (pre + "rowb", 4)), writes=((pre + "rowb", 2),))
    p.op("dve", lambda e: e.tensor_scalar(rowb[0:1, 1:2], rowb[0:1, 3:4], 0.3, None, ALU.mult),
         reads=((pre + "rowb", 3),), writes=((pre + "rowb", 1),))

    out_keys = []
    for blk in range(TLOC // TB):
        load_fm(p, xsrc, xc, pre + "x", TB, dcol0=blk * TB, rtag=xsn)
        ln_fm(p, c, xc, pre + "x", TB, z_chunks=zbx, z_name=pre + "zx", zf_chunks=ob[0], zf_name=pre + "o0")
        for cyc in range(3):
            cur, nxt = cyc % 2, (cyc + 1) % 2
            if cyc > 0:
                for k in range(NCH):
                    p.op("pool", (lambda k, cur: lambda e: e.tensor_copy(
                        zbx[k][:, 0:TB], ob[cur][k][:, 0:TB]))(k, cur),
                        reads=((f"{pre}o{cur}", k),), writes=((pre + "zx", k),))
            def epi_h1(mi, t0, tw, bank, btag):
                p.op("act", (lambda mi, t0, tw, bank: lambda e: e.activation(
                    h1b[mi][:, t0:t0 + tw], bank[:, 0:tw], AF.Gelu,
                    bias=cbd[pre + "cnb1"][:, mi:mi + 1]))(mi, t0, tw, bank),
                    reads=(btag, (pre + "cnb1", 0)), writes=((pre + "h1b", mi),))
            proj_sweep(p, PWcn1, zbx, pre + "zx", TB, sw_banks, sw_bts, epi_h1)
            def epi_h2(mi, t0, tw, bank, btag):
                p.op("act", (lambda mi, t0, tw, bank: lambda e: e.activation(
                    h2b[mi][:, t0:t0 + tw], bank[:, 0:tw], AF.Gelu,
                    bias=cbd[pre + "cnb2"][:, mi:mi + 1]))(mi, t0, tw, bank),
                    reads=(btag, (pre + "cnb2", 0)), writes=((pre + "h2b", mi),))
            proj_sweep(p, PWcn2, h1b, pre + "h1b", TB, sw_banks, sw_bts, epi_h2)
            def epi_h3(mi, t0, tw, bank, btag):
                p.op("act", (lambda mi, t0, tw, bank: lambda e: e.activation(
                    h3b[mi][:, t0:t0 + tw], bank[:, 0:tw], AF.Gelu,
                    bias=cbd[pre + "cnb3"][:, mi:mi + 1]))(mi, t0, tw, bank),
                    reads=(btag, (pre + "cnb3", 0)), writes=((pre + "h3b", mi),))
            proj_sweep(p, PWcn3, h2b, pre + "h2b", TB, sw_banks, sw_bts, epi_h3)
            def epi_h4(mi, t0, tw, bank, btag):
                p.op("act", (lambda t0, tw, bank: lambda e: e.activation(
                    tfr[0:1, t0:t0 + tw], bank[0:1, 0:tw], AF.Sigmoid,
                    bias=cbd[pre + "scl"][0:1, 0:1]))(t0, tw, bank),
                    reads=(btag, (pre + "scl", 0)), writes=((pre + "tfr", 0),))
            proj_sweep(p, PWcn4, h3b, pre + "h3b", TB, sw_banks, sw_bts, epi_h4)
            for t0 in range(0, TB, 512):
                tw = min(512, TB - t0)
                p.op("dve", (lambda t0, tw: lambda e: e.tensor_scalar(
                    cpr[0:1, t0:t0 + tw], tfr[0:1, t0:t0 + tw], rowb[0:1, 1:2], rowb[0:1, 2:3],
                    ALU.mult, ALU.add))(t0, tw),
                    reads=((pre + "tfr", 0), (pre + "rowb", 1), (pre + "rowb", 2)),
                    writes=((pre + "cpr", 0),))
                p.op("pe", (lambda t0, tw: lambda e: e.matmul(
                    cp_bank[:, 0:tw], c.onesr[:], cpr[0:1, t0:t0 + tw], start=True, stop=True))(t0, tw),
                    reads=((pre + "cpr", 0), ("ln_onesr", 0)), writes=(cp_bt,))
                for k in range(NCH):
                    src_, sgn = TWIST_SRC[k]
                    p.op("dve", (lambda k, src_, t0, tw, cur: lambda e: e.tensor_tensor(
                        mtmp[:, 0:tw], ob[cur][src_][:, t0:t0 + tw], cp_bank[:, 0:tw], ALU.mult))(k, src_, t0, tw, cur),
                        reads=((f"{pre}o{cur}", src_), cp_bt), writes=((pre + "mtmp", 0),))
                    p.op("dve", (lambda k, sgn, t0, tw, cur, nxt: lambda e: e.scalar_tensor_tensor(
                        ob[nxt][k][:, t0:t0 + tw], mtmp[:, 0:tw], sgn, ob[cur][k][:, t0:t0 + tw],
                        ALU.mult, ALU.add))(k, sgn, t0, tw, cur, nxt),
                        reads=((pre + "mtmp", 0), (f"{pre}o{cur}", k)), writes=((f"{pre}o{nxt}", k),))
        fin = 3 % 2
        for k in range(NCH):
            p.op("dve", (lambda k, fin: lambda e: e.scalar_tensor_tensor(
                xc[k][:, 0:TB], ob[fin][k][:, 0:TB], scale, xc[k][:, 0:TB], ALU.mult, ALU.add))(k, fin),
                reads=((f"{pre}o{fin}", k), (pre + "x", k)), writes=((pre + "x", k),))
        out_keys += store_fm(p, xdst, xc, pre + "x", TB, dcol0=blk * TB, wtag=xdn)
    p.pop_scope()
    return out_keys



def emit_l3attn(p, sh, d, xsrc, xsn, xdst, xdn, agout, agt, xbs, pre, Tfull=8192, L=64):
    c = sh.c
    p.push_scope()
    banks, bts, btr, btrt = sh.banks, sh.bts, sh.btr, sh.btrt
    idb, idt = sh.idb, sh.idt
    NC_ = Tfull // L
    TBL = 512
    svecs = p.sbuf(f"{pre}svecs", [P, 8], F32, side="right")
    At = [p.sbuf(f"{pre}At{k}", [P, 1024], BF16, side="right") for k in range(NCH)]
    A64 = [p.sbuf(f"{pre}A64_{k}", [P, 1024], BF16, side="right") for k in range(NCH)]
    vrow = p.sbuf(f"{pre}vrow", [1, 8], F32, side="right")
    ones2 = p.sbuf(f"{pre}ones2", [1, 2], F32, side="right")
    uv = p.sbuf(f"{pre}uv", [P, 16], F32, side="right")
    vv = p.sbuf(f"{pre}vv", [P, 16], F32, side="right")
    wv = p.sbuf(f"{pre}wv", [P, 16], F32, side="right")
    vsq = p.sbuf(f"{pre}vsq", [P, 16], F32, side="right")
    mh = p.sbuf(f"{pre}mh", [P, 2], F32, side="right")
    p.op("pool", lambda e: e.memset(ones2[:], 1.0), writes=((pre + "ones2", 0),))
    p.op("sp", lambda e: e.dma_start(mh[:], d["maskH"][:]), writes=((pre + "mh", 0),), dma=pre + "mh")
    p.push_scope()
    xc = [p.sbuf(f"{pre}x{k}", [P, TBL], F32, side="right") for k in range(NCH)]
    zb = [p.sbuf(f"{pre}z{k}", [P, TBL], BF16, side="right") for k in range(NCH)]
    xbb = [p.sbuf(f"{pre}xbb{k}", [P, TBL], BF16, side="right") for k in range(NCH)]
    PWf1 = ProjWeights(p, d["l3a_wA"], 1024, 1024, pre + "pf1", dtype=F32, side="right")
    PWf2 = ProjWeights(p, d["l3a_wAT"], 1024, 1024, pre + "pf2", dtype=F32, side="right")
    PWb = ProjWeights(p, d["l3a_wBT"], 1024, 1024, pre + "wb", side="right")

    def rownorm_inv(nm, eps):
        p.op("act", lambda e: e.activation(vrow[0:1, 0:2], vrow[0:1, 2:4], AF.Sqrt),
             reads=((pre + "vrow", 1),), writes=((pre + "vrow", 0),))
        p.op("dve", (lambda eps: lambda e: e.tensor_scalar(
            vrow[0:1, 0:2], vrow[0:1, 0:2], eps, None, ALU.add))(eps),
            reads=((pre + "vrow", 0),), writes=((pre + "vrow", 0),))
        p.op("dve", lambda e: e.reciprocal(vrow[0:1, 0:2], vrow[0:1, 0:2]),
             reads=((pre + "vrow", 0),), writes=((pre + "vrow", 0),))

    def matvec(dst, dst_tag, src_, src_tag, PWx, dram, nm):
        for mi in range(8):
            b, wtag, mw = PWx.load(mi, dram=dram)
            bank, bt = banks[4], bts[4]
            for k in range(NCH):
                p.op("pe", (lambda b, k, PWx: lambda e: e.matmul(
                    bank[0:P, 0:2], PWx.tile(b, k, P), src_[:, 2 * k:2 * k + 2],
                    start=(k == 0), stop=(k == NCH - 1)))(b, k, PWx),
                    reads=(wtag, (src_tag, 0)), writes=(bt,))
            p.op("dve", (lambda mi, bank: lambda e: e.tensor_copy(
                dst[:, 2 * mi:2 * mi + 2], bank[0:P, 0:2]))(mi, bank),
                reads=(bt,), writes=((dst_tag, 0),))

    def vnorm2_to_row(src_, src_tag, nm):
        p.op("act", (lambda: lambda e: e.activation(vsq[:], src_[:], AF.Square))(),
             reads=((src_tag, 0),), writes=((pre + "vsq", 0),))
        bank, bt = banks[5], bts[5]
        p.op("pe", lambda e: e.matmul(bank[0:1, 0:16], c.onesc[:], vsq[:], start=True, stop=True),
             reads=((pre + "vsq", 0), ("ln_onesc", 0)), writes=(bt,))
        p.op("dve", lambda e: e.reduce_sum(vrow[0:1, 4:5], bank[0:1, 0:16], axis=mybir.AxisListType.X),
             reads=(bt,), writes=((pre + "vrow", 2),))
        p.op("dve", lambda e: e.tensor_scalar(
            vrow[0:1, 2:4], ones2[0:1, 0:2], vrow[0:1, 4:5], 0.5, ALU.mult, ALU.mult),
            reads=((pre + "vrow", 2), (pre + "ones2", 0)), writes=((pre + "vrow", 1),))

    def scale_vec(vec, vec_tag, nm):
        bank, bt = banks[5], bts[5]
        p.op("pe", lambda e: e.matmul(bank[:, 0:2], c.onesr[:], vrow[0:1, 0:2], start=True, stop=True),
             reads=((pre + "vrow", 0), ("ln_onesr", 0)), writes=(bt,))
        for k in range(NCH):
            p.op("dve", (lambda k, bank: lambda e: e.tensor_tensor(
                vec[:, 2 * k:2 * k + 2], vec[:, 2 * k:2 * k + 2], bank[:, 0:2], ALU.mult))(k, bank),
                reads=((vec_tag, 0), bt), writes=((vec_tag, 0),))

    def sigma(dram_nat, dram_t, svec_col, nm):
        p.op("pool", lambda e: e.memset(uv[:], 1.0 / 32.0), writes=((pre + "uv", 0),))
        for it in range(10):
            matvec(vv, pre + "vv", uv, pre + "uv", PWf1, dram_nat, nm)
            vnorm2_to_row(vv, pre + "vv", nm)
            rownorm_inv(nm, 1e-12)
            scale_vec(vv, pre + "vv", nm)
            matvec(uv, pre + "uv", vv, pre + "vv", PWf2, dram_t, nm)
            vnorm2_to_row(uv, pre + "uv", nm)
            rownorm_inv(nm, 1e-12)
            scale_vec(uv, pre + "uv", nm)
        matvec(vv, pre + "vv", uv, pre + "uv", PWf1, dram_nat, nm)
        vnorm2_to_row(vv, pre + "vv", nm)
        rownorm_inv(nm, 1e-12)
        scale_vec(vv, pre + "vv", nm)
        matvec(wv, pre + "wv", vv, pre + "vv", PWf2, dram_t, nm)
        p.op("dve", lambda e: e.tensor_tensor(vsq[:], uv[:], wv[:], ALU.mult),
             reads=((pre + "uv", 0), (pre + "wv", 0)), writes=((pre + "vsq", 0),))
        bank, bt = banks[5], bts[5]
        p.op("pe", lambda e: e.matmul(bank[0:1, 0:16], c.onesc[:], vsq[:], start=True, stop=True),
             reads=((pre + "vsq", 0), ("ln_onesc", 0)), writes=(bt,))
        p.op("dve", lambda e: e.reduce_sum(vrow[0:1, 4:5], bank[0:1, 0:16], axis=mybir.AxisListType.X),
             reads=(bt,), writes=((pre + "vrow", 2),))
        p.op("dve", lambda e: e.tensor_scalar(vrow[0:1, 4:5], vrow[0:1, 4:5], 0.5, None, ALU.mult),
             reads=((pre + "vrow", 2),), writes=((pre + "vrow", 2),))
        p.op("dve", lambda e: e.reciprocal(vrow[0:1, 4:5], vrow[0:1, 4:5]),
             reads=((pre + "vrow", 2),), writes=((pre + "vrow", 2),))
        p.op("pe", lambda e: e.matmul(bank[:, 0:2], c.onesr[:], vrow[0:1, 4:6], start=True, stop=True),
             reads=((pre + "vrow", 2), ("ln_onesr", 0)), writes=(bt,))
        p.op("dve", (lambda svec_col, bank: lambda e: e.tensor_copy(
            svecs[:, svec_col:svec_col + 1], bank[:, 0:1]))(svec_col, bank),
            reads=(bt,), writes=((pre + "svecs", svec_col),))

    sigma(d["l3a_wA"], d["l3a_wAT"], 0, "sA")
    sigma(d["l3a_wB"], d["l3a_wBT"], 1, "sB")
    sigma(d["l3a_wC"], d["l3a_wCT"], 2, "sC")
    p.op("dve", lambda e: e.tensor_tensor(svecs[:, 4:5], svecs[:, 2:3], mh[:, 0:1], ALU.mult),
         reads=((pre + "svecs", 2), (pre + "mh", 0)), writes=((pre + "svecs", 4),))
    p.op("dve", lambda e: e.tensor_tensor(svecs[:, 5:6], svecs[:, 2:3], mh[:, 1:2], ALU.mult),
         reads=((pre + "svecs", 2), (pre + "mh", 0)), writes=((pre + "svecs", 5),))

    for blk in range(Tfull // TBL):
        b0 = blk * TBL
        half = b0 // TLOC
        cb0 = b0 % TLOC
        for k in range(NCH):
            p.op("pool", (lambda k, half, cb0: lambda e: e.dma_start(
                xc[k][:, 0:TBL], agout[half * 1024 + P * k:half * 1024 + P * (k + 1), cb0:cb0 + TBL]))(k, half, cb0),
                reads=((agt, 0),), writes=((pre + "x", k),), dma=f"{pre}fx{k}")
        ln_fm(p, c, xc, pre + "x", TBL, z_chunks=zb, z_name=pre + "z")
        def epi_b(mi, t0, tw, bank, btag):
            p.op("dve", (lambda mi, t0, tw, bank: lambda e: e.tensor_scalar(
                xbb[mi][:, t0:t0 + tw], bank[:, 0:tw], svecs[:, 1:2], None, ALU.mult))(mi, t0, tw, bank),
                reads=(btag, (pre + "svecs", 1)), writes=((pre + "xbb", mi),))
        proj_sweep(p, PWb, zb, pre + "z", TBL, banks[0:4], bts[0:4], epi_b)
        for k in range(NCH):
            p.op("sp", (lambda k, b0: lambda e: e.dma_start(
                xbs[P * k:P * (k + 1), b0:b0 + TBL], xbb[k][:, 0:TBL]))(k, b0),
                reads=((pre + "xbb", k),), dma=f"{pre}SXB{k}")
    fstrip = p.sbuf(f"{pre}fstrip", [P, 1024], F32, side="right")
    An = [p.sbuf(f"{pre}An{k}", [P, 1024], BF16, side="right") for k in range(NCH)]
    for (dst_, dname, srcd) in ((At, pre + "At", d["l3a_wAT"]), (An, pre + "An", d["l3a_wA"])):
        for k in range(NCH):
            p.op("sp", (lambda k, srcd: lambda e: e.dma_start(
                fstrip[:], srcd[P * k:P * (k + 1), :]))(k, srcd),
                writes=((pre + "fstrip", 0),), dma=pre + "Lfs")
            p.op("dve", (lambda k, dst_: lambda e: e.tensor_scalar(
                dst_[k][:], fstrip[:], svecs[:, 0:1], None, ALU.mult))(k, dst_),
                reads=((pre + "fstrip", 0), (pre + "svecs", 0)), writes=((dname, k),))
    pw_n = [[p.sbuf(f"{pre}pwn{s}_{k}", [P, 1024], BF16, side="right") for k in range(NCH)] for s in range(2)]
    pw_t = [[p.sbuf(f"{pre}pwt{s}_{k}", [P, 1024], BF16, side="right") for k in range(NCH)] for s in range(2)]
    cur_n, cur_t, cur_nt, cur_tt = An, At, pre + "An", pre + "At"
    for s in range(6):
        dst_n, dst_t = pw_n[s % 2], pw_t[s % 2]
        dnt, dtt = f"{pre}pwn{s % 2}", f"{pre}pwt{s % 2}"
        for mi in range(NCH):
            bank, bt = banks[mi % 2], bts[mi % 2]
            for jt in range(2):
                for k in range(NCH):
                    p.op("pe", (lambda mi, jt, k, bank, cur_t, cur_n: lambda e: e.matmul(
                        bank[:, 0:512], cur_t[k][:, mi * P:(mi + 1) * P],
                        cur_n[k][:, jt * 512:(jt + 1) * 512],
                        start=(k == 0), stop=(k == NCH - 1)))(mi, jt, k, bank, cur_t, cur_n),
                        reads=((cur_tt, k), (cur_nt, k)), writes=(bt,))
                p.op("dve", (lambda mi, jt, bank, dst_n: lambda e: e.tensor_copy(
                    dst_n[mi][:, jt * 512:(jt + 1) * 512], bank[:, 0:512]))(mi, jt, bank, dst_n),
                    reads=(bt,), writes=((dnt, mi),))
            for jt8 in range(8):
                p.op("pe", (lambda mi, jt8, dst_n: lambda e: e.transpose(
                    btr[:, (jt8 % 8) * P:(jt8 % 8 + 1) * P], dst_n[mi][:, jt8 * P:(jt8 + 1) * P],
                    idb[:]))(mi, jt8, dst_n),
                    reads=((dnt, mi), idt), writes=(btrt,))
                p.op("dve", (lambda mi, jt8, dst_t: lambda e: e.tensor_copy(
                    dst_t[jt8][:, mi * P:(mi + 1) * P], btr[:, (jt8 % 8) * P:(jt8 % 8 + 1) * P]))(mi, jt8, dst_t),
                    reads=(btrt,), writes=((dtt, jt8),))
        cur_n, cur_t, cur_nt, cur_tt = dst_n, dst_t, dnt, dtt
    for k in range(NCH):
        p.op("pool", (lambda k, cur_t: lambda e: e.tensor_copy(A64[k][:], cur_t[k][:]))(k, cur_t),
             reads=((cur_tt, k),), writes=((pre + "A64", k),))
    p.phase()
    p.pop_scope()

    p.push_scope()
    xb = [p.sbuf(f"{pre}hs{k}", [P, Tfull], BF16, side="right") for k in range(NCH)]
    Uc = [[p.sbuf(f"{pre}U{s}_{k}", [P, NC_], BF16, side="right") for k in range(NCH)] for s in range(2)]
    Bi = [p.sbuf(f"{pre}Bi{k}", [P, NC_], BF16, side="right") for k in range(NCH)]
    G = [p.sbuf(f"{pre}G{k}", [P, NC_], BF16, side="right") for k in range(NCH)]
    gv = [p.sbuf(f"{pre}gv{s}", [P, NCH], BF16, side="right") for s in range(2)]
    for k in range(NCH):
        p.op("sp", (lambda k: lambda e: e.dma_start(
            xb[k][:], xbs[P * k:P * (k + 1), :]))(k),
            writes=((pre + "hs", k),), dma=f"{pre}LXB{k}")
        p.op("pool", (lambda k: lambda e: e.memset(G[k][:], 0.0))(k), writes=((pre + "G", k),))
        p.op("pool", (lambda k: lambda e: e.memset(Uc[0][k][:], 0.0))(k), writes=((pre + "U0", k),))
    for i in range(L):
        par, nxt = i % 2, (i + 1) % 2
        for k in range(NCH):
            p.op("pool", (lambda k, i: lambda e: e.tensor_copy(
                Bi[k][:], xb[k].rearrange("p (cc l) -> p l cc", l=L)[:, i:i + 1, :]))(k, i),
                reads=((pre + "hs", k),), writes=((pre + "Bi", k),))
        for mi in range(NCH):
            bank, bt = banks[mi % 4], bts[mi % 4]
            for k in range(NCH):
                p.op("pe", (lambda mi, k, par, bank: lambda e: e.matmul(
                    bank[:, 0:NC_], At[k][:, mi * P:(mi + 1) * P], Uc[par][k][:],
                    start=(k == 0), stop=False))(mi, k, par, bank),
                    reads=((pre + "At", k), (f"{pre}U{par}", k)), writes=(bt,))
            p.op("pe", (lambda mi, bank: lambda e: e.matmul(
                bank[:, 0:NC_], idb[:], Bi[mi][:], start=False, stop=True))(mi, bank),
                reads=((pre + "Bi", mi), idt), writes=(bt,))
            p.op("dve", (lambda mi, nxt, bank: lambda e: e.tensor_copy(
                Uc[nxt][mi][:], bank[:, 0:NC_]))(mi, nxt, bank),
                reads=(bt,), writes=((f"{pre}U{nxt}", mi),))
            p.op("pool", (lambda mi, nxt, i: lambda e: e.tensor_copy(
                xb[mi].rearrange("p (cc l) -> p l cc", l=L)[:, i:i + 1, :], Uc[nxt][mi][:]))(mi, nxt, i),
                reads=((f"{pre}U{nxt}", mi),), writes=((pre + "hs", mi),))
    p.op("pool", lambda e: e.memset(gv[0][:], 0.0), writes=((pre + "gv0", 0),))
    for cc in range(NC_):
        par, nxt = cc % 2, (cc + 1) % 2
        for mi in range(NCH):
            bank, bt = banks[4 + mi % 2], bts[4 + mi % 2]
            for k in range(NCH):
                p.op("pe", (lambda mi, k, par, cc, bank: lambda e: e.matmul(
                    bank[:, 0:1], A64[k][:, mi * P:(mi + 1) * P], gv[par][:, k:k + 1],
                    start=(k == 0), stop=False))(mi, k, par, cc, bank),
                    reads=((pre + "A64", k), (f"{pre}gv{par}", 0)), writes=(bt,))
            p.op("pe", (lambda mi, cc, bank: lambda e: e.matmul(
                bank[:, 0:1], idb[:],
                xb[mi].rearrange("p (cc l) -> p cc l", l=L)[:, cc:cc + 1, L - 1:L],
                start=False, stop=True))(mi, cc, bank),
                reads=((pre + "hs", mi), idt), writes=(bt,))
            p.op("dve", (lambda mi, nxt, bank: lambda e: e.tensor_copy(
                gv[nxt][:, mi:mi + 1], bank[:, 0:1]))(mi, nxt, bank),
                reads=(bt,), writes=((f"{pre}gv{nxt}", 0),))
            if cc + 1 < NC_:
                p.op("pool", (lambda mi, nxt, cc: lambda e: e.tensor_copy(
                    G[mi][:, cc + 1:cc + 2], gv[nxt][:, mi:mi + 1]))(mi, nxt, cc),
                    reads=((f"{pre}gv{nxt}", 0),), writes=((pre + "G", mi),))
    Pp = [[p.sbuf(f"{pre}Pp{s}_{k}", [P, NC_], BF16, side="right") for k in range(NCH)] for s in range(2)]
    for k in range(NCH):
        p.op("pool", (lambda k: lambda e: e.tensor_copy(Pp[1][k][:], G[k][:]))(k),
             reads=((pre + "G", k),), writes=((f"{pre}Pp1", k),))
    for i in range(L):
        par, nxt = (i + 1) % 2, i % 2
        for mi in range(NCH):
            bank, bt = banks[mi % 4], bts[mi % 4]
            for k in range(NCH):
                p.op("pe", (lambda mi, k, par, bank: lambda e: e.matmul(
                    bank[:, 0:NC_], At[k][:, mi * P:(mi + 1) * P], Pp[par][k][:],
                    start=(k == 0), stop=(k == NCH - 1)))(mi, k, par, bank),
                    reads=((pre + "At", k), (f"{pre}Pp{par}", k)), writes=(bt,))
            p.op("dve", (lambda mi, nxt, bank: lambda e: e.tensor_copy(
                Pp[nxt][mi][:], bank[:, 0:NC_]))(mi, nxt, bank),
                reads=(bt,), writes=((f"{pre}Pp{nxt}", mi),))
            p.op("pool", (lambda mi, nxt, i: lambda e: e.tensor_tensor(
                xb[mi].rearrange("p (cc l) -> p l cc", l=L)[:, i:i + 1, :],
                xb[mi].rearrange("p (cc l) -> p l cc", l=L)[:, i:i + 1, :],
                Pp[nxt][mi][:], ALU.add))(mi, nxt, i),
                reads=((f"{pre}Pp{nxt}", mi), (pre + "hs", mi)), writes=((pre + "hs", mi),))
    xc2 = [p.sbuf(f"{pre}x2{k}", [P, TBL], F32, side="right") for k in range(NCH)]
    PWc = ProjWeights(p, d["l3a_wCT"], 1024, 1024, pre + "wc", side="right")
    out_keys = []
    for blk in range(Tfull // TBL):
        b0 = blk * TBL
        half = b0 // TLOC
        j0 = b0 % TLOC
        if half == 0:
            load_fm(p, xsrc, xc2, pre + "x2", TBL, dcol0=j0, rtag=xsn)
        else:
            load_fm(p, xdst, xc2, pre + "x2", TBL, dcol0=j0, rtag=xdn, eng="pool")
        svcol = 4 + half
        def epi_c(mi, t0, tw, bank, btag):
            p.op("dve", (lambda mi, t0, tw, bank, svcol: lambda e: e.scalar_tensor_tensor(
                xc2[mi][:, t0:t0 + tw], bank[:, 0:tw], svecs[:, svcol:svcol + 1],
                xc2[mi][:, t0:t0 + tw], ALU.mult, ALU.add))(mi, t0, tw, bank, svcol),
                reads=(btag, (pre + "svecs", svcol), (pre + "x2", mi)), writes=((pre + "x2", mi),))
        proj_sweep(p, PWc, xb, pre + "hs", TBL, banks[0:4], bts[0:4], epi_c, z_col0=b0)
        out_keys += store_fm(p, xdst, xc2, pre + "x2", TBL, dcol0=j0, wtag=xdn)
    p.pop_scope()
    p.pop_scope()
    return out_keys



def emit_l3ffn_app(p, sh, d, xsrc, xsn, xdst, xdn, agout30, agt, ybufs, ybn, ysc, y2,
                   pre, Text0=7168, RING=8):
    c = sh.c
    p.push_scope()
    banks, bts, btr, btrt = sh.banks, sh.bts, sh.btr, sh.btrt
    idb, idt = sh.idb, sh.idt
    TB = 512
    yb = [p.sbuf(f"{pre}y{k}", [P, TB + 2], F32, side="right") for k in range(NCH)]
    ybs = [p.sbuf(f"{pre}ys{k}", [P, TB + 2], F32, side="right") for k in range(4)]
    wb = [p.sbuf(f"{pre}w{k}", [P, TB], BF16, side="right") for k in range(NCH)]
    pf = [p.sbuf(f"{pre}pf{k}", [P, TB], F32, side="right") for k in range(NCH)]
    vb = [p.sbuf(f"{pre}vbk{k}", [P, TB], BF16, side="right") for k in range(NCH)]
    Qr = [p.sbuf(f"{pre}Qr{k}", [P, TB], BF16, side="right") for k in range(NCH)]
    Kring = [p.sbuf(f"{pre}Kr{k}", [P, RING * P], BF16, side="right") for k in range(NCH)]
    Vring = p.sbuf(f"{pre}Vring", [P, RING * 1024], BF16, side="right")
    Ofm = [p.sbuf(f"{pre}Of{k}", [P, TB], BF16, side="right") for k in range(NCH)]
    Ab = [p.sbuf(f"{pre}Ab{k}", [P, TB], BF16, side="right") for k in range(22)]
    tabs = [p.sbuf(f"{pre}tab{i}", [P, TB], F32, side="right") for i in range(4)]
    P1x = [p.sbuf(f"{pre}P1_{i}", [P, 512], BF16, side="right") for i in range(4)]
    PTx = [p.sbuf(f"{pre}PT_{i}", [P, 1024], BF16, side="right") for i in range(2)]
    rwsx = [p.sbuf(f"{pre}rws{i}", [P, 8], F32, side="right") for i in range(4)]
    P1 = P1x[0]
    PT = PTx[0]
    rws = rwsx[0]
    r2x = [p.sbuf(f"{pre}r2_{i}", [2, 512], F32, side="right") for i in range(2)]
    scrx = [p.sbuf(f"{pre}scr{i}", [P, TB], F32, side="right") for i in range(2)]
    scrbx = [p.sbuf(f"{pre}scrb{i}", [P, TB], BF16, side="right") for i in range(2)]
    r2 = r2x[0]
    scr = scrx[0]
    scrb = scrbx[0]
    hob = p.sbuf(f"{pre}hob", [P, 2], F32, side="right")
    heb = p.sbuf(f"{pre}heb", [2, P], F32, side="right")
    qss = p.sbuf(f"{pre}qss", [P, 2], F32, side="right")
    lgb = p.sbuf(f"{pre}lgb", [P, 16], F32, side="right")
    b2s = p.sbuf(f"{pre}b2s", [P, NCH], F32, side="right")
    b1as = p.sbuf(f"{pre}b1as", [P, 22], F32, side="right")
    b1gs = p.sbuf(f"{pre}b1gs", [P, 22], F32, side="right")
    p32s = p.sbuf(f"{pre}p32s", [P, P], BF16, side="right")
    msks = p.sbuf(f"{pre}msks", [P, 4 * 512], F32, side="right")
    mlt = p.sbuf(f"{pre}mlt", [P, 1], F32, side="right")
    PWqkv = ProjWeights(p, d["l3f_qkvw"][0], 1024, 3072, pre + "wqkv", side="right")
    PWo = ProjWeights(p, d["l3f_wow"][0], 1024, 1024, pre + "wo", side="right")
    PW1a = ProjWeights(p, d["l3f_w1w"][0], 1024, 2730, pre + "w1a", side="right")
    PW1g = ProjWeights(p, d["l3f_w1w"][0], 1024, 2730, pre + "w1g", side="right")
    PW2 = ProjWeights(p, d["l3f_w2w"][0], 2730, 1024, pre + "w2", side="right")

    btr2, btrt2 = btr, btrt
    for nm, dst_, sr in ((pre + "hob", hob, d["l3f_hones"]), (pre + "heb", heb, d["l3f_hexp"]),
                         (pre + "p32s", p32s, d["l3f_p32"]), (pre + "mlt", mlt, d["maskL"])):
        p.op("sp", (lambda dst_, sr: lambda e: e.dma_start(dst_[:], sr[:]))(dst_, sr),
             writes=((nm, 0),), dma="L" + nm)
    msrc = d["l3f_masks"].rearrange("s q m -> q s m")
    p.op("sp", lambda e: e.dma_start(msks[:, :].rearrange("q (s m) -> q s m", m=512), msrc),
         writes=((pre + "msks", 0),), dma=pre + "msk")
    for k in range(NCH):
        p.op("pool", (lambda k: lambda e: e.memset(Kring[k][:], 0.0))(k), writes=((pre + "KR", k),))
    p.op("pool", lambda e: e.memset(Vring[:], 0.0), writes=((pre + "VR", -1),))

    yinit = ybufs[0]
    p.op("pool", lambda e: e.memset(scr[:, 0:2], 0.0), writes=((pre + "scr", 0),))
    for k in range(NCH):
        p.op("sp", (lambda k: lambda e: e.dma_start(
            yinit[P * k:P * (k + 1), 0:2], scr[:, 0:2]))(k),
            reads=((pre + "scr", 0),), writes=((ybn[0], k),), dma=f"{pre}z2{k}")
    for blk in range(Text0 // TB):
        g2 = blk * TB
        if g2 < 3072:
            for k in range(NCH):
                p.op("pool", (lambda k, g2: lambda e: e.dma_start(
                    yb[k][:, 0:TB], agout30[P * k:P * (k + 1), g2:g2 + TB]))(k, g2),
                    reads=((agt, 0),), writes=((pre + "y", k),), dma=f"{pre}hy{k}")
                p.op("dve", (lambda k: lambda e: e.tensor_scalar(
                    yb[k][:, 0:TB], yb[k][:, 0:TB], mlt[:, 0:1], None, ALU.mult))(k),
                    reads=((pre + "y", k), (pre + "mlt", 0)), writes=((pre + "y", k),))
        else:
            load_fm(p, xsrc, yb, pre + "y", TB, dcol0=g2 - 3072, rtag=xsn)
        ln_fm(p, c, yb, pre + "y", TB, zf_chunks=pf, zf_name=pre + "pf")
        for k in range(NCH):
            p.op("sp", (lambda k, g2: lambda e: e.dma_start(
                yinit[P * k:P * (k + 1), 2 + g2:2 + g2 + TB], pf[k][:, 0:TB]))(k, g2),
                reads=((pre + "pf", k),), writes=((ybn[0], k),), dma=f"{pre}Sy0{k}")
    p.phase()

    def l2norm_rope(src_, src_name, dst_, dst_name, tw, gcol, tabC_, tabCt, tabS_, tabSt, dcol0):
        for ch in range(NCH):
            si = ch % 2
            bset = ((banks[4], bts[4]), (banks[5], bts[5]), (banks[6], bts[6]))
            (bank_s, bst), (bank_rb, brt), (bank_sw, bwt) = bset
            scr = scrx[si]
            scrb = scrbx[si]
            r2 = r2x[si]
            scrt = (pre + "scr", si)
            scrbt = (pre + "scrb", si)
            r2t = (pre + "r2", si)
            sq = ch % 2
            p.op("act", (lambda ch, sq, tw: lambda e: e.activation(
                c.xsq[sq][:, 0:tw], src_[ch][:, 0:tw], AF.Square))(ch, sq, tw),
                reads=((src_name, ch),), writes=(("ln_xsq", sq),))
            p.op("pe", (lambda sq, tw: lambda e: e.matmul(
                bank_s[0:2, 0:tw], hob[:], c.xsq[sq][:, 0:tw], start=True, stop=True))(sq, tw),
                reads=(("ln_xsq", sq), (pre + "hob", 0)), writes=(bst,))
            p.op("act", (lambda tw, r2, bank_s: lambda e: e.activation(
                r2[0:2, 0:tw], bank_s[0:2, 0:tw], AF.Sqrt))(tw, r2, bank_s),
                reads=(bst,), writes=(r2t,))
            p.op("dve", (lambda tw, r2: lambda e: e.tensor_scalar(
                r2[0:2, 0:tw], r2[0:2, 0:tw], 1e-12, None, ALU.max))(tw, r2),
                reads=(r2t,), writes=(r2t,))
            p.op("dve", (lambda tw, r2: lambda e: e.reciprocal(
                r2[0:2, 0:tw], r2[0:2, 0:tw]))(tw, r2),
                reads=(r2t,), writes=(r2t,))
            p.op("pe", (lambda tw, r2, bank_rb: lambda e: e.matmul(
                bank_rb[:, 0:tw], heb[:], r2[0:2, 0:tw], start=True, stop=True))(tw, r2, bank_rb),
                reads=(r2t, (pre + "heb", 0)), writes=(brt,))
            p.op("dve", (lambda ch, tw, gcol, scr, bank_rb: lambda e: e.scalar_tensor_tensor(
                scr[:, 0:tw], src_[ch][:, 0:tw], qss[:, gcol:gcol + 1], bank_rb[:, 0:tw],
                ALU.mult, ALU.mult))(ch, tw, gcol, scr, bank_rb),
                reads=((src_name, ch), (pre + "qss", 0), brt), writes=(scrt,))
            p.op("pool", (lambda tw, scrb, scr: lambda e: e.tensor_copy(
                scrb[:, 0:tw], scr[:, 0:tw]))(tw, scrb, scr),
                reads=(scrt,), writes=(scrbt,))
            p.op("pe", (lambda tw, scrb, bank_sw: lambda e: e.matmul(
                bank_sw[:, 0:tw], p32s[:], scrb[:, 0:tw], start=True, stop=True))(tw, scrb, bank_sw),
                reads=(scrbt, (pre + "p32s", 0)), writes=(bwt,))
            p.op("dve", (lambda tw, tabC_, tabCt, scr: lambda e: e.tensor_tensor(
                scr[:, 0:tw], scr[:, 0:tw], tabC_[:, 0:tw], ALU.mult))(tw, tabC_, tabCt, scr),
                reads=(scrt, tabCt), writes=(scrt,))
            p.op("dve", (lambda tw, tabS_, tabSt, scrb, bank_sw: lambda e: e.tensor_tensor(
                scrb[:, 0:tw], tabS_[:, 0:tw], bank_sw[:, 0:tw], ALU.mult))(tw, tabS_, tabSt, scrb, bank_sw),
                reads=(tabSt, bwt), writes=(scrbt,))
            p.op("dve", (lambda ch, tw, dcol0, scr, scrb: lambda e: e.tensor_tensor(
                dst_[ch][:, dcol0:dcol0 + tw], scr[:, 0:tw], scrb[:, 0:tw], ALU.add))(ch, tw, dcol0, scr, scrb),
                reads=(scrt, scrbt), writes=((dst_name, ch),))

    for li in range(6):
        last = li == 5
        b_i = 512 * li
        Li = Text0 - b_i
        Ltiles = Li // P
        ycur, ycn = ybufs[li % 2], ybn[li % 2]
        ynxt, ynn = ybufs[(li + 1) % 2], ybn[(li + 1) % 2]
        for nm, dst_, sr in ((pre + "qss", qss, d["l3f_qsc"][li]), (pre + "lgb", lgb, d["l3f_lngb"][li]),
                             (pre + "b2s", b2s, d["l3f_b2c"][li]), (pre + "b1as", b1as, d["l3f_b1a"][li]),
                             (pre + "b1gs", b1gs, d["l3f_b1g"][li])):
            p.op("sp", (lambda dst_, sr: lambda e: e.dma_start(dst_[:], sr[:]))(dst_, sr),
                 writes=((nm, 0),), dma="L" + nm)
        sp0 = 24 - 4 * li
        def mslot(t):
            if t == sp0:
                return 2
            if t == sp0 + 1:
                return 3
            return t % 2
        blocks = [(0, 2, False), (2, 2, True)]
        t = 4
        while t < Ltiles:
            nt = min(4, Ltiles - t)
            blocks.append((t, nt, True))
            t += nt
        for (t0, nt, do_q) in blocks:
            gf = b_i + P * t0
            bw = P * nt
            load_fm(p, ycur, yb, pre + "y", bw + 2, dcol0=gf, rtag=ycn)
            for k in range(NCH):
                ofs = 2 if k < 4 else 1
                p.op("pool", (lambda k, ofs, bw: lambda e: e.tensor_copy(
                    wb[k][:, 0:bw], yb[k][:, ofs:ofs + bw]))(k, ofs, bw),
                    reads=((pre + "y", k),), writes=((pre + "w", k),))
            tcol = b_i + P * t0
            for (ti, tnm) in ((0, "l3f_tabQC"), (1, "l3f_tabQS"), (2, "l3f_tabKC"), (3, "l3f_tabKS")):
                p.op("sp", (lambda ti, tnm, tcol, bw: lambda e: e.dma_start(
                    tabs[ti][:, 0:bw], d[tnm][:, tcol:tcol + bw]))(ti, tnm, tcol, bw),
                    writes=((pre + "tab", ti),), dma=f"{pre}tb{ti}")
            if do_q:
                def epi_q(mi, tt0, tw, bank, btag):
                    p.op("dve", (lambda mi, tt0, tw, bank: lambda e: e.tensor_copy(
                        pf[mi][:, tt0:tt0 + tw], bank[:, 0:tw]))(mi, tt0, tw, bank),
                        reads=(btag,), writes=((pre + "pf", mi),))
                proj_sweep(p, PWqkv, wb, pre + "w", bw, banks[0:4], bts[0:4], epi_q,
                           m_chunks=list(range(0, 8)), w_dram=d["l3f_qkvw"][li])
                l2norm_rope(pf, pre + "pf", Qr, pre + "Qr", bw, 0, tabs[0], (pre + "tab", 0),
                            tabs[1], (pre + "tab", 1), 0)
            def epi_k(mi, tt0, tw, bank, btag):
                mo = mi - 8
                p.op("dve", (lambda mo, tt0, tw, bank: lambda e: e.tensor_copy(
                    pf[mo][:, tt0:tt0 + tw], bank[:, 0:tw]))(mo, tt0, tw, bank),
                    reads=(btag,), writes=((pre + "pf", mo),))
            proj_sweep(p, PWqkv, wb, pre + "w", bw, banks[0:4], bts[0:4], epi_k,
                       m_chunks=list(range(8, 16)), w_dram=d["l3f_qkvw"][li])
            l2norm_rope(pf, pre + "pf", vb, pre + "kw", bw, 1, tabs[2], (pre + "tab", 2),
                        tabs[3], (pre + "tab", 3), 0)
            for tt in range(nt):
                slot = (t0 + tt) % RING
                for k in range(NCH):
                    p.op("pool", (lambda k, tt, slot: lambda e: e.tensor_copy(
                        Kring[k][:, slot * P:(slot + 1) * P], vb[k][:, tt * P:(tt + 1) * P]))(k, tt, slot),
                        reads=((pre + "kw", k),), writes=((pre + "KR", k),))
            def epi_v(mi, tt0, tw, bank, btag):
                mo = mi - 16
                p.op("dve", (lambda mo, tt0, tw, bank: lambda e: e.tensor_copy(
                    vb[mo][:, tt0:tt0 + tw], bank[:, 0:tw]))(mo, tt0, tw, bank),
                    reads=(btag,), writes=((pre + "kw", mo),))
            proj_sweep(p, PWqkv, wb, pre + "w", bw, banks[0:4], bts[0:4], epi_v,
                       m_chunks=list(range(16, 24)), w_dram=d["l3f_qkvw"][li])
            for tt in range(nt):
                slot = (t0 + tt) % RING
                for ch in range(NCH):
                    p.op("pe", (lambda tt, ch: lambda e: e.transpose(
                        btr[:, (ch % 8) * P:(ch % 8 + 1) * P], vb[ch][:, tt * P:(tt + 1) * P],
                        idb[:]))(tt, ch),
                        reads=((pre + "kw", ch), idt), writes=(btrt,))
                    p.op("dve", (lambda slot, ch: lambda e: e.tensor_copy(
                        Vring[:, slot * 1024 + ch * P:slot * 1024 + (ch + 1) * P],
                        btr[:, (ch % 8) * P:(ch % 8 + 1) * P]))(slot, ch),
                        reads=(btrt,), writes=((pre + "VR", slot),))
            if not do_q:
                continue
            for tt in range(nt):
                tl = t0 + tt
                nk = 3 if tl % 2 == 0 else 4
                kt0 = tl - nk + 1
                slotL = [(kt0 + j) % RING for j in range(nk)]
                sl = mslot(tl)
                for ch in range(NCH):
                    si = ch % 2
                    bA, bAt = banks[2 * si], bts[2 * si]
                    bB, bBt = banks[2 * si + 1], bts[2 * si + 1]
                    PTs = PTx[si]
                    ptt = (pre + "PTc", si)
                    for (bank, bt2, hof) in ((bA, bAt, 0), (bB, bBt, 64)):
                        for j in range(nk):
                            p.op("pe", (lambda ch, tt, hof, j, slotL, bank, nk: lambda e: e.matmul(
                                bank[0:P, j * P:(j + 1) * P],
                                Qr[ch][hof:hof + 64, tt * P:(tt + 1) * P],
                                Kring[ch][hof:hof + 64, slotL[j] * P:(slotL[j] + 1) * P],
                                start=(j == 0), stop=(j == nk - 1),
                                tile_position=(hof, 0)))(ch, tt, hof, j, slotL, bank, nk),
                                reads=((pre + "Qr", ch), (pre + "KR", ch)), writes=(bt2,))
                    for hi, (bank, bt2, hof) in enumerate(((bA, bAt, 0), (bB, bBt, 64))):
                        P1s = P1x[2 * si + hi]
                        rwss = rwsx[2 * si + hi]
                        p1t = (pre + "P1", si, hi)
                        rwt = (lambda si, hi: lambda i: (pre + "rws", si, hi, i))(si, hi)
                        p.op("dve", (lambda bank, sl, nk: lambda e: e.scalar_tensor_tensor(
                            bank[:, 0:nk * P], bank[:, 0:nk * P], 8.0,
                            msks[:, sl * 512:sl * 512 + nk * P], ALU.mult, ALU.add))(bank, sl, nk),
                            reads=(bt2, (pre + "msks", 0)), writes=(bt2,))
                        p.op("dve", (lambda bank, nk, rwss: lambda e: e.reduce_max(
                            rwss[:, 0:1], bank[:, 0:nk * P], axis=mybir.AxisListType.X,
                            negate=True))(bank, nk, rwss),
                            reads=(bt2,), writes=(rwt(1),))
                        p.op("act", (lambda bank, nk, P1s, rwss: lambda e: e.activation(
                            P1s[:, 0:nk * P], bank[:, 0:nk * P], AF.Exp, bias=rwss[:, 0:1],
                            accum_out=rwss[:, 2:3]))(bank, nk, P1s, rwss),
                            reads=(bt2, rwt(1)), writes=(p1t, rwt(2)))
                        p.op("dve", (lambda rwss: lambda e: e.reciprocal(
                            rwss[:, 3:4], rwss[:, 2:3]))(rwss),
                            reads=(rwt(2),), writes=(rwt(3),))
                        p.op("dve", (lambda nk, P1s, rwss: lambda e: e.tensor_scalar(
                            P1s[:, 0:nk * P], P1s[:, 0:nk * P], rwss[:, 3:4], None, ALU.mult))(nk, P1s, rwss),
                            reads=(p1t, rwt(3)), writes=(p1t,))
                    for hi in (0, 1):
                        P1s = P1x[2 * si + hi]
                        p1t = (pre + "P1", si, hi)
                        for j in range(nk):
                            p.op("pe", (lambda hi, j, nk, P1s: lambda e: e.transpose(
                                btr[:, (hi * nk + j) * P:(hi * nk + j + 1) * P],
                                P1s[:, j * P:(j + 1) * P], idb[:]))(hi, j, nk, P1s),
                                reads=(p1t, idt), writes=(btrt,))
                    p.op("dve", (lambda nk, PTs: lambda e: e.tensor_copy(
                        PTs[:, 0:2 * nk * P], btr[:, 0:2 * nk * P]))(nk, PTs),
                        reads=(btrt,), writes=(ptt,))
                    for hi, (bank, bt2, hof) in enumerate(((bA, bAt, 0), (bB, bBt, 64))):
                        for j in range(nk):
                            p.op("pe", (lambda hi, j, ch, hof, slotL, bank, PTs, nk: lambda e: e.matmul(
                                bank[0:64, 0:P],
                                Vring[:, slotL[j] * 1024 + ch * P + hof:slotL[j] * 1024 + ch * P + hof + 64],
                                PTs[:, (hi * nk + j) * P:(hi * nk + j + 1) * P],
                                start=(j == 0), stop=(j == nk - 1)))(hi, j, ch, hof, slotL, bank, PTs, nk),
                                reads=(ptt, (pre + "VR", slotL[j])), writes=(bt2,))
                        p.op("dve", (lambda ch, tt, hof, bank: lambda e: e.tensor_copy(
                            Ofm[ch][hof:hof + 64, tt * P:(tt + 1) * P], bank[0:64, 0:P]))(ch, tt, hof, bank),
                            reads=(bt2,), writes=((pre + "Of", ch),))
            def epi_o(mi, tt0, tw, bank, btag):
                p.op("dve", (lambda mi, tt0, tw, bank: lambda e: e.tensor_tensor(
                    yb[mi][:, 2 + tt0:2 + tt0 + tw], yb[mi][:, 2 + tt0:2 + tt0 + tw],
                    bank[:, 0:tw], ALU.add))(mi, tt0, tw, bank),
                    reads=(btag, (pre + "y", mi)), writes=((pre + "y", mi),))
            proj_sweep(p, PWo, Ofm, pre + "Of", bw, banks[0:4], bts[0:4], epi_o,
                       w_dram=d["l3f_wow"][li])
            for k in range(NCH):
                p.op("sp", (lambda k, gf, bw: lambda e: e.dma_start(
                    ysc[P * k:P * (k + 1), gf + 2:gf + 2 + bw], yb[k][:, 2:2 + bw]))(k, gf, bw),
                    reads=((pre + "y", k),), writes=((pre + "ysc", k),), dma=f"{pre}Sc{k}")
        g_ap = [lgb[:, k:k + 1] for k in range(NCH)]
        b_ap = [lgb[:, 8 + k:8 + k + 1] for k in range(NCH)]
        fstart = b_i + 512
        fblocks = []
        t = fstart
        while t < 2 + Text0:
            bwf = min(TB, 2 + Text0 - t)
            fblocks.append((t, bwf))
            t += bwf
        for (g0, bwf) in fblocks:
            load_fm(p, ysc, yb, pre + "y", bwf + 2, dcol0=g0 - 2, rtag=pre + "ysc")
            for k in range(4):
                p.op("sp", (lambda k, g0, bwf: lambda e: e.dma_start(
                    ybs[k][:, 0:bwf + 2], ysc[P * (k + 4):P * (k + 5), g0 - 3:g0 - 1 + bwf]))(k, g0, bwf),
                    reads=((pre + "ysc", k + 4),), writes=((pre + "ybs", k),), dma=f"{pre}Ls{k}")
            lnx = yb[0:4] + ybs
            ln_fm(p, c, lnx, pre + "y", bwf, z_chunks=wb, z_name=pre + "w",
                  col0s=[2] * 8, gain_ap=g_ap, gain_tag=(pre + "lgb", 0), bias_ap=b_ap,
                  x_col_tags=lambda k: (pre + "y", k) if k < 4 else (pre + "ybs", k - 4))
            def epi_a(mi, tt0, tw, bank, btag):
                mw2 = min(P, 2730 - mi * P)
                p.op("dve", (lambda mi, tt0, tw, bank, mw2: lambda e: e.tensor_scalar(
                    Ab[mi][0:mw2, tt0:tt0 + tw], bank[0:mw2, 0:tw], b1as[0:mw2, mi:mi + 1],
                    None, ALU.add))(mi, tt0, tw, bank, mw2),
                    reads=(btag, (pre + "b1as", 0)), writes=((pre + "Ab", mi),))
            proj_sweep(p, PW1a, wb, pre + "w", bwf, banks[0:4], bts[0:4], epi_a,
                       w_dram=d["l3f_w1w"][li])
            def epi_g(mi, tt0, tw, bank, btag):
                mw2 = min(P, 2730 - mi * P)
                sg = scrbx[mi % 2]
                sgt = (pre + "scrb", mi % 2)
                p.op("act", (lambda mi, tt0, tw, bank, mw2, sg: lambda e: e.activation(
                    sg[0:mw2, 0:tw], bank[0:mw2, 0:tw], AF.Gelu, bias=b1gs[0:mw2, mi:mi + 1]))(mi, tt0, tw, bank, mw2, sg),
                    reads=(btag, (pre + "b1gs", 0)), writes=(sgt,))
                p.op("dve", (lambda mi, tt0, tw, mw2, sg: lambda e: e.tensor_tensor(
                    Ab[mi][0:mw2, tt0:tt0 + tw], Ab[mi][0:mw2, tt0:tt0 + tw],
                    sg[0:mw2, 0:tw], ALU.mult))(mi, tt0, tw, mw2, sg),
                    reads=(sgt, (pre + "Ab", mi)), writes=((pre + "Ab", mi),))
            proj_sweep(p, PW1g, wb, pre + "w", bwf, banks[0:4], bts[0:4], epi_g, w_m_ofs=2730,
                       w_dram=d["l3f_w1w"][li])
            def epi_2(mi, tt0, tw, bank, btag):
                p.op("dve", (lambda mi, tt0, tw, bank: lambda e: e.scalar_tensor_tensor(
                    yb[mi][:, 2 + tt0:2 + tt0 + tw], bank[:, 0:tw], b2s[:, mi:mi + 1],
                    yb[mi][:, 2 + tt0:2 + tt0 + tw], ALU.add, ALU.add))(mi, tt0, tw, bank),
                    reads=(btag, (pre + "b2s", 0), (pre + "y", mi)), writes=((pre + "y", mi),))
            proj_sweep(p, PW2, Ab, pre + "Ab", bwf, banks[0:4], bts[0:4], epi_2,
                       w_dram=d["l3f_w2w"][li])
            o0 = max(g0, (3074 if last else b_i + 513))
            if o0 < g0 + bwf:
                if last:
                    store_fm(p, y2, yb, pre + "y", g0 + bwf - o0, dcol0=o0 - 3074,
                             col0=2 + o0 - g0, wtag=pre + "y2")
                else:
                    store_fm(p, ynxt, yb, pre + "y", g0 + bwf - o0, dcol0=o0,
                             col0=2 + o0 - g0, wtag=ynn)
        p.phase()
    out_keys = []
    TBF = 512
    for blk in range(TLOC // TBF):
        load_fm(p, xsrc, yb, pre + "y", TBF, dcol0=blk * TBF, rtag=xsn)
        load_fm(p, y2, pf, pre + "pf", TBF, dcol0=blk * TBF, rtag=pre + "y2", eng="pool")
        for k in range(NCH):
            p.op("dve", (lambda k: lambda e: e.scalar_tensor_tensor(
                yb[k][:, 0:TBF], pf[k][:, 0:TBF], 0.5, yb[k][:, 0:TBF],
                ALU.mult, ALU.add))(k),
                reads=((pre + "pf", k), (pre + "y", k)), writes=((pre + "y", k),))
        out_keys += store_fm(p, xdst, yb, pre + "y", TBF, dcol0=blk * TBF, wtag=xdn)
    p.pop_scope()
    return out_keys



PARTS_FULL = ["l0f", "l0a", "l0f", "moe", "l1a", "moe", "l2f", "l2a", "l2f",
              "l3f", "l3a", "l3f"]


def build_fused_full(nparts=len(PARTS_FULL)):
    nc = bass.Bass("TRN2", target_bir_lowering=False)
    dr = {}
    def di(name, shape, dtype=F32):
        dr[name] = nc.dram_tensor(name, list(shape), dtype, kind="ExternalInput").ap()
        return dr[name]
    x0 = di("x0", [1024, TLOC])
    xout = nc.dram_tensor("xout", [1024, TLOC], F32, kind="ExternalOutput").ap()
    xA = nc.dram_tensor("xA", [1024, TLOC], F32, kind="Internal").ap()
    xB = nc.dram_tensor("xB", [1024, TLOC], F32, kind="Internal").ap()
    ag2in = nc.dram_tensor("ag2in", [1024, 2], F32, kind="Internal").ap()
    ag2out = nc.dram_tensor("ag2out", [2048, 2], F32, kind="Internal").ap()
    ag5in = nc.dram_tensor("ag5in", [1024, 512], F32, kind="Internal").ap()
    ag5out = nc.dram_tensor("ag5out", [2048, 512], F32, kind="Internal").ap()
    agfin = nc.dram_tensor("agfin", [1024, TLOC], BF16, kind="Internal").ap()
    agfout = nc.dram_tensor("agfout", [2048, TLOC], BF16, kind="Internal").ap()
    ag30in = nc.dram_tensor("ag30in", [1024, 3072], BF16, kind="Internal").ap()
    ag30out = nc.dram_tensor("ag30out", [2048, 3072], BF16, kind="Internal").ap()
    fin = nc.dram_tensor("fencein", [2, 2], F32, kind="Internal").ap()
    fout = nc.dram_tensor("fenceout", [2, 2], F32, kind="Internal").ap()
    Ks = nc.dram_tensor("Ks", [1024, 8192], BF16, kind="Internal").ap()
    Vs = nc.dram_tensor("Vs", [8192, 1024], BF16, kind="Internal").ap()
    xbs = nc.dram_tensor("xbs", [1024, 8192], BF16, kind="Internal").ap()
    yb0 = nc.dram_tensor("yb0", [1024, 2 + 7168], F32, kind="Internal").ap()
    yb1 = nc.dram_tensor("yb1", [1024, 2 + 7168], F32, kind="Internal").ap()
    ysc = nc.dram_tensor("yscr", [1024, 2 + 7168], F32, kind="Internal").ap()
    y2 = nc.dram_tensor("yfin", [1024, TLOC], F32, kind="Internal").ap()

    di("identb", [P, P], BF16)
    di("maskL", [P, 1])
    di("maskH", [P, 2])
    di("identf", [P, P])
    di("l0f_wrT", [1024, 1024]); di("l0f_wkT", [1024, 1024]); di("l0f_wvT", [1024, 1024])
    di("l0f_mu", [P, 16])
    di("l0a_wqT", [1024, 1024]); di("l0a_wkT", [1024, 256]); di("l0a_wvT", [1024, 256])
    di("l0a_woT", [1024, 1024]); di("l0a_qg", [P, 2]); di("l0a_kg", [P, 2])
    di("l0a_ropeC", [256, 512 + TLOC]); di("l0a_ropeS", [256, 512 + TLOC])
    di("l0a_pswap", [P, P], BF16); di("l0a_masks", [5, P, 640])
    di("moe_gwT", [1024, 16]); di("moe_ebias", [P, 16]); di("moe_iota", [P, 16])
    di("moe_gpT", [16, 1024, 512]); di("moe_upT", [16, 1024, 512]); di("moe_dpT", [16, 512, 1024])
    di("moe_selm", [16, 16 * P])
    di("l1a_wqT", [1024, 1024]); di("l1a_wkT", [1024, 1024]); di("l1a_bqv", [P, 16])
    di("l2a_wqkvT", [1024, 3072]); di("l2a_bqv", [P, 24])
    di("l2f_w_cn1", [1024, 1024]); di("l2f_w_cn2", [1024, 512]); di("l2f_w_cn3", [512, 256])
    di("l2f_w_cn4", [256, 1]); di("l2f_w_gc1", [1024, 512]); di("l2f_w_gc2", [512, 256])
    di("l2f_w_gc3", [256, 1])
    di("l2f_cnb1", [P, 8]); di("l2f_cnb2", [P, 4]); di("l2f_cnb3", [P, 2])
    di("l2f_gcb1", [P, 4]); di("l2f_gcb2", [P, 2]); di("l2f_scl", [1, 4])
    di("l3a_wA", [1024, 1024]); di("l3a_wAT", [1024, 1024])
    di("l3a_wB", [1024, 1024]); di("l3a_wBT", [1024, 1024])
    di("l3a_wC", [1024, 1024]); di("l3a_wCT", [1024, 1024])
    di("l3f_qkvw", [6, 1024, 3072]); di("l3f_wow", [6, 1024, 1024])
    di("l3f_w1w", [6, 1024, 5460]); di("l3f_w2w", [6, 2730, 1024])
    di("l3f_qsc", [6, P, 2]); di("l3f_lngb", [6, P, 16]); di("l3f_b2c", [6, P, 8])
    di("l3f_b1a", [6, P, 22]); di("l3f_b1g", [6, P, 22])
    di("l3f_tabQC", [P, 7168]); di("l3f_tabQS", [P, 7168])
    di("l3f_tabKC", [P, 7168]); di("l3f_tabKS", [P, 7168])
    di("l3f_hones", [P, 2]); di("l3f_hexp", [2, P]); di("l3f_p32", [P, P], BF16)
    di("l3f_masks", [4, P, 512])

    p = Prog(nc)
    sh = Shared(p, dr["identb"])
    parts = PARTS_FULL[:nparts]
    srcs = [x0] + [xA if i % 2 == 0 else xB for i in range(len(parts))]
    names = ["x0"] + ["xA" if i % 2 == 0 else "xB" for i in range(len(parts))]
    out_keys = []
    agn = [0]
    for i, kind in enumerate(parts):
        pre = f"p{i}_"
        xsrc, xsn = srcs[i], names[i]
        last = i == len(parts) - 1
        xdst = xout if last else srcs[i + 1]
        xdn = "xout" if last else names[i + 1]
        gk = f"g{agn[0]}"; agn[0] += 1
        if kind == "l0f":
            emit_ag(p, xsrc, xsn, ag2in, ag2out, 1024, 2, gk, fence=(fin, fout), src_col0=TLOC - 2)
            out_keys = emit_l0ffn(p, sh, dr, xsrc, xsn, xdst, xdn, ag2out, gk + "out", pre)
        elif kind == "l0a":
            emit_ag(p, xsrc, xsn, ag5in, ag5out, 1024, 512, gk, fence=(fin, fout), src_col0=TLOC - 512)
            out_keys = emit_l0attn(p, sh, dr, xsrc, xsn, xdst, xdn, ag5out, gk + "out", pre)
        elif kind == "moe":
            out_keys = emit_l1ffn(p, sh, dr, xsrc, xsn, xdst, xdn, pre)
        elif kind == "l1a":
            emit_ag(p, xsrc, xsn, agfin, agfout, 1024, TLOC, gk, fence=(fin, fout))
            out_keys = emit_fullattn(p, sh, dr, "l1", xsrc, xsn, xdst, xdn, agfout, gk + "out",
                                     Ks, Vs, pre)
        elif kind == "l2a":
            emit_ag(p, xsrc, xsn, agfin, agfout, 1024, TLOC, gk, fence=(fin, fout))
            out_keys = emit_fullattn(p, sh, dr, "l2", xsrc, xsn, xdst, xdn, agfout, gk + "out",
                                     Ks, Vs, pre)
        elif kind == "l2f":
            emit_ag(p, xsrc, xsn, agfin, agfout, 1024, TLOC, gk, fence=(fin, fout))
            out_keys = emit_l2ffn(p, sh, dr, xsrc, xsn, xdst, xdn, agfout, gk + "out", pre)
        elif kind == "l3a":
            emit_ag(p, xsrc, xsn, agfin, agfout, 1024, TLOC, gk, fence=(fin, fout))
            out_keys = emit_l3attn(p, sh, dr, xsrc, xsn, xdst, xdn, agfout, gk + "out", xbs, pre)
        elif kind == "l3f":
            emit_ag(p, xsrc, xsn, ag30in, ag30out, 1024, 3072, gk, fence=(fin, fout),
                    src_col0=TLOC - 3072)
            out_keys = emit_l3ffn_app(p, sh, dr, xsrc, xsn, xdst, xdn, ag30out, gk + "out",
                                      [yb0, yb1], ["yb0", "yb1"], ysc, y2, pre)
        p.phase()
    p.finalize(final_waits=out_keys)
    return nc


def stage_fused_full(inp, xfm_b, h, shared=None):
    if shared is None:
        shared = {}
    def memo(key, fn):
        if key not in shared:
            shared[key] = fn()
        return shared[key]
    zero8 = np.zeros((1024, 8192), np.float32)
    a0 = stage_l0attn(zero8, h, inp)
    m = memo("moe", lambda: stage_l1ffn(zero8, 0, inp))
    l2f = memo("l2f", lambda: stage_l2ffn(zero8, 0, inp))
    l1a = memo("l1a", lambda: stage_fullattn("l1", zero8, 0, inp))
    l2a = memo("l2a", lambda: stage_fullattn("l2", zero8, 0, inp))
    l3a = memo("l3a", lambda: stage_l3attn(zero8, inp))
    qC, qS, kC, kS = l3ffn_tables(h, 0)
    l3ws = memo("l3w", lambda: dict(
        l3f_qkvw=np.ascontiguousarray(inp["L3_ffn_Wqkv"]),
        l3f_wow=np.ascontiguousarray(inp["L3_ffn_Wo"]),
        l3f_w1w=np.ascontiguousarray(inp["L3_ffn_W1"]),
        l3f_w2w=np.ascontiguousarray(inp["L3_ffn_W2"])))
    cc_ = chunk_cols
    qsc6 = np.stack([np.stack([np.tile(inp["L3_ffn_q_scale"][li], 2),
                               np.tile(inp["L3_ffn_k_scale"][li], 2)], axis=1)
                     for li in range(6)]).astype(np.float32)
    lngb6 = np.stack([np.concatenate([cc_(inp["L3_ffn_ln_g"][li]), cc_(inp["L3_ffn_ln_b"][li])], axis=1)
                      for li in range(6)])
    b2c6 = np.stack([cc_(inp["L3_ffn_b2"][li]) for li in range(6)])
    b1a6 = np.stack([_pad_chunkcols(inp["L3_ffn_b1"][li][:2730], 22) for li in range(6)])
    b1g6 = np.stack([_pad_chunkcols(inp["L3_ffn_b1"][li][2730:], 22) for li in range(6)])
    d = {
        "x0": np.ascontiguousarray(xfm_b[:, h * TLOC:(h + 1) * TLOC]),
        "identb": np.eye(128, dtype=BF16NP),
        "identf": np.eye(128, dtype=np.float32),
        "maskL": np.full((128, 1), 1.0 if h == 1 else 0.0, np.float32),
        "maskH": np.ascontiguousarray(np.stack(
            [np.full(128, 1.0 if h == 0 else 0.0), np.full(128, 1.0 if h == 1 else 0.0)],
            axis=1).astype(np.float32)),
        "l0f_wrT": memo("l0f_wrT", lambda: wT(inp["L0_ffn_W_r"])),
        "l0f_wkT": memo("l0f_wkT", lambda: wT(inp["L0_ffn_W_k"])),
        "l0f_wvT": memo("l0f_wvT", lambda: wT(inp["L0_ffn_W_v"])),
        "l0f_mu": memo("l0f_mu", lambda: np.ascontiguousarray(np.concatenate(
            [chunk_cols(inp["L0_ffn_mu_r"]), chunk_cols(inp["L0_ffn_mu_k"])], axis=1).astype(np.float32))),
        "l0a_wqT": a0["wqT"], "l0a_wkT": a0["wkT"], "l0a_wvT": a0["wvT"], "l0a_woT": a0["woT"],
        "l0a_qg": a0["qg"], "l0a_kg": a0["kg"], "l0a_ropeC": a0["ropeC"], "l0a_ropeS": a0["ropeS"],
        "l0a_pswap": a0["pswap"], "l0a_masks": a0["masks"],
        "moe_gwT": m["gwT"], "moe_ebias": m["ebias"], "moe_iota": m["iota16"],
        "moe_gpT": m["gpT"], "moe_upT": m["upT"], "moe_dpT": m["dpT"], "moe_selm": m["selm"],
        "l1a_wqT": l1a["wqT"], "l1a_wkT": l1a["wkT"], "l1a_bqv": l1a["bqv"],
        "l2a_wqkvT": l2a["wqkvT"], "l2a_bqv": l2a["bqv"],
        "l2f_w_cn1": l2f["w_cn1"], "l2f_w_cn2": l2f["w_cn2"], "l2f_w_cn3": l2f["w_cn3"],
        "l2f_w_cn4": l2f["w_cn4"], "l2f_w_gc1": l2f["w_gc1"], "l2f_w_gc2": l2f["w_gc2"],
        "l2f_w_gc3": l2f["w_gc3"], "l2f_cnb1": l2f["cnb1"], "l2f_cnb2": l2f["cnb2"],
        "l2f_cnb3": l2f["cnb3"], "l2f_gcb1": l2f["gcb1"], "l2f_gcb2": l2f["gcb2"],
        "l2f_scl": l2f["scl"],
        "l3a_wA": l3a["wA"], "l3a_wAT": l3a["wAT"], "l3a_wB": l3a["wB"], "l3a_wBT": l3a["wBT"],
        "l3a_wC": l3a["wC"], "l3a_wCT": l3a["wCT"],
        "l3f_qsc": qsc6, "l3f_lngb": lngb6, "l3f_b2c": b2c6, "l3f_b1a": b1a6, "l3f_b1g": b1g6,
        "l3f_tabQC": qC, "l3f_tabQS": qS, "l3f_tabKC": kC, "l3f_tabKS": kS,
        "l3f_hones": np.ascontiguousarray(np.stack(
            [np.concatenate([np.ones(64), np.zeros(64)]),
             np.concatenate([np.zeros(64), np.ones(64)])], axis=1).astype(np.float32)),
        "l3f_hexp": np.ascontiguousarray(np.stack(
            [np.concatenate([np.ones(64), np.zeros(64)]),
             np.concatenate([np.zeros(64), np.ones(64)])], axis=0).astype(np.float32)),
        "l3f_p32": _p32mat(),
        "l3f_masks": l3ffn_masks(h, 0),
    }
    d.update(l3ws)
    return d



import os as _os
_os.environ.setdefault("GAUGE_NTFF_IGNORE_DMA", "1")

_FPROG = {}


def kernel(**inp):
    x = np.asarray(inp["L0_attn_x"], dtype=np.float32)
    if "full" not in _FPROG:
        _FPROG["full"] = build_fused_full()
    nc = _FPROG["full"]
    shared = {}
    in_maps = []
    xfms = [to_fm(x[b]) for b in range(4)]
    for core in range(8):
        b, h = core // 2, core % 2
        in_maps.append(stage_fused_full(inp, xfms[b], h, shared))
    res = run_bass_kernel_spmd(nc, in_maps, core_ids=list(range(8)))
    out = np.zeros((4, 8192, 1024), np.float32)
    for core in range(8):
        b, h = core // 2, core % 2
        out[b, h * 4096:(h + 1) * 4096] = res.results[core]["xout"].T
    return out
